# version 13
# baseline (speedup 1.0000x reference)
import contextlib
import numpy as np
import ml_dtypes
import concourse.bass as bass
import concourse.mybir as mybir
from concourse.bass_utils import run_bass_kernel_spmd

F32 = mybir.dt.float32
BF16 = mybir.dt.bfloat16
AF = mybir.ActivationFunctionType
ALU = mybir.AluOpType
AX = mybir.AxisListType
BF = ml_dtypes.bfloat16

NCORE = 8
TOK = 2048
NT = TOK // 128
S = 8192
D = 1024
NH = 16
HD = 64


ENGS = ("gpsimd", "sync", "tensor", "scalar", "vector")


class Res:
    __slots__ = ("w", "rs", "name")

    def __init__(self, name=""):
        self.w = None
        self.rs = []
        self.name = name


class Op:
    def __init__(self, eng, emit, deps, chan, is_dma):
        self.eng = eng
        self.emit = emit
        self.deps = deps
        self.chan = chan
        self.is_dma = is_dma
        self.gi = -1
        self.ei = -1
        self.marked = False
        self._waits = []


class Chan:
    __slots__ = ("sem", "ops", "name")

    def __init__(self, name):
        self.sem = None
        self.ops = []
        self.name = name


class Prog:
    def __init__(self, nc):
        self.nc = nc
        self.ops = []
        self.chans = {}
        self._es = None

    def chan(self, name):
        if name not in self.chans:
            self.chans[name] = Chan(name)
        return self.chans[name]

    def op(self, eng, emit, reads=(), writes=(), chan=None, extra_deps=()):
        is_dma = chan is not None
        if is_dma:
            chan = self.chan(chan)
        deps = []
        for r in reads:
            if r.w is not None:
                deps.append(r.w)
        for w in writes:
            if w.w is not None:
                deps.append(w.w)
            deps.extend(w.rs)
        deps.extend(extra_deps)
        o = Op(eng, emit, deps, chan, is_dma)
        o.gi = len(self.ops)
        self.ops.append(o)
        if is_dma:
            chan.ops.append(o)
        for r in reads:
            r.rs.append(o)
        for w in writes:
            w.w = o
            w.rs = []
        return o

    def dma(self, eng, chan, out_ap, in_ap, reads=(), writes=(), extra_deps=(), **kw):
        return self.op(
            eng,
            lambda e, _o=out_ap, _i=in_ap, _kw=kw: e.dma_start(_o, _i, **_kw),
            reads=reads, writes=writes, chan=chan, extra_deps=extra_deps)

    def build(self, es):
        nc = self.nc
        eng_sems = {}
        for en in ENGS:
            eng_sems[en] = es.enter_context(nc.semaphore(f"sem_{en}"))
        for ch in self.chans.values():
            ch.sem = es.enter_context(nc.semaphore(f"ch_{ch.name}"))
        eidx = {en: 0 for en in ENGS}
        for o in self.ops:
            o.ei = eidx[o.eng]
            eidx[o.eng] += 1
            for d in o.deps:
                d.marked = True
        inc_cum = {en: [] for en in ENGS}
        cnt = {en: 0 for en in ENGS}
        for o in self.ops:
            if (not o.is_dma) and o.marked:
                cnt[o.eng] += 1
            inc_cum[o.eng].append(cnt[o.eng])
        import bisect
        chan_gis = {id(ch): sorted(co.gi for co in ch.ops) for ch in self.chans.values()}

        def chan_count_before(ch, gi):
            return bisect.bisect_left(chan_gis[id(ch)], gi)

        watermark = {}
        for o in self.ops:
            waits = []
            for d in o.deps:
                if d.is_dma:
                    sem = d.chan.sem
                    val = 16 * chan_count_before(d.chan, o.gi)
                else:
                    if d.eng == o.eng and o.eng in ("tensor", "sync"):
                        continue
                    if d.eng == o.eng and d.gi == o.gi:
                        continue
                    sem = eng_sems[d.eng]
                    val = inc_cum[d.eng][d.ei]
                key = (o.eng, id(sem))
                if watermark.get(key, -1) >= val:
                    continue
                waits.append((sem, val, key))
            best = {}
            for sem, val, key in waits:
                if key not in best or best[key][1] < val:
                    best[key] = (sem, val)
            o._waits = [best[k] for k in best]
            for k, (sem, val) in best.items():
                watermark[k] = max(watermark.get(k, -1), val)
        blk = es.enter_context(nc.Block())
        prog = self

        def run_stream(eng_handle, en):
            for o in prog.ops:
                if o.eng != en:
                    continue
                for sem, val in o._waits:
                    eng_handle.wait_ge(sem, val)
                instr = o.emit(eng_handle)
                if o.is_dma:
                    instr.then_inc(o.chan.sem, 16)
                elif o.marked:
                    instr.then_inc(eng_sems[en], 1)

        blk.gpsimd(lambda e: run_stream(e, "gpsimd"))
        blk.sync(lambda e: run_stream(e, "sync"))
        blk.tensor(lambda e: run_stream(e, "tensor"))
        blk.scalar(lambda e: run_stream(e, "scalar"))
        blk.vector(lambda e: run_stream(e, "vector"))


class Arena:

    def __init__(self, tensor, nbytes):
        self.t = tensor
        self.nbytes = nbytes
        self.free_list = [(0, nbytes)]
        self.history = []

    def alloc(self, nbytes, name="", side="bottom"):
        nbytes = (nbytes + 63) & ~63
        idxs = range(len(self.free_list))
        if side == "top":
            idxs = reversed(list(idxs))
        for i in idxs:
            (s, e) = self.free_list[i]
            if e - s >= nbytes:
                if side == "top":
                    s = e - nbytes
                    self.free_list[i] = (self.free_list[i][0], s)
                else:
                    self.free_list[i] = (s + nbytes, e)
                if self.free_list[i][0] == self.free_list[i][1]:
                    self.free_list.pop(i)
                res = Res(name)
                prime = []
                for (hs, he, hres) in self.history:
                    if hs < s + nbytes and he > s:
                        if hres.w is not None:
                            prime.append(hres.w)
                        prime.extend(hres.rs)
                res.rs = prime
                self.history.append((s, s + nbytes, res))
                return _Region(self, s, nbytes, res)
        raise MemoryError(f"arena: cannot alloc {nbytes} B for {name}")

    def free(self, region):
        s, e = region.start, region.start + region.size
        self.free_list.append((s, e))
        self.free_list.sort()
        merged = []
        for iv in self.free_list:
            if merged and merged[-1][1] >= iv[0]:
                merged[-1] = (merged[-1][0], max(merged[-1][1], iv[1]))
            else:
                merged.append(list(iv))
        self.free_list = [tuple(m) for m in merged]


class _Region:
    def __init__(self, arena, start, size, res):
        self.arena = arena
        self.start = start
        self.size = size
        self.res = res

    def view(self, dtype, cols, offset_bytes=0, parts=128):
        isz = {F32: 4, BF16: 2}[dtype]
        off = (self.start + offset_bytes) // isz
        ap = self.arena.t[:].bitcast(dtype)[0:parts, off:off + cols]
        return ap

    def free(self):
        self.arena.free(self)



def ln_tile(p, x_ap, out_ap, r_x, r_out, tmp, r_tmp, eps):
    sq, c0, c1, c2 = tmp["sq"], tmp["c0"], tmp["c1"], tmp["c2"]
    r_sq, r_c0, r_c1, r_c2 = r_tmp["sq"], r_tmp["c0"], r_tmp["c1"], r_tmp["c2"]
    n = x_ap.shape[-1]
    p.op("vector", lambda e: e.reduce_sum(c0[:], x_ap, axis=AX.X),
         reads=[r_x], writes=[r_c0])
    p.op("vector", lambda e: e.tensor_scalar_mul(c0[:], c0[:], -1.0 / n),
         reads=[r_c0], writes=[r_c0])
    p.op("scalar", lambda e: e.activation(sq[:], x_ap, AF.Square, bias=c0[:]),
         reads=[r_x, r_c0], writes=[r_sq])
    p.op("vector", lambda e: e.reduce_sum(c1[:], sq[:], axis=AX.X),
         reads=[r_sq], writes=[r_c1])
    p.op("vector", lambda e: e.tensor_scalar(c1[:], c1[:], 1.0 / n, eps, ALU.mult, ALU.add),
         reads=[r_c1], writes=[r_c1])
    p.op("scalar", lambda e: e.activation(c1[:], c1[:], AF.Sqrt),
         reads=[r_c1], writes=[r_c1])
    p.op("vector", lambda e: e.reciprocal(c2[:], c1[:]),
         reads=[r_c1], writes=[r_c2])
    p.op("vector", lambda e: e.tensor_scalar(out_ap, x_ap, c0[:], c2[:], ALU.add, ALU.mult),
         reads=[r_x, r_c0, r_c2], writes=[r_out])



def build_pre0():
    nc = bass.Bass("TRN2", target_bir_lowering=False)
    x_in = nc.dram_tensor("x_in", [TOK, D], F32, kind="ExternalInput").ap()
    wq_in = nc.dram_tensor("wq_in", [D, D], F32, kind="ExternalInput").ap()
    wk_in = nc.dram_tensor("wk_in", [D, D], F32, kind="ExternalInput").ap()
    wv_in = nc.dram_tensor("wv_in", [D, D], F32, kind="ExternalInput").ap()
    iden_in = nc.dram_tensor("iden_in", [128, 128], F32, kind="ExternalInput").ap()
    xnT_o = nc.dram_tensor("xnT_o", [128, 8, TOK], BF16, kind="ExternalOutput").ap()
    qT_o = nc.dram_tensor("qT_o", [128, 8, TOK], BF16, kind="ExternalOutput").ap()
    kT_o = nc.dram_tensor("kT_o", [128, 8, TOK], BF16, kind="ExternalOutput").ap()
    v_o = nc.dram_tensor("v_o", [128, NT, D], BF16, kind="ExternalOutput").ap()

    with contextlib.ExitStack() as es:
        sb = lambda name, shape, dt: es.enter_context(nc.sbuf_tensor(name, shape, dt))
        ps = lambda name, shape, dt: es.enter_context(nc.psum_tensor(name, shape, dt))
        xs = sb("xs", [128, NT * D], F32)
        xnT = sb("xnT", [128, 8 * TOK], BF16)
        wq = sb("wq", [128, 8 * D], BF16)
        wk = sb("wk", [128, 8 * D], BF16)
        wv = sb("wv", [128, 8 * D], BF16)
        idb = sb("idb", [128, 128], BF16)
        c0 = sb("c0", [128, 1], F32)
        c1 = sb("c1", [128, 1], F32)
        c2 = sb("c2", [128, 1], F32)
        sq = sb("sq", [128, D], F32)
        xn_t = sb("xn_t", [128, D], BF16)
        ev = [sb(f"ev{i}", [128, 512], BF16) for i in range(4)]
        pst = ps("pst", [128, D], BF16)
        pmm = [ps(f"pmm{i}", [128, 512], F32) for i in range(4)]

        p = Prog(nc)
        r_xs = Res("xs")
        r_xnT = Res("xnT")
        r_w = {0: Res("wq"), 1: Res("wk"), 2: Res("wv")}
        r_id = Res("id")
        r_tmp = {"sq": Res(), "c0": Res(), "c1": Res(), "c2": Res()}
        tmp = {"sq": sq, "c0": c0, "c1": c1, "c2": c2}
        r_xn_t = Res("xn_t")
        r_ev = [Res(f"ev{i}") for i in range(4)]
        r_pst = Res("pst")
        r_pmm = [Res(f"pmm{i}") for i in range(4)]

        p.dma("sync", "ldx", xs[:].rearrange("p (t d) -> p t d", t=NT),
              x_in.rearrange("(t p) d -> p t d", p=128), writes=[r_xs])
        for wsb, win, rw in ((wq, wq_in, r_w[0]), (wk, wk_in, r_w[1]), (wv, wv_in, r_w[2])):
            p.dma("gpsimd", "ldw", wsb[:].rearrange("p (a o) -> p a o", a=8),
                  win.rearrange("(a p) o -> p a o", p=128), writes=[rw])
        p.dma("gpsimd", "ldw", idb[:], iden_in, writes=[r_id])

        for tt in range(NT):
            x_ap = xs[:, tt * D:(tt + 1) * D]
            ln_tile(p, x_ap, xn_t[:], r_xs, r_xn_t, tmp, r_tmp, 1e-5)
            for c in range(8):
                p.op("tensor",
                     lambda e, c=c: e.transpose(pst[:, c * 128:(c + 1) * 128],
                                                xn_t[:, c * 128:(c + 1) * 128], idb[:]),
                     reads=[r_xn_t, r_id], writes=[r_pst])
            dst = xnT[:].rearrange("p (c t) -> p c t", c=8)[:, :, tt * 128:(tt + 1) * 128]
            src = pst[:].rearrange("p (c t) -> p c t", c=8)
            p.op("vector", lambda e, dst=dst, src=src: e.tensor_copy(dst, src),
                 reads=[r_pst], writes=[r_xnT])

        p.dma("sync", "st", xnT_o[:, :, :], xnT[:].rearrange("p (c t) -> p c t", c=8),
              reads=[r_xnT])
        for wi, (wsb, out_dram) in enumerate(((wq, qT_o), (wk, kT_o))):
            for ft in range(8):
                for tc in range(4):
                    bank = r_pmm[(ft * 4 + tc) % 4]
                    pb = pmm[(ft * 4 + tc) % 4]
                    for i in range(8):
                        p.op("tensor",
                             lambda e, i=i, ft=ft, tc=tc, pb=pb, wsb=wsb: e.matmul(
                                 pb[:],
                                 wsb[:, i * D + ft * 128: i * D + ft * 128 + 128],
                                 xnT[:].rearrange("p (c t) -> p c t", c=8)[:, i, tc * 512:(tc + 1) * 512],
                                 start=(i == 0), stop=(i == 7)),
                             reads=[r_w[wi], r_xnT], writes=[bank])
                    eb = r_ev[(ft * 4 + tc) % 4]
                    esb = ev[(ft * 4 + tc) % 4]
                    p.op("scalar", lambda e, pb=pb, esb=esb: e.activation(esb[:], pb[:], AF.Copy),
                         reads=[bank], writes=[eb])
                    p.dma("sync", "st", out_dram[:, ft, tc * 512:(tc + 1) * 512], esb[:],
                          reads=[eb])
        for tt in range(NT):
            for oc in range(2):
                k = (tt * 2 + oc) % 4
                pb, bank, esb, eb = pmm[k], r_pmm[k], ev[k], r_ev[k]
                for i in range(8):
                    p.op("tensor",
                         lambda e, i=i, tt=tt, oc=oc, pb=pb: e.matmul(
                             pb[:],
                             xnT[:].rearrange("p (c t) -> p c t", c=8)[:, i, tt * 128:(tt + 1) * 128],
                             wv[:, i * D + oc * 512: i * D + oc * 512 + 512],
                             start=(i == 0), stop=(i == 7)),
                         reads=[r_w[2], r_xnT], writes=[bank])
                p.op("scalar", lambda e, pb=pb, esb=esb: e.activation(esb[:], pb[:], AF.Copy),
                     reads=[bank], writes=[eb])
                p.dma("sync", "st", v_o[:, tt, oc * 512:(oc + 1) * 512], esb[:],
                      reads=[eb])
        st_ops = list(p.chan("st").ops)
        p.op("sync", lambda e: e.nop(), extra_deps=st_ops)
        p.build(es)
    return nc



PAIRS = [(a, l) for a in range(16) for l in range(16) if a - l >= 12]


def build_attn0():
    nc = bass.Bass("TRN2", target_bir_lowering=False)
    qT_i = nc.dram_tensor("qT_i", [128, 8, TOK], BF16, kind="ExternalInput").ap()
    kT_i = nc.dram_tensor("kT_i", [128, 8, S], BF16, kind="ExternalInput").ap()
    vaug_i = nc.dram_tensor("vaug_i", [128, 4, 64, 260], BF16, kind="ExternalInput").ap()
    grT_i = nc.dram_tensor("grT_i", [len(PAIRS), S, 128], F32, kind="ExternalInput").ap()
    b0_i = nc.dram_tensor("b0_i", [128, 1], F32, kind="ExternalInput").ap()
    attnT_o = nc.dram_tensor("attnT_o", [64, NH, TOK], F32, kind="ExternalOutput").ap()
    u_o = nc.dram_tensor("u_o", [1, D], F32, kind="ExternalOutput").ap()

    SCALE = 1.0 / 32.0

    with contextlib.ExitStack() as es:
        sb = lambda name, shape, dt: es.enter_context(nc.sbuf_tensor(name, shape, dt))
        qT = sb("qT", [128, 8 * TOK], BF16)
        kTp = sb("kTp", [128, 2 * S], BF16)
        vaug = sb("vaug", [128, 64 * 260], BF16)
        aT_sl = sb("aT_sl", [64, 4 * TOK], F32)
        pT = [sb(f"pT{i}", [128, 2048], BF16) for i in range(2)]
        b0c = sb("b0c", [128, 1], F32)
        b0cb = sb("b0cb", [128, 1], BF16)
        u_out = sb("u_out", [1, D], F32)
        linv = sb("linv", [1, 512], F32)
        pv_sb = sb("pv_sb", [64, 512], F32)
        grT_sb = sb("grT_sb", [128, 64 * 128], F32)
        dT = sb("dT", [128, 64 * 128], BF16)
        onesf = sb("onesf", [1, 64], F32)
        ps_s = es.enter_context(nc.psum_tensor("ps_s", [128, 2048], F32))
        ps_o = es.enter_context(nc.psum_tensor("ps_o", [128, 2048], F32))

        p = Prog(nc)
        r_qT, r_kTp, r_vaug, r_aT = Res("qT"), Res("kTp"), Res("vaug"), Res("aT")
        r_pT = [Res("pT0"), Res("pT1")]
        r_b0c, r_b0cb, r_uout = Res(), Res(), Res()
        r_linv, r_pvsb, r_grT, r_dT, r_onesf = Res(), Res(), Res(), Res(), Res()
        r_ps_s, r_ps_o = Res("ps_s"), Res("ps_o")

        kTp_v = kTp[:].rearrange("p (c t) -> p c t", c=2)
        qT_v = qT[:].rearrange("p (c t) -> p c t", c=8)
        vaug_v = vaug[:].rearrange("p (k c) -> p k c", k=64)
        aT_v = aT_sl[:].rearrange("p (c t) -> p c t", c=4)
        dT_v = dT[:].rearrange("p (k m) -> p k m", k=64)

        p.dma("sync", "ldq", qT[:].rearrange("p (c t) -> p c t", c=8), qT_i,
              writes=[r_qT])
        p.dma("sync", "ldq", b0c[:], b0_i, writes=[r_b0c])
        p.dma("gpsimd", "ldb", b0cb[:], b0_i, writes=[r_b0cb])
        p.op("vector", lambda e: e.memset(onesf[:], 1.0), writes=[r_onesf])

        for hp in range(4):
            p.dma("sync", "ldk", kTp_v, kT_i[:, 2 * hp:2 * hp + 2, :], writes=[r_kTp])
            p.dma("sync", "ldv", vaug_v, vaug_i[:, hp], writes=[r_vaug])
            for kc in range(64):
                p.op("tensor",
                     lambda e, kc=kc: e.matmul(ps_s[0:1, 0:260], b0cb[:],
                                               vaug_v[:, kc, :],
                                               start=(kc == 0), stop=(kc == 63)),
                     reads=[r_b0cb, r_vaug], writes=[r_ps_s])
            dst = u_out[0:1, 256 * hp:256 * hp + 256].rearrange("a (h c) -> a h c", h=4)
            src = ps_s[0:1, 0:260].rearrange("a (h c) -> a h c", h=4)[:, :, 0:64]
            p.op("vector", lambda e, dst=dst, src=src: e.tensor_copy(dst, src),
                 reads=[r_ps_s], writes=[r_uout])

            for qm in range(4):
                for kc in range(64):
                    for hh in range(4):
                        lo = 64 * (hh & 1)
                        p.op("tensor",
                             lambda e, kc=kc, hh=hh, lo=lo, hp=hp, qm=qm: e.matmul(
                                 ps_s[:, 512 * hh:512 * hh + 512],
                                 kTp_v[lo:lo + 64, hh >> 1, 128 * kc:128 * kc + 128],
                                 qT_v[lo:lo + 64, 2 * hp + (hh >> 1), 512 * qm:512 * qm + 512],
                                 start=True, stop=True),
                             reads=[r_kTp, r_qT], writes=[r_ps_s])
                    pb = kc % 2
                    p.op("scalar",
                         lambda e, pb=pb: e.activation(pT[pb][:], ps_s[:], AF.Exp, scale=SCALE),
                         reads=[r_ps_s], writes=[r_pT[pb]])
                    for hh in range(4):
                        p.op("tensor",
                             lambda e, kc=kc, hh=hh, pb=pb: e.matmul(
                                 ps_o[0:65, 512 * hh:512 * hh + 512],
                                 vaug_v[:, kc, 65 * hh:65 * hh + 65],
                                 pT[pb][:, 512 * hh:512 * hh + 512],
                                 start=(kc == 0), stop=(kc == 63)),
                             reads=[r_vaug, r_pT[pb]], writes=[r_ps_o])
                for hh in range(4):
                    p.op("vector",
                         lambda e, hh=hh: e.reciprocal(linv[:], ps_o[64:65, 512 * hh:512 * hh + 512]),
                         reads=[r_ps_o], writes=[r_linv])
                    p.op("scalar",
                         lambda e, hh=hh: e.activation(pv_sb[:], ps_o[0:64, 512 * hh:512 * hh + 512], AF.Copy),
                         reads=[r_ps_o], writes=[r_pvsb])
                    p.op("tensor",
                         lambda e: e.matmul(ps_s[0:64, 0:512], onesf[:], linv[:],
                                            start=True, stop=True),
                         reads=[r_onesf, r_linv], writes=[r_ps_s])
                    dst = aT_v[:, hh, 512 * qm:512 * qm + 512]
                    p.op("vector",
                         lambda e, dst=dst: e.tensor_mul(dst, pv_sb[:], ps_s[0:64, 0:512]),
                         reads=[r_pvsb, r_ps_s], writes=[r_aT])
            if hp < 3:
                p.dma("sync", "sta", attnT_o[:, 4 * hp:4 * hp + 4, :], aT_v,
                      reads=[r_aT])

        for j, (a, l) in enumerate(PAIRS):
            p.dma("sync", "ldg", grT_sb[:].rearrange("p (c m) -> p c m", c=64),
                  grT_i[j].rearrange("(c p) m -> p c m", p=128), writes=[r_grT])
            p.op("vector",
                 lambda e: e.tensor_scalar(dT[:], grT_sb[:], b0c[:], None, ALU.subtract),
                 reads=[r_grT, r_b0c], writes=[r_dT])
            hh = a - 12
            for kc in range(64):
                p.op("tensor",
                     lambda e, kc=kc, hh=hh: e.matmul(
                         ps_o[0:64, 0:128],
                         vaug_v[:, kc, 65 * hh:65 * hh + 64],
                         dT_v[:, kc, :],
                         start=(kc == 0), stop=(kc == 63)),
                     reads=[r_vaug, r_dT], writes=[r_ps_o])
            dst = aT_v[:, a - 12, :].rearrange("p (t0 t1) -> p t0 t1", t1=16)[:, :, l]
            p.op("vector",
                 lambda e, dst=dst: e.tensor_tensor(dst, dst, ps_o[0:64, 0:128], ALU.add),
                 reads=[r_ps_o, r_aT], writes=[r_aT])
        p.dma("sync", "sta", attnT_o[:, 12:16, :], aT_v, reads=[r_aT])
        p.dma("sync", "sta", u_o, u_out[:], reads=[r_uout])
        st_ops = list(p.chan("sta").ops)
        p.op("sync", lambda e: e.nop(), extra_deps=st_ops)
        p.build(es)
    return nc



def build_ffn0():
    nc = bass.Bass("TRN2", target_bir_lowering=False)
    DH = 4096
    x_in = nc.dram_tensor("x_in", [TOK, D], F32, kind="ExternalInput").ap()
    attnT_i = nc.dram_tensor("attnT_i", [64, NH, TOK], F32, kind="ExternalInput").ap()
    u_i = nc.dram_tensor("u_i", [1, D], F32, kind="ExternalInput").ap()
    gamma_i = nc.dram_tensor("gamma_i", [1, D], F32, kind="ExternalInput").ap()
    beta_i = nc.dram_tensor("beta_i", [1, D], F32, kind="ExternalInput").ap()
    xnT_i = nc.dram_tensor("xnT_i", [128, 8, TOK], BF16, kind="ExternalInput").ap()
    w1t_i = nc.dram_tensor("w1t_i", [32, D, 128], F32, kind="ExternalInput").ap()
    w2t_i = nc.dram_tensor("w2t_i", [32, D, 128], F32, kind="ExternalInput").ap()
    w1s_i = nc.dram_tensor("w1s_i", [128, 32], F32, kind="ExternalInput").ap()
    w2s_i = nc.dram_tensor("w2s_i", [128, 32], F32, kind="ExternalInput").ap()
    h1t_i = nc.dram_tensor("h1t_i", [32, DH, 128], F32, kind="ExternalInput").ap()
    h2t_i = nc.dram_tensor("h2t_i", [32, DH, 128], F32, kind="ExternalInput").ap()
    h1s_i = nc.dram_tensor("h1s_i", [128, 32], F32, kind="ExternalInput").ap()
    h2s_i = nc.dram_tensor("h2s_i", [128, 32], F32, kind="ExternalInput").ap()
    w3t_i = nc.dram_tensor("w3t_i", [DH, D], F32, kind="ExternalInput").ap()
    w3s_i = nc.dram_tensor("w3s_i", [1, D], F32, kind="ExternalInput").ap()
    iden_i = nc.dram_tensor("iden_i", [128, 128], F32, kind="ExternalInput").ap()
    x1_o = nc.dram_tensor("x1_o", [128, NT, D], F32, kind="ExternalOutput").ap()
    bt_d = nc.dram_tensor("bt_d", [32, 128, TOK], BF16, kind="Internal").ap()
    ct_d = nc.dram_tensor("ct_d", [32, 128, TOK], BF16, kind="Internal").ap()
    x1tmp = nc.dram_tensor("x1tmp", [128, NT * D], F32, kind="Internal").ap()

    with contextlib.ExitStack() as es:
        sb = lambda name, shape, dt: es.enter_context(nc.sbuf_tensor(name, shape, dt))
        big = sb("big", [128, 32 * TOK], BF16)
        xs_v = big[:].bitcast(F32)[:, 0:NT * D]
        aT_v = big[:].rearrange("p (c t) -> p c t", c=32)
        w3t_v = big[:, 0:32 * D].rearrange("p (c m) -> p c m", c=32)
        xnT = sb("xnT", [128, 8 * TOK], BF16)
        xnT_v = xnT[:].rearrange("p (c t) -> p c t", c=8)
        x1b = [xnT[:].bitcast(F32)[:, i * D:(i + 1) * D] for i in range(2)]
        abuf = sb("abuf", [128, D], F32)
        t2 = sb("t2", [128, D], F32)
        rowsA = sb("rowsA", [65, D], F32)
        rowsB = sb("rowsB", [1, D], F32)
        u_sb, g_sb, be_sb = rowsA[0:1, :], rowsA[32:33, :], rowsA[64:65, :]
        w3s_sb = rowsB[0:1, :]
        u_bc = sb("u_bc", [128, D], F32)
        g_bc = sb("g_bc", [128, D], F32)
        b_bc = sb("b_bc", [128, D], F32)
        rg_w3sbc = A.alloc(4 * KB, "w3s_bc")
        w3s_bc = rg_w3sbc.view(F32, D)
        onesf_t = sb("onesf", [65, 128], F32)
        onesf = onesf_t[0:1, :]
        idf = sb("idf", [128, 128], F32)
        c0 = sb("c0", [128, 1], F32)
        c1 = sb("c1", [128, 1], F32)
        c2 = sb("c2", [128, 1], F32)
        hbuf = [sb(f"hbuf{i}", [128, 32 * 128], BF16) for i in range(2)]
        w1b = [h[:, 0:1024] for h in hbuf]
        w2b = [h[:, 1024:2048] for h in hbuf]
        gsb = [abuf[:, 0:512], abuf[:, 512:1024]]
        asl = [h[:].bitcast(F32)[0:64, 0:NH * 128] for h in hbuf]
        ctb = hbuf
        bst = [t2[:].bitcast(BF16)[:, 0:512], t2[:].bitcast(BF16)[:, 512:1024]]
        w1s = sb("w1s", [128, 32], F32)
        w2s = sb("w2s", [128, 32], F32)
        h1s = sb("h1s", [128, 32], F32)
        h2s = sb("h2s", [128, 32], F32)
        ps_tr = es.enter_context(nc.psum_tensor("ps_tr", [128, D], F32))
        ps_bc = es.enter_context(nc.psum_tensor("ps_bc", [128, D], F32))
        ps_g = [es.enter_context(nc.psum_tensor(f"ps_g{i}", [128, 512], F32)) for i in range(2)]
        ps_u = [es.enter_context(nc.psum_tensor(f"ps_u{i}", [128, 512], F32)) for i in range(2)]

        p = Prog(nc)
        R = lambda n="": Res(n)
        r_big, r_xnT, r_abuf, r_t2 = R("big"), R("xnT"), R(), R()
        r_rows = R("rows")
        r_usb = r_gsb_ = r_besb = r_w3ssb = r_rows
        r_ubc, r_gbc, r_bbc = R(), R(), R()
        r_w3sbc = rg_w3sbc.res
        r_onesf, r_idf = R(), R()
        r_tmp = {"sq": r_t2, "c0": R(), "c1": R(), "c2": R()}
        tmp = {"sq": t2, "c0": c0, "c1": c1, "c2": c2}
        r_hbuf = [R(), R()]
        r_w1b = r_hbuf
        r_w2b = r_hbuf
        r_gs = [r_abuf, r_abuf]
        r_bst = [r_t2, r_t2]
        r_ctb = r_hbuf
        r_asl = r_hbuf
        r_x1b = [r_xnT, r_xnT]
        r_scales = {"w1": R(), "w2": R(), "h1": R(), "h2": R()}
        r_pstr, r_psbc = R("pstr"), R("psbc")
        r_psg = [R(), R()]
        r_psu = [R(), R()]

        p.dma("sync", "ld0", xs_v.rearrange("p (t d) -> p t d", t=NT),
              x_in.rearrange("(t p) d -> p t d", p=128), writes=[r_big])
        p.dma("sync", "ld0", xnT_v, xnT_i, writes=[r_xnT])
        p.dma("sync", "ld0", u_sb[:], u_i, writes=[r_usb])
        p.dma("sync", "ld0", g_sb[:], gamma_i, writes=[r_gsb_])
        p.dma("sync", "ld0", be_sb[:], beta_i, writes=[r_besb])
        p.dma("sync", "ld0", w3s_sb[:], w3s_i, writes=[r_w3ssb])
        p.dma("sync", "ld0", idf[:], iden_i, writes=[r_idf])
        for nm, ssb, sin in (("w1", w1s, w1s_i), ("w2", w2s, w2s_i),
                             ("h1", h1s, h1s_i), ("h2", h2s, h2s_i)):
            p.dma("sync", "ld0", ssb[:], sin, writes=[r_scales[nm]])
        p.op("vector", lambda e: e.memset(onesf_t[:], 1.0), writes=[r_onesf])

        def build_bc(src_sb, base, r_src, dst, r_dst):
            for half in range(2):
                p.op("tensor",
                     lambda e, src_sb=src_sb, half=half, base=base: e.matmul(
                         ps_bc[:, 512 * half:512 * half + 512],
                         onesf_t[base:base + 1, :],
                         src_sb[0:1, 512 * half:512 * half + 512], start=True, stop=True),
                     reads=[r_onesf, r_src], writes=[r_psbc])
            p.op("vector", lambda e, dst=dst: e.tensor_copy(dst[:], ps_bc[:]),
                 reads=[r_psbc], writes=[r_dst])

        build_bc(u_sb, 0, r_usb, u_bc, r_ubc)
        build_bc(g_sb, 32, r_gsb_, g_bc, r_gbc)
        build_bc(be_sb, 64, r_besb, b_bc, r_bbc)

        for tt in range(NT):
            par = tt % 2
            p.dma("sync", f"lda{par}",
                  asl[par].rearrange("p (h t) -> p h t", h=NH),
                  attnT_i[:, :, 128 * tt:128 * tt + 128], writes=[r_asl[par]])
            pst = ps_tr if par == 0 else ps_bc
            r_pst = r_pstr if par == 0 else r_psbc
            for h in range(NH):
                p.op("tensor",
                     lambda e, h=h, par=par, pst=pst: e.transpose(
                         pst[:, 64 * h:64 * h + 64],
                         asl[par].rearrange("p (h t) -> p h t", h=NH)[:, h, :],
                         idf[0:64, 0:64]),
                     reads=[r_asl[par], r_idf], writes=[r_pst])
            p.op("vector", lambda e, pst=pst: e.tensor_add(abuf[:], pst[:], u_bc[:]),
                 reads=[r_pst, r_ubc], writes=[r_abuf])
            ln_tile(p, abuf[:], t2[:], r_abuf, r_t2, tmp, r_tmp, 1e-3)
            p.op("vector", lambda e: e.tensor_mul(t2[:], t2[:], g_bc[:]),
                 reads=[r_t2, r_gbc], writes=[r_t2])
            p.op("vector", lambda e: e.tensor_add(t2[:], t2[:], b_bc[:]),
                 reads=[r_t2, r_bbc], writes=[r_t2])
            xt = xs_v[:, tt * D:(tt + 1) * D]
            p.op("vector", lambda e, xt=xt: e.tensor_add(xt, xt, t2[:]),
                 reads=[r_t2, r_big], writes=[r_big])
        stx_op = p.dma("sync", "stx", x1tmp, xs_v, reads=[r_big])

        for ot in range(32):
            par = ot % 2
            p.dma("gpsimd", f"lw1{par}", w1b[par].rearrange("p (a m) -> p a m", a=8),
                  w1t_i[ot].rearrange("(a p) m -> p a m", p=128), writes=[r_w1b[par]])
            p.dma("gpsimd", f"lw2{par}", w2b[par].rearrange("p (a m) -> p a m", a=8),
                  w2t_i[ot].rearrange("(a p) m -> p a m", p=128), writes=[r_w2b[par]])
            for tch in range(4):
                qar = tch % 2
                for i in range(8):
                    p.op("tensor",
                         lambda e, i=i, par=par, qar=qar, tch=tch: e.matmul(
                             ps_g[qar][:],
                             w1b[par].rearrange("p (a m) -> p a m", a=8)[:, i, :],
                             xnT_v[:, i, 512 * tch:512 * tch + 512],
                             start=(i == 0), stop=(i == 7)),
                         reads=[r_w1b[par], r_xnT], writes=[r_psg[qar]])
                for i in range(8):
                    p.op("tensor",
                         lambda e, i=i, par=par, qar=qar, tch=tch: e.matmul(
                             ps_u[qar][:],
                             w2b[par].rearrange("p (a m) -> p a m", a=8)[:, i, :],
                             xnT_v[:, i, 512 * tch:512 * tch + 512],
                             start=(i == 0), stop=(i == 7)),
                         reads=[r_w2b[par], r_xnT], writes=[r_psu[qar]])
                p.op("scalar",
                     lambda e, qar=qar, ot=ot: e.activation(
                         gsb[qar], ps_g[qar][:], AF.Silu, scale=w1s[:, ot:ot + 1]),
                     reads=[r_psg[qar], r_scales["w1"]], writes=[r_gs[qar]])
                dst = aT_v[:, ot, 512 * tch:512 * tch + 512]
                p.op("vector",
                     lambda e, qar=qar, ot=ot, dst=dst: e.scalar_tensor_tensor(
                         dst, ps_u[qar][:], w2s[:, ot:ot + 1], gsb[qar],
                         ALU.mult, ALU.mult),
                     reads=[r_psu[qar], r_scales["w2"], r_gs[qar]], writes=[r_big])

        build_bc(w3s_sb, 0, r_w3ssb, w3s_bc, r_w3sbc)
        st2_stores = []
        for o2 in range(32):
            par = o2 % 2
            p.dma("gpsimd", f"lh{par}", hbuf[par][:].rearrange("p (c m) -> p c m", c=32),
                  h1t_i[o2].rearrange("(c p) m -> p c m", p=128), writes=[r_hbuf[par]])
            for tch in range(4):
                qar = tch % 2
                for c in range(32):
                    p.op("tensor",
                         lambda e, c=c, par=par, qar=qar, tch=tch: e.matmul(
                             ps_g[qar][:],
                             hbuf[par][:].rearrange("p (c m) -> p c m", c=32)[:, c, :],
                             aT_v[:, c, 512 * tch:512 * tch + 512],
                             start=(c == 0), stop=(c == 31)),
                         reads=[r_hbuf[par], r_big], writes=[r_psg[qar]])
                p.op("scalar",
                     lambda e, qar=qar, o2=o2: e.activation(
                         bst[qar], ps_g[qar][:], AF.Gelu, scale=h1s[:, o2:o2 + 1]),
                     reads=[r_psg[qar], r_scales["h1"]], writes=[r_bst[qar]])
                st2_stores.append(
                    p.dma("sync", "stb", bt_d[o2, :, 512 * tch:512 * tch + 512],
                          bst[qar], reads=[r_bst[qar]]))

        p.dma("sync", "ldb2", big[:].rearrange("p (c t) -> p c t", c=32),
              bt_d.rearrange("c p t -> p c t"), writes=[r_big],
              extra_deps=st2_stores)
        st3_stores = []
        for o3 in range(32):
            par = o3 % 2
            p.dma("gpsimd", f"lh{par}", hbuf[par][:].rearrange("p (c m) -> p c m", c=32),
                  h2t_i[o3].rearrange("(c p) m -> p c m", p=128), writes=[r_hbuf[par]])
            for tch in range(4):
                qar = tch % 2
                for c in range(32):
                    p.op("tensor",
                         lambda e, c=c, par=par, qar=qar, tch=tch: e.matmul(
                             ps_g[qar][:],
                             hbuf[par][:].rearrange("p (c m) -> p c m", c=32)[:, c, :],
                             aT_v[:, c, 512 * tch:512 * tch + 512],
                             start=(c == 0), stop=(c == 31)),
                         reads=[r_hbuf[par], r_big], writes=[r_psg[qar]])
                p.op("scalar",
                     lambda e, qar=qar, o3=o3: e.activation(
                         bst[qar], ps_g[qar][:], AF.Gelu, scale=h2s[:, o3:o3 + 1]),
                     reads=[r_psg[qar], r_scales["h2"]], writes=[r_bst[qar]])
                st3_stores.append(
                    p.dma("sync", "stb", ct_d[o3, :, 512 * tch:512 * tch + 512],
                          bst[qar], reads=[r_bst[qar]]))

        p.dma("gpsimd", "lw3", w3t_v, w3t_i.rearrange("(c p) m -> p c m", p=128),
              writes=[r_big])
        for tt in range(NT):
            par = tt % 2
            p.dma("sync", f"lc{par}", ctb[par][:].rearrange("p (c m) -> p c m", c=32),
                  ct_d[:, :, 128 * tt:128 * tt + 128].rearrange("c p m -> p c m"),
                  writes=[r_ctb[par]], extra_deps=st3_stores)
            p.dma("sync", f"lx{par}", x1b[par], x1tmp[:, tt * D:(tt + 1) * D],
                  writes=[r_x1b[par]], extra_deps=[stx_op])
            pst = ps_tr if par == 0 else ps_bc
            r_pst = r_pstr if par == 0 else r_psbc
            for c in range(32):
                for half in range(2):
                    p.op("tensor",
                         lambda e, c=c, par=par, half=half, pst=pst: e.matmul(
                             pst[:, 512 * half:512 * half + 512],
                             ctb[par][:].rearrange("p (c m) -> p c m", c=32)[:, c, :],
                             w3t_v[:, c, 512 * half:512 * half + 512],
                             start=(c == 0), stop=(c == 31)),
                         reads=[r_ctb[par], r_big], writes=[r_pst])
            p.op("vector", lambda e, pst=pst: e.tensor_mul(abuf[:], pst[:], w3s_bc[:]),
                 reads=[r_pst, r_w3sbc], writes=[r_abuf])
            p.op("vector", lambda e, par=par: e.tensor_add(abuf[:], abuf[:], x1b[par]),
                 reads=[r_abuf, r_x1b[par]], writes=[r_abuf])
            p.dma("sync", "sty", x1_o[:, tt, :], abuf[:], reads=[r_abuf])
        st_ops = list(p.chan("sty").ops)
        p.op("sync", lambda e: e.nop(), extra_deps=st_ops)
        p.build(es)
    return nc



def build_pre1():
    nc = bass.Bass("TRN2", target_bir_lowering=False)
    x1_i = nc.dram_tensor("x1_i", [128, NT, D], F32, kind="ExternalInput").ap()
    wqkvT_i = nc.dram_tensor("wqkvT_i", [D, 3 * D], F32, kind="ExternalInput").ap()
    bqkv_i = nc.dram_tensor("bqkv_i", [1, 3 * D], F32, kind="ExternalInput").ap()
    scale_i = nc.dram_tensor("scale_i", [1, 1], F32, kind="ExternalInput").ap()
    iden_i = nc.dram_tensor("iden_i", [128, 128], F32, kind="ExternalInput").ap()
    xn1T_o = nc.dram_tensor("xn1T_o", [128, 8, TOK], BF16, kind="ExternalOutput").ap()
    q1_o = nc.dram_tensor("q1_o", [128, NT, D], F32, kind="ExternalOutput").ap()
    k1_o = nc.dram_tensor("k1_o", [128, NT, D], BF16, kind="ExternalOutput").ap()
    kv1_o = nc.dram_tensor("kv1_o", [128, NT, D], BF16, kind="ExternalOutput").ap()
    tot_o = nc.dram_tensor("tot_o", [1, 2 * D], F32, kind="ExternalOutput").ap()

    with contextlib.ExitStack() as es:
        sb = lambda name, shape, dt: es.enter_context(nc.sbuf_tensor(name, shape, dt))
        x1s = sb("x1s", [128, NT * D], F32)
        xn1T = sb("xn1T", [128, 8 * TOK], BF16)
        xn1T_v = xn1T[:].rearrange("p (c t) -> p c t", c=8)
        wq = sb("wq", [128, 8 * 3 * D], BF16)
        wq_v = wq[:].rearrange("p (a o) -> p a o", a=8)
        biasS = sb("biasS", [128, 2 * D], F32)
        bias_v = sb("bias_v", [128, D], F32)
        brow = sb("brow", [1, 3 * D], F32)
        srow = sb("srow", [1, 1], F32)
        scol = sb("scol", [128, 1], F32)
        onesf_t = sb("onesf", [1, 128], F32)
        onesb = sb("onesb", [128, 1], BF16)
        idb = sb("idb", [128, 128], BF16)
        xn_t = sb("xn_t", [128, D], BF16)
        qs = sb("qs", [128, 512], F32)
        m0 = sb("m0", [128, 512], F32)
        e0 = sb("e0", [128, 512], F32)
        r0 = sb("r0", [128, 512], F32)
        q1st = sb("q1st", [128, D], F32)
        k1st = sb("k1st", [128, D], F32)
        k1bf = sb("k1bf", [128, D], BF16)
        kvst = sb("kvst", [128, D], BF16)
        totst = sb("totst", [1, 2 * D], F32)
        c0 = sb("c0", [128, 1], F32)
        c1 = sb("c1", [128, 1], F32)
        c2 = sb("c2", [128, 1], F32)
        sq = sb("sq", [128, D], F32)
        ps_mm = [es.enter_context(nc.psum_tensor(f"ps_mm{i}", [128, 512], F32)) for i in range(2)]
        ps_tp = es.enter_context(nc.psum_tensor("ps_tp", [128, D], BF16))
        ps_tot = es.enter_context(nc.psum_tensor("ps_tot", [1, 2 * D], F32))

        p = Prog(nc)
        R = lambda n="": Res(n)
        r_x1s, r_xn1T, r_wq = R(), R(), R()
        r_biasS, r_biasv, r_brow, r_srow, r_scol = R(), R(), R(), R(), R()
        r_onesf, r_onesb, r_idb, r_xnt = R(), R(), R(), R()
        r_qs, r_m0, r_e0, r_r0 = R(), R(), R(), R()
        r_q1st, r_k1st, r_k1bf, r_kvst, r_totst = R(), R(), R(), R(), R()
        r_tmp = {"sq": R(), "c0": R(), "c1": R(), "c2": R()}
        tmp = {"sq": sq, "c0": c0, "c1": c1, "c2": c2}
        r_ps = [R(), R()]
        r_pstp, r_pstot = R(), R()

        p.dma("sync", "ld0", x1s[:].rearrange("p (t d) -> p t d", t=NT), x1_i,
              writes=[r_x1s])
        p.dma("gpsimd", "ldw", wq_v, wqkvT_i.rearrange("(a p) o -> p a o", p=128),
              writes=[r_wq])
        p.dma("sync", "ld0", brow[:], bqkv_i, writes=[r_brow])
        p.dma("sync", "ld0", srow[:], scale_i, writes=[r_srow])
        p.dma("gpsimd", "ldw", idb[:], iden_i, writes=[r_idb])
        p.op("vector", lambda e: e.memset(onesf_t[:], 1.0), writes=[r_onesf])
        p.op("vector", lambda e: e.memset(onesb[:], 1.0), writes=[r_onesb])

        p.op("tensor", lambda e: e.matmul(ps_mm[0][:, 0:1], onesf_t[:], srow[:],
                                          start=True, stop=True),
             reads=[r_onesf, r_srow], writes=[r_ps[0]])
        p.op("vector", lambda e: e.tensor_copy(scol[:], ps_mm[0][:, 0:1]),
             reads=[r_ps[0]], writes=[r_scol])
        for jc in range(6):
            par = jc % 2
            p.op("tensor",
                 lambda e, jc=jc, par=par: e.matmul(
                     ps_mm[par][:], onesf_t[:], brow[0:1, 512 * jc:512 * jc + 512],
                     start=True, stop=True),
                 reads=[r_onesf, r_brow], writes=[r_ps[par]])
            if jc < 4:
                p.op("vector",
                     lambda e, jc=jc, par=par: e.tensor_scalar_mul(
                         biasS[:, 512 * jc:512 * jc + 512], ps_mm[par][:], scol[:]),
                     reads=[r_ps[par], r_scol], writes=[r_biasS])
            else:
                p.op("vector",
                     lambda e, jc=jc, par=par: e.tensor_copy(
                         bias_v[:, 512 * (jc - 4):512 * (jc - 4) + 512], ps_mm[par][:]),
                     reads=[r_ps[par]], writes=[r_biasv])

        for tt in range(NT):
            x_ap = x1s[:, tt * D:(tt + 1) * D]
            ln_tile(p, x_ap, xn_t[:], r_x1s, r_xnt, tmp, r_tmp, 1e-5)
            for c in range(8):
                p.op("tensor",
                     lambda e, c=c: e.transpose(ps_tp[:, c * 128:(c + 1) * 128],
                                                xn_t[:, c * 128:(c + 1) * 128], idb[:]),
                     reads=[r_xnt, r_idb], writes=[r_pstp])
            dst = xn1T_v[:, :, tt * 128:(tt + 1) * 128]
            src = ps_tp[:].rearrange("p (c t) -> p c t", c=8)
            p.op("vector", lambda e, dst=dst, src=src: e.tensor_copy(dst, src),
                 reads=[r_pstp], writes=[r_xn1T])

            def elu1(jc, par, dst_ap, r_dst):
                p.op("vector",
                     lambda e, jc=jc, par=par: e.scalar_tensor_tensor(
                         qs[:], ps_mm[par][:], scol[:], biasS[:, 512 * jc:512 * jc + 512],
                         ALU.mult, ALU.add),
                     reads=[r_ps[par], r_scol, r_biasS], writes=[r_qs])
                p.op("vector", lambda e: e.tensor_scalar_min(m0[:], qs[:], 0.0),
                     reads=[r_qs], writes=[r_m0])
                p.op("scalar", lambda e: e.activation(e0[:], m0[:], AF.Exp),
                     reads=[r_m0], writes=[r_e0])
                p.op("vector", lambda e: e.tensor_scalar_max(r0[:], qs[:], 0.0),
                     reads=[r_qs], writes=[r_r0])
                p.op("vector", lambda e, dst_ap=dst_ap: e.tensor_add(dst_ap, e0[:], r0[:]),
                     reads=[r_e0, r_r0], writes=[r_dst])

            for jc in range(6):
                par = jc % 2
                for i in range(8):
                    p.op("tensor",
                         lambda e, i=i, jc=jc, par=par, tt=tt: e.matmul(
                             ps_mm[par][:],
                             xn1T_v[:, i, tt * 128:(tt + 1) * 128],
                             wq_v[:, i, 512 * jc:512 * jc + 512],
                             start=(i == 0), stop=(i == 7)),
                         reads=[r_xn1T, r_wq], writes=[r_ps[par]])
                half = jc % 2
                if jc < 2:
                    elu1(jc, par, q1st[:, 512 * half:512 * half + 512], r_q1st)
                elif jc < 4:
                    elu1(jc, par, k1st[:, 512 * half:512 * half + 512], r_k1st)
                else:
                    p.op("vector",
                         lambda e, par=par, half=half: e.tensor_add(
                             m0[:], ps_mm[par][:], bias_v[:, 512 * half:512 * half + 512]),
                         reads=[r_ps[par], r_biasv], writes=[r_m0])
                    p.op("vector",
                         lambda e, half=half: e.tensor_mul(
                             kvst[:, 512 * half:512 * half + 512], m0[:],
                             k1st[:, 512 * half:512 * half + 512]),
                         reads=[r_m0, r_k1st], writes=[r_kvst])
            p.op("vector", lambda e: e.tensor_copy(k1bf[:], k1st[:]),
                 reads=[r_k1st], writes=[r_k1bf])
            for half in range(2):
                p.op("tensor",
                     lambda e, half=half, tt=tt: e.matmul(
                         ps_tot[0:1, 512 * half:512 * half + 512], onesb[:],
                         k1bf[:, 512 * half:512 * half + 512],
                         start=(tt == 0), stop=(tt == NT - 1)),
                     reads=[r_onesb, r_k1bf], writes=[r_pstot])
                p.op("tensor",
                     lambda e, half=half, tt=tt: e.matmul(
                         ps_tot[0:1, 1024 + 512 * half:1024 + 512 * half + 512], onesb[:],
                         kvst[:, 512 * half:512 * half + 512],
                         start=(tt == 0), stop=(tt == NT - 1)),
                     reads=[r_onesb, r_kvst], writes=[r_pstot])
            p.dma("sync", "st", q1_o[:, tt, :], q1st[:], reads=[r_q1st])
            p.dma("sync", "st", k1_o[:, tt, :], k1bf[:], reads=[r_k1bf])
            p.dma("sync", "st", kv1_o[:, tt, :], kvst[:], reads=[r_kvst])
        p.op("vector", lambda e: e.tensor_copy(totst[:], ps_tot[:]),
             reads=[r_pstot], writes=[r_totst])
        p.dma("sync", "st", tot_o, totst[:], reads=[r_totst])
        p.dma("sync", "st", xn1T_o, xn1T_v, reads=[r_xn1T])
        st_ops = list(p.chan("st").ops)
        p.op("sync", lambda e: e.nop(), extra_deps=st_ops)
        p.build(es)
    return nc



def build_attn1():
    nc = bass.Bass("TRN2", target_bir_lowering=False)
    q1_i = nc.dram_tensor("q1_i", [128, NT, D], F32, kind="ExternalInput").ap()
    k1_i = nc.dram_tensor("k1_i", [128, NT, D], BF16, kind="ExternalInput").ap()
    kv1_i = nc.dram_tensor("kv1_i", [128, NT, D], BF16, kind="ExternalInput").ap()
    tots_i = nc.dram_tensor("tots_i", [3, 2 * D], F32, kind="ExternalInput").ap()
    tri_i = nc.dram_tensor("tri_i", [128, 128], F32, kind="ExternalInput").ap()
    wpT_i = nc.dram_tensor("wpT_i", [D, D], F32, kind="ExternalInput").ap()
    bp_i = nc.dram_tensor("bp_i", [1, D], F32, kind="ExternalInput").ap()
    x1_i = nc.dram_tensor("x1_i", [128, NT, D], F32, kind="ExternalInput").ap()
    iden_i = nc.dram_tensor("iden_i", [128, 128], F32, kind="ExternalInput").ap()
    x1a_o = nc.dram_tensor("x1a_o", [128, NT, D], F32, kind="ExternalOutput").ap()
    EPS = 1e-3

    with contextlib.ExitStack() as es:
        sb = lambda name, shape, dt: es.enter_context(nc.sbuf_tensor(name, shape, dt))
        wp = sb("wp", [128, 8 * D], BF16)
        wp_v = wp[:].rearrange("p (a o) -> p a o", a=8)
        bp_bc = sb("bp_bc", [128, D], F32)
        tri = sb("tri", [128, 128], BF16)
        idb = sb("idb", [128, 128], BF16)
        tots = sb("tots", [3, 2 * D], F32)
        carry_k = sb("carry_k", [1, D], F32)
        carry_kv = sb("carry_kv", [1, D], F32)
        onesf_t = sb("onesf", [3, 128], F32)
        onesb = sb("onesb", [128, 1], BF16)
        q1b = [sb(f"q1b{i}", [128, D], F32) for i in range(2)]
        k1b = [sb(f"k1b{i}", [128, D], BF16) for i in range(2)]
        kvb = [sb(f"kvb{i}", [128, D], BF16) for i in range(2)]
        x1b = [sb(f"x1b{i}", [128, D], F32) for i in range(2)]
        dq = sb("dq", [128, 512], F32)
        den = sb("den", [128, 16], F32)
        dinv = sb("dinv", [128, 16], F32)
        y0 = sb("y0", [128, 512], F32)
        y_bf = sb("y_bf", [128, D], BF16)
        yT = sb("yT", [128, 8 * 128], BF16)
        ot1 = sb("ot1", [128, D], F32)
        bprow = sb("bprow", [1, D], F32)
        ps_cs = es.enter_context(nc.psum_tensor("ps_cs", [128, 512], F32))
        ps_kv = es.enter_context(nc.psum_tensor("ps_kv", [128, 512], F32))
        ps_rk = es.enter_context(nc.psum_tensor("ps_rk", [1, D], F32))
        ps_rkv = es.enter_context(nc.psum_tensor("ps_rkv", [1, D], F32))
        ps_tp = es.enter_context(nc.psum_tensor("ps_tp", [128, D], BF16))
        ps_pr = es.enter_context(nc.psum_tensor("ps_pr", [128, 512], F32))

        p = Prog(nc)
        R = lambda n="": Res(n)
        r_wp, r_bpbc, r_tri, r_idb, r_tots = R(), R(), R(), R(), R()
        r_ck, r_ckv, r_onesf, r_bprow, r_onesb = R(), R(), R(), R(), R()
        r_q1b, r_k1b, r_kvb, r_x1b = [R(), R()], [R(), R()], [R(), R()], [R(), R()]
        r_dq, r_den, r_dinv, r_y0, r_ybf, r_yT, r_ot1 = R(), R(), R(), R(), R(), R(), R()
        r_pscs, r_pskv, r_psrk, r_psrkv, r_pstp, r_pspr = R(), R(), R(), R(), R(), R()

        p.dma("gpsimd", "ldw", wp_v, wpT_i.rearrange("(a p) o -> p a o", p=128),
              writes=[r_wp])
        p.dma("gpsimd", "ldw", tri[:], tri_i, writes=[r_tri])
        p.dma("gpsimd", "ldw", idb[:], iden_i, writes=[r_idb])
        p.dma("sync", "ld0", tots[:], tots_i, writes=[r_tots])
        p.dma("sync", "ld0", bprow[:], bp_i, writes=[r_bprow])
        p.op("vector", lambda e: e.memset(onesf_t[:], 1.0), writes=[r_onesf])
        p.op("vector", lambda e: e.memset(onesb[:], 1.0), writes=[r_onesb])

        for half in range(2):
            p.op("tensor",
                 lambda e, half=half: e.matmul(
                     ps_pr[:], onesf_t[0:1, :], bprow[0:1, 512 * half:512 * half + 512],
                     start=True, stop=True),
                 reads=[r_onesf, r_bprow], writes=[r_pspr])
            p.op("vector",
                 lambda e, half=half: e.tensor_copy(bp_bc[:, 512 * half:512 * half + 512], ps_pr[:]),
                 reads=[r_pspr], writes=[r_bpbc])
        for half in range(2):
            p.op("tensor",
                 lambda e, half=half: e.matmul(
                     ps_rk[0:1, 512 * half:512 * half + 512], onesf_t[:, 0:1],
                     tots[:, 512 * half:512 * half + 512], start=True, stop=True),
                 reads=[r_onesf, r_tots], writes=[r_psrk])
            p.op("tensor",
                 lambda e, half=half: e.matmul(
                     ps_rkv[0:1, 512 * half:512 * half + 512], onesf_t[:, 0:1],
                     tots[:, 1024 + 512 * half:1024 + 512 * half + 512], start=True, stop=True),
                 reads=[r_onesf, r_tots], writes=[r_psrkv])
        p.op("vector", lambda e: e.tensor_copy(carry_k[:], ps_rk[:]),
             reads=[r_psrk], writes=[r_ck])
        p.op("vector", lambda e: e.tensor_copy(carry_kv[:], ps_rkv[:]),
             reads=[r_psrkv], writes=[r_ckv])

        for tt in range(NT):
            par = tt % 2
            p.dma("sync", f"lq{par}", q1b[par][:], q1_i[:, tt, :], writes=[r_q1b[par]])
            p.dma("sync", f"lk{par}", k1b[par][:], k1_i[:, tt, :], writes=[r_k1b[par]])
            p.dma("sync", f"lv{par}", kvb[par][:], kv1_i[:, tt, :], writes=[r_kvb[par]])
            p.dma("sync", f"lx{par}", x1b[par][:], x1_i[:, tt, :], writes=[r_x1b[par]])
            for half in range(2):
                sl = slice(512 * half, 512 * half + 512)
                p.op("tensor",
                     lambda e, par=par, sl=sl: e.matmul(ps_cs[:], tri[:], k1b[par][:, sl],
                                                        start=True, stop=False),
                     reads=[r_tri, r_k1b[par]], writes=[r_pscs])
                p.op("tensor",
                     lambda e, sl=sl: e.matmul(ps_cs[:], onesf_t[0:1, :], carry_k[0:1, sl],
                                               start=False, stop=True),
                     reads=[r_onesf, r_ck], writes=[r_pscs])
                p.op("tensor",
                     lambda e, par=par, sl=sl: e.matmul(ps_kv[:], tri[:], kvb[par][:, sl],
                                                        start=True, stop=False),
                     reads=[r_tri, r_kvb[par]], writes=[r_pskv])
                p.op("tensor",
                     lambda e, sl=sl: e.matmul(ps_kv[:], onesf_t[0:1, :], carry_kv[0:1, sl],
                                               start=False, stop=True),
                     reads=[r_onesf, r_ckv], writes=[r_pskv])
                p.op("vector",
                     lambda e, par=par, sl=sl: e.tensor_mul(dq[:], q1b[par][:, sl], ps_cs[:]),
                     reads=[r_q1b[par], r_pscs], writes=[r_dq])
                dsl = slice(8 * half, 8 * half + 8)
                p.op("vector",
                     lambda e, dsl=dsl: e.tensor_reduce(
                         den[:, dsl].unsqueeze(2),
                         dq[:].rearrange("p (a b) -> p a b", a=8), AX.X, ALU.add),
                     reads=[r_dq], writes=[r_den])
                p.op("vector",
                     lambda e, dsl=dsl: e.tensor_scalar_max(den[:, dsl], den[:, dsl], EPS),
                     reads=[r_den], writes=[r_den])
                p.op("vector",
                     lambda e, dsl=dsl: e.reciprocal(dinv[:, dsl], den[:, dsl]),
                     reads=[r_den], writes=[r_dinv])
                p.op("vector",
                     lambda e, par=par, sl=sl: e.tensor_mul(y0[:], q1b[par][:, sl], ps_kv[:]),
                     reads=[r_q1b[par], r_pskv], writes=[r_y0])
                bview = dinv[:, dsl].unsqueeze(2).broadcast_to([128, 8, 64])
                p.op("vector",
                     lambda e, sl=sl, bview=bview: e.tensor_tensor(
                         y_bf[:, sl].rearrange("p (a b) -> p a b", a=8),
                         y0[:].rearrange("p (a b) -> p a b", a=8), bview, ALU.mult),
                     reads=[r_y0, r_dinv], writes=[r_ybf])
            if tt < NT - 1:
                for half in range(2):
                    sl = slice(512 * half, 512 * half + 512)
                    p.op("tensor",
                         lambda e, par=par, sl=sl: e.matmul(
                             ps_rk[0:1, sl], onesb[:], k1b[par][:, sl],
                             start=False, stop=False),
                         reads=[r_onesb, r_k1b[par]], writes=[r_psrk])
                    p.op("tensor",
                         lambda e, par=par, sl=sl: e.matmul(
                             ps_rkv[0:1, sl], onesb[:], kvb[par][:, sl],
                             start=False, stop=False),
                         reads=[r_onesb, r_kvb[par]], writes=[r_psrkv])
                p.op("vector", lambda e: e.tensor_copy(carry_k[:], ps_rk[:]),
                     reads=[r_psrk], writes=[r_ck])
                p.op("vector", lambda e: e.tensor_copy(carry_kv[:], ps_rkv[:]),
                     reads=[r_psrkv], writes=[r_ckv])
            for c in range(8):
                p.op("tensor",
                     lambda e, c=c: e.transpose(ps_tp[:, c * 128:(c + 1) * 128],
                                                y_bf[:, c * 128:(c + 1) * 128], idb[:]),
                     reads=[r_ybf, r_idb], writes=[r_pstp])
            p.op("vector", lambda e: e.tensor_copy(yT[:], ps_tp[:]),
                 reads=[r_pstp], writes=[r_yT])
            for half in range(2):
                sl = slice(512 * half, 512 * half + 512)
                for c in range(8):
                    p.op("tensor",
                         lambda e, c=c, sl=sl: e.matmul(
                             ps_pr[:], yT[:].rearrange("p (c t) -> p c t", c=8)[:, c, :],
                             wp_v[:, c, sl], start=(c == 0), stop=(c == 7)),
                         reads=[r_yT, r_wp], writes=[r_pspr])
                p.op("vector",
                     lambda e, sl=sl: e.tensor_add(ot1[:, sl], ps_pr[:], bp_bc[:, sl]),
                     reads=[r_pspr, r_bpbc], writes=[r_ot1])
            p.op("vector", lambda e, par=par: e.tensor_add(ot1[:], ot1[:], x1b[par][:]),
                 reads=[r_ot1, r_x1b[par]], writes=[r_ot1])
            p.dma("sync", "st", x1a_o[:, tt, :], ot1[:], reads=[r_ot1])
        st_ops = list(p.chan("st").ops)
        p.op("sync", lambda e: e.nop(), extra_deps=st_ops)
        p.build(es)
    return nc



def build_ffn1():
    nc = bass.Bass("TRN2", target_bir_lowering=False)
    xn1T_i = nc.dram_tensor("xn1T_i", [128, 8, TOK], BF16, kind="ExternalInput").ap()
    wg_i = nc.dram_tensor("wg_i", [8, 32, D, 128], F32, kind="ExternalInput").ap()
    wu_i = nc.dram_tensor("wu_i", [8, 32, D, 128], F32, kind="ExternalInput").ap()
    wd_i = nc.dram_tensor("wd_i", [8, 32, 128, D], F32, kind="ExternalInput").ap()
    x1a_i = nc.dram_tensor("x1a_i", [128, NT, D], F32, kind="ExternalInput").ap()
    y_o = nc.dram_tensor("y_o", [128, NT, D], F32, kind="ExternalOutput").ap()

    with contextlib.ExitStack() as es:
        sb = lambda name, shape, dt: es.enter_context(nc.sbuf_tensor(name, shape, dt))
        x1a = sb("x1a", [128, NT * D], F32)
        xn1T = sb("xn1T", [128, 8 * TOK], BF16)
        xn1T_v = xn1T[:].rearrange("p (c t) -> p c t", c=8)
        hT = sb("hT", [128, 32 * 1024], BF16)
        hT_v = hT[:].rearrange("p (c t) -> p c t", c=32)
        wd_res = sb("wd_res", [128, 16 * D], BF16)
        wd_v = wd_res[:].rearrange("p (h m) -> p h m", h=16)
        wgb = [sb(f"wgb{i}", [128, 8 * 128], BF16) for i in range(2)]
        wub = [sb(f"wub{i}", [128, 8 * 128], BF16) for i in range(2)]
        gs = sb("gs", [128, 1024], F32)
        ps_g = es.enter_context(nc.psum_tensor("ps_g", [128, 1024], F32))
        ps_u = es.enter_context(nc.psum_tensor("ps_u", [128, 1024], F32))
        ps_d = [es.enter_context(nc.psum_tensor(f"ps_d{i}", [128, 1024], F32)) for i in range(2)]

        p = Prog(nc)
        R = lambda n="": Res(n)
        r_x1a, r_xn1T, r_hT, r_gs, r_wd = R("x1a"), R(), R("hT"), R(), R("wd")
        r_wgb, r_wub = [R(), R()], [R(), R()]
        r_psg, r_psu = R(), R()
        r_psd = [R(), R()]

        p.dma("sync", "ld0", x1a[:].rearrange("p (t d) -> p t d", t=NT), x1a_i,
              writes=[r_x1a])
        p.dma("sync", "ld0", xn1T_v, xn1T_i, writes=[r_xn1T])

        for e_ in range(8):
            for half in range(2):
                hoff = 1024 * half
                for hid in range(32):
                    par = hid % 2
                    p.dma("gpsimd", f"lg{par}", wgb[par][:].rearrange("p (a m) -> p a m", a=8),
                          wg_i[e_, hid].rearrange("(a p) m -> p a m", p=128),
                          writes=[r_wgb[par]])
                    p.dma("gpsimd", f"lu{par}", wub[par][:].rearrange("p (a m) -> p a m", a=8),
                          wu_i[e_, hid].rearrange("(a p) m -> p a m", p=128),
                          writes=[r_wub[par]])
                    for i in range(8):
                        for q2 in range(2):
                            p.op("tensor",
                                 lambda e, i=i, q2=q2, par=par, hoff=hoff: e.matmul(
                                     ps_g[:, 512 * q2:512 * q2 + 512],
                                     wgb[par][:].rearrange("p (a m) -> p a m", a=8)[:, i, :],
                                     xn1T_v[:, i, hoff + 512 * q2:hoff + 512 * q2 + 512],
                                     start=(i == 0), stop=(i == 7)),
                                 reads=[r_wgb[par], r_xn1T], writes=[r_psg])
                    for i in range(8):
                        for q2 in range(2):
                            p.op("tensor",
                                 lambda e, i=i, q2=q2, par=par, hoff=hoff: e.matmul(
                                     ps_u[:, 512 * q2:512 * q2 + 512],
                                     wub[par][:].rearrange("p (a m) -> p a m", a=8)[:, i, :],
                                     xn1T_v[:, i, hoff + 512 * q2:hoff + 512 * q2 + 512],
                                     start=(i == 0), stop=(i == 7)),
                                 reads=[r_wub[par], r_xn1T], writes=[r_psu])
                    p.op("scalar", lambda e: e.activation(gs[:], ps_g[:], AF.Silu),
                         reads=[r_psg], writes=[r_gs])
                    p.op("vector",
                         lambda e, hid=hid: e.tensor_mul(hT_v[:, hid, :], gs[:], ps_u[:]),
                         reads=[r_gs, r_psu], writes=[r_hT])
                for hg in range(2):
                    p.dma("gpsimd", "lwd", wd_v,
                          wd_i[e_, 16 * hg:16 * hg + 16].rearrange("h p m -> p h m"),
                          writes=[r_wd])
                    for t8 in range(8):
                        tpar = t8 % 2
                        tt = 8 * half + t8
                        for hl in range(16):
                            hid = 16 * hg + hl
                            for q2 in range(2):
                                p.op("tensor",
                                     lambda e, hid=hid, hl=hl, q2=q2, t8=t8, tpar=tpar: e.matmul(
                                         ps_d[tpar][:, 512 * q2:512 * q2 + 512],
                                         hT_v[:, hid, 128 * t8:128 * t8 + 128],
                                         wd_v[:, hl, 512 * q2:512 * q2 + 512],
                                         start=(hl == 0), stop=(hl == 15)),
                                     reads=[r_hT, r_wd], writes=[r_psd[tpar]])
                        xt = x1a[:, tt * D:(tt + 1) * D]
                        p.op("vector",
                             lambda e, xt=xt, tpar=tpar: e.tensor_add(xt, xt, ps_d[tpar][:]),
                             reads=[r_psd[tpar], r_x1a], writes=[r_x1a])
        p.dma("sync", "st", y_o, x1a[:].rearrange("p (t d) -> p t d", t=NT),
              reads=[r_x1a])
        st_ops = list(p.chan("st").ops)
        p.op("sync", lambda e: e.nop(), extra_deps=st_ops)
        p.build(es)
    return nc

def _f32(a):
    return np.ascontiguousarray(a, dtype=np.float32)


def stage_kTf(kT_cores):
    return np.ascontiguousarray(np.concatenate(kT_cores, axis=2))


def stage_vaug(v_cores):
    v_all = np.concatenate(v_cores, axis=1)
    vaug = np.zeros((128, 4, 64, 260), dtype=BF)
    for hp in range(4):
        for hh in range(4):
            h = 4 * hp + hh
            vaug[:, hp, :, 65 * hh:65 * hh + 64] = v_all[:, :, 64 * h:64 * h + 64]
            vaug[:, hp, :, 65 * hh + 64] = np.ones((), dtype=BF)
    return vaug


def stage_grT(RT, pos):
    ks = np.arange(S)
    k16 = ks >> 4
    kl = ks & 15
    out = np.empty((len(PAIRS), S, 128), dtype=np.float32)
    for j, (a, l) in enumerate(PAIRS):
        C = 8191 - 512 * a + 512 * l
        for jj in range(128):
            r = C - (128 * pos + jj)
            out[j, :, jj] = RT[np.minimum(r + k16, 2046), kl]
    return out


def stage_b0(RT):
    return np.ascontiguousarray(RT[2046, np.arange(128) % 16, None].astype(np.float32))


def _tileize(x):
    return np.ascontiguousarray(x.reshape(NT, 128, D).transpose(1, 0, 2))


def _untile(x):
    return np.asarray(x).transpose(1, 0, 2).reshape(TOK, D)


def _pretile(wT):
    return np.ascontiguousarray(wT.reshape(wT.shape[0], 32, 128).transpose(1, 0, 2))


def _scale128(s):
    return np.ascontiguousarray(np.asarray(s, dtype=np.float32).reshape(-1, 128).T)


_BUILD_CACHE = {}


def _get(name, fn):
    if name not in _BUILD_CACHE:
        _BUILD_CACHE[name] = fn()
    return _BUILD_CACHE[name]


def kernel_unfused(L0_attn_x, L0_attn_Wq, L0_attn_Wk, L0_attn_Wv, L0_attn_rel_table,
           L0_attn_gamma, L0_attn_beta, L0_ffn_w1_w, L0_ffn_w1_s, L0_ffn_w2_w,
           L0_ffn_w2_s, L0_ffn_h1_w, L0_ffn_h1_s, L0_ffn_h2_w, L0_ffn_h2_s,
           L0_ffn_w3_w, L0_ffn_w3_s, L1_attn_Wqkv, L1_attn_bqkv, L1_attn_Wproj,
           L1_attn_bproj, L1_attn_scale, L1_ffn_Wg, L1_ffn_Wu, L1_ffn_Wd):
    x = _f32(L0_attn_x)
    wq, wk, wv = _f32(L0_attn_Wq), _f32(L0_attn_Wk), _f32(L0_attn_Wv)
    RT = _f32(L0_attn_rel_table)
    iden = np.eye(128, dtype=np.float32)
    tri = np.triu(np.ones((128, 128), dtype=np.float32))
    cores = [(c // 4, c % 4) for c in range(NCORE)]
    cid = list(range(NCORE))

    nc1 = _get("pre0", build_pre0)
    maps1 = [dict(x_in=np.ascontiguousarray(x[b, 2048 * p:2048 * (p + 1)]),
                  wq_in=wq, wk_in=wk, wv_in=wv, iden_in=iden)
             for (b, p) in cores]
    r1 = run_bass_kernel_spmd(nc1, maps1, core_ids=cid).results

    kTf = [stage_kTf([r1[4 * b + j]["kT_o"] for j in range(4)]) for b in range(2)]
    vaug = [stage_vaug([r1[4 * b + j]["v_o"] for j in range(4)]) for b in range(2)]
    grT = [stage_grT(RT, p) for p in range(4)]
    b0 = stage_b0(RT)
    nc2 = _get("attn0", build_attn0)
    maps2 = [dict(qT_i=np.asarray(r1[c]["qT_o"]), kT_i=kTf[b], vaug_i=vaug[b],
                  grT_i=grT[p], b0_i=b0)
             for c, (b, p) in enumerate(cores)]
    r2 = run_bass_kernel_spmd(nc2, maps2, core_ids=cid).results

    nc3 = _get("ffn0", build_ffn0)
    com3 = dict(
        gamma_i=_f32(L0_attn_gamma)[None], beta_i=_f32(L0_attn_beta)[None],
        w1t_i=_pretile(_f32(L0_ffn_w1_w).T), w2t_i=_pretile(_f32(L0_ffn_w2_w).T),
        w1s_i=_scale128(L0_ffn_w1_s), w2s_i=_scale128(L0_ffn_w2_s),
        h1t_i=_pretile(_f32(L0_ffn_h1_w).T), h2t_i=_pretile(_f32(L0_ffn_h2_w).T),
        h1s_i=_scale128(L0_ffn_h1_s), h2s_i=_scale128(L0_ffn_h2_s),
        w3t_i=np.ascontiguousarray(_f32(L0_ffn_w3_w).T), w3s_i=_f32(L0_ffn_w3_s)[None],
        iden_i=iden)
    maps3 = [dict(x_in=np.ascontiguousarray(x[b, 2048 * p:2048 * (p + 1)]),
                  attnT_i=np.asarray(r2[c]["attnT_o"]), u_i=np.asarray(r2[c]["u_o"]),
                  xnT_i=np.asarray(r1[c]["xnT_o"]), **com3)
             for c, (b, p) in enumerate(cores)]
    r3 = run_bass_kernel_spmd(nc3, maps3, core_ids=cid).results

    nc4 = _get("pre1", build_pre1)
    com4 = dict(wqkvT_i=np.ascontiguousarray(_f32(L1_attn_Wqkv).T),
                bqkv_i=_f32(L1_attn_bqkv)[None],
                scale_i=_f32(L1_attn_scale).reshape(1, 1), iden_i=iden)
    maps4 = [dict(x1_i=np.asarray(r3[c]["x1_o"]), **com4) for c in range(NCORE)]
    r4 = run_bass_kernel_spmd(nc4, maps4, core_ids=cid).results

    nc5 = _get("attn1", build_attn1)
    com5 = dict(tri_i=tri, wpT_i=np.ascontiguousarray(_f32(L1_attn_Wproj).T),
                bp_i=_f32(L1_attn_bproj)[None], iden_i=iden)
    maps5 = []
    for c, (b, p) in enumerate(cores):
        tots = np.zeros((3, 2 * D), dtype=np.float32)
        for j in range(p):
            tots[j] = np.asarray(r4[4 * b + j]["tot_o"])[0]
        maps5.append(dict(q1_i=np.asarray(r4[c]["q1_o"]), k1_i=np.asarray(r4[c]["k1_o"]),
                          kv1_i=np.asarray(r4[c]["kv1_o"]), tots_i=tots,
                          x1_i=np.asarray(r3[c]["x1_o"]), **com5))
    r5 = run_bass_kernel_spmd(nc5, maps5, core_ids=cid).results

    nc6 = _get("ffn1", build_ffn1)
    Wg, Wu, Wd = _f32(L1_ffn_Wg), _f32(L1_ffn_Wu), _f32(L1_ffn_Wd)
    com6 = dict(wg_i=np.ascontiguousarray(Wg.reshape(8, D, 32, 128).transpose(0, 2, 1, 3)),
                wu_i=np.ascontiguousarray(Wu.reshape(8, D, 32, 128).transpose(0, 2, 1, 3)),
                wd_i=np.ascontiguousarray(Wd.reshape(8, 32, 128, D)))
    maps6 = [dict(xn1T_i=np.asarray(r4[c]["xn1T_o"]), x1a_i=np.asarray(r5[c]["x1a_o"]),
                  **com6) for c in range(NCORE)]
    r6 = run_bass_kernel_spmd(nc6, maps6, core_ids=cid).results

    out = np.empty((2, S, D), dtype=np.float32)
    for c, (b, p) in enumerate(cores):
        out[b, 2048 * p:2048 * (p + 1)] = _untile(r6[c]["y_o"])
    return out



def build_fused():
    nc = bass.Bass("TRN2", target_bir_lowering=False)
    DH = 4096
    T = nc.dram_tensor
    x_in = T("x_in", [TOK, D], F32, kind="ExternalInput").ap()
    wq_in = T("wq_in", [D, D], F32, kind="ExternalInput").ap()
    wk_in = T("wk_in", [D, D], F32, kind="ExternalInput").ap()
    wv_in = T("wv_in", [D, D], F32, kind="ExternalInput").ap()
    iden_in = T("iden_in", [128, 128], F32, kind="ExternalInput").ap()
    grT_i = T("grT_i", [len(PAIRS), S, 128], F32, kind="ExternalInput").ap()
    b0_i = T("b0_i", [128, 1], F32, kind="ExternalInput").ap()
    gamma_i = T("gamma_i", [1, D], F32, kind="ExternalInput").ap()
    beta_i = T("beta_i", [1, D], F32, kind="ExternalInput").ap()
    w1t_i = T("w1t_i", [32, D, 128], F32, kind="ExternalInput").ap()
    w2t_i = T("w2t_i", [32, D, 128], F32, kind="ExternalInput").ap()
    w1s_i = T("w1s_i", [128, 32], F32, kind="ExternalInput").ap()
    w2s_i = T("w2s_i", [128, 32], F32, kind="ExternalInput").ap()
    h1t_i = T("h1t_i", [32, DH, 128], F32, kind="ExternalInput").ap()
    h2t_i = T("h2t_i", [32, DH, 128], F32, kind="ExternalInput").ap()
    h1s_i = T("h1s_i", [128, 32], F32, kind="ExternalInput").ap()
    h2s_i = T("h2s_i", [128, 32], F32, kind="ExternalInput").ap()
    w3t_i = T("w3t_i", [DH, D], F32, kind="ExternalInput").ap()
    w3s_i = T("w3s_i", [1, D], F32, kind="ExternalInput").ap()
    wqkvT_i = T("wqkvT_i", [D, 3 * D], F32, kind="ExternalInput").ap()
    bqkv_i = T("bqkv_i", [1, 3 * D], F32, kind="ExternalInput").ap()
    scale_i = T("scale_i", [1, 1], F32, kind="ExternalInput").ap()
    wpT_i = T("wpT_i", [D, D], F32, kind="ExternalInput").ap()
    bp_i = T("bp_i", [1, D], F32, kind="ExternalInput").ap()
    tri_i = T("tri_i", [128, 128], F32, kind="ExternalInput").ap()
    mask_i = T("mask_i", [4, 1], F32, kind="ExternalInput").ap()
    wg_i = T("wg_i", [8, 32, D, 128], F32, kind="ExternalInput").ap()
    wu_i = T("wu_i", [8, 32, D, 128], F32, kind="ExternalInput").ap()
    wd_i = T("wd_i", [8, 32, 128, D], F32, kind="ExternalInput").ap()
    y_o = T("y_o", [128, NT, D], F32, kind="ExternalOutput").ap()
    kt_c = T("kt_c", [128, 8, TOK], BF16, kind="Internal").ap()
    kt_g = T("kt_g", [512, 8 * TOK], BF16, kind="Internal").ap()
    va_c = T("va_c", [128, 8 * NT * 130], BF16, kind="Internal").ap()
    va_g = T("va_g", [512, 8 * NT * 130], BF16, kind="Internal").ap()
    qt_d = T("qt_d", [128, 8, TOK], BF16, kind="Internal").ap()
    attnT_d = T("attnT_d", [64, NH, TOK], F32, kind="Internal").ap()
    x1tmp = T("x1tmp", [128, NT * D], F32, kind="Internal").ap()
    bt_d = T("bt_d", [32, 128, TOK], BF16, kind="Internal").ap()
    ct_d = T("ct_d", [32, 128, TOK], BF16, kind="Internal").ap()
    x1_d = T("x1_d", [128, NT, D], F32, kind="Internal").ap()
    q1_d = T("q1_d", [128, NT, D], F32, kind="Internal").ap()
    k1_d = T("k1_d", [128, NT, D], BF16, kind="Internal").ap()
    kv1_d = T("kv1_d", [128, NT, D], BF16, kind="Internal").ap()
    tot_c = T("tot_c", [1, 2 * D], F32, kind="Internal").ap()
    tot_g = T("tot_g", [4, 2 * D], F32, kind="Internal").ap()

    SCALE = 1.0 / 32.0
    RG = [[0, 1, 2, 3], [4, 5, 6, 7]]
    KB = 1024
    EPS1 = 1e-3

    with contextlib.ExitStack() as es:
        sb = lambda name, shape, dt: es.enter_context(nc.sbuf_tensor(name, shape, dt))
        idb = sb("idb", [128, 128], BF16)
        idf = sb("idf", [128, 128], F32)
        onesf_t = sb("onesf", [65, 128], F32)
        onesb = sb("onesb", [128, 1], BF16)
        c0 = sb("c0", [128, 1], F32)
        c1 = sb("c1", [128, 1], F32)
        c2 = sb("c2", [128, 1], F32)
        scol = sb("scol", [128, 1], F32)
        wsc = {k: sb(f"wsc_{k}", [128, 32], F32) for k in ("w1", "w2", "h1", "h2")}
        sqt = sb("sqt", [128, D], F32)
        ARENA_B = 206784
        arena_t = sb("arena", [128, ARENA_B // 2], BF16)
        A = Arena(arena_t, ARENA_B)
        psum_t = es.enter_context(nc.psum_tensor("psar", [128, 4096], F32))
        PA = Arena(psum_t, 16384)

        p = Prog(nc)
        R = lambda n="": Res(n)
        r_idb, r_idf, r_onesf, r_onesb = R(), R(), R(), R()
        r_c0, r_c1, r_c2, r_scol, r_sq = R(), R(), R(), R(), R()
        r_wsc = {k: R() for k in wsc}
        tmp = {"sq": sqt, "c0": c0, "c1": c1, "c2": c2}
        r_tmp = {"sq": r_sq, "c0": r_c0, "c1": r_c1, "c2": r_c2}

        p.dma("gpsimd", "ldi", idb[:], iden_in, writes=[r_idb])
        p.dma("sync", "ldi", idf[:], iden_in, writes=[r_idf])
        p.op("vector", lambda e: e.memset(onesf_t[:], 1.0), writes=[r_onesf])
        p.op("vector", lambda e: e.memset(onesb[:], 1.0), writes=[r_onesb])

        rg_xs = A.alloc(64 * KB, "xs")
        xs = rg_xs.view(F32, NT * D)
        rg_xnT = A.alloc(32 * KB, "xnT", side="top")
        xnT_v = rg_xnT.view(BF16, 8 * TOK).rearrange("p (c t) -> p c t", c=8)
        rg_w1p = A.alloc(16 * KB, "wproj")
        wpb_v = rg_w1p.view(BF16, 8 * D).rearrange("p (a o) -> p a o", a=8)
        rg_vloc = A.alloc(8 * NT * 130 * 2, "vloc")
        vloc_v = rg_vloc.view(BF16, 8 * NT * 130).rearrange(
            "p (a t c) -> p a t c", a=8, t=NT)
        rg_xnt1 = A.alloc(2 * KB, "xn_t")
        xn_t = rg_xnt1.view(BF16, D)
        rg_ev = [A.alloc(KB, f"ev{i}") for i in range(4)]
        ev = [rg.view(BF16, 512) for rg in rg_ev]
        r_xs, r_xnT, r_wp = rg_xs.res, rg_xnT.res, rg_w1p.res
        r_vloc, r_xnt = rg_vloc.res, rg_xnt1.res
        r_ev = [rg.res for rg in rg_ev]
        rg_pmm = [PA.alloc(2048, f"pmm{i}") for i in range(4)]
        pmm = [rg.view(F32, 512) for rg in rg_pmm]
        r_pmm = [rg.res for rg in rg_pmm]
        rg_pst = PA.alloc(2048, "pst")
        pst = rg_pst.view(BF16, D)
        r_pst = rg_pst.res

        p.dma("sync", "ldx", xs.rearrange("p (t d) -> p t d", t=NT),
              x_in.rearrange("(t p) d -> p t d", p=128), writes=[r_xs])
        p.op("vector", lambda e: e.memset(vloc_v, 1.0), writes=[r_vloc])
        for tt in range(NT):
            x_ap = xs[:, tt * D:(tt + 1) * D]
            ln_tile(p, x_ap, xn_t, r_xs, r_xnt, tmp, r_tmp, 1e-5)
            for c in range(8):
                p.op("tensor",
                     lambda e, c=c: e.transpose(pst[:, c * 128:(c + 1) * 128],
                                                xn_t[:, c * 128:(c + 1) * 128], idb[:]),
                     reads=[r_xnt, r_idb], writes=[r_pst])
            dst = xnT_v[:, :, tt * 128:(tt + 1) * 128]
            src = pst.rearrange("p (c t) -> p c t", c=8)
            p.op("vector", lambda e, dst=dst, src=src: e.tensor_copy(dst, src),
                 reads=[r_pst], writes=[r_xnT])
        rg_xs.free()
        rg_psg2 = [PA.alloc(2048, f"psg2{i}", side="top") for i in range(2)]
        psg2 = [rg.view(F32, 512) for rg in rg_psg2]
        r_psg2 = [rg.res for rg in rg_psg2]
        rg_bigh = A.alloc(64 * KB, "bigh")
        bigh_v = rg_bigh.view(BF16, 32 * 1024).rearrange("p (c t) -> p c t", c=32)
        rg_hb = [A.alloc(8 * KB, f"hb{i}") for i in range(2)]
        hb_v = [rg.view(BF16, 32 * 128) for rg in rg_hb]
        gsb = [sqt[:, 0:512], sqt[:, 512:1024]]
        rg_bst = [A.alloc(KB, f"bst{i}") for i in range(2)]
        bst = [rg.view(BF16, 512) for rg in rg_bst]
        for nm, sin in (("w1", w1s_i), ("w2", w2s_i), ("h1", h1s_i), ("h2", h2s_i)):
            p.dma("sync", "lds", wsc[nm][:], sin, writes=[r_wsc[nm]])
        r_bigh = rg_bigh.res
        r_hb = [rg.res for rg in rg_hb]
        r_gs = [r_sq, r_sq]
        r_bst = [rg.res for rg in rg_bst]
        st2_stores = []
        st3_stores = []

        def ffn_gen():
            for half in range(2):
                hoff = 1024 * half
                for ot in range(32):
                    par = ot % 2
                    p.dma("gpsimd", f"lw1{par}",
                          hb_v[par][:, 0:1024].rearrange("p (a m) -> p a m", a=8),
                          w1t_i[ot].rearrange("(a p) m -> p a m", p=128), writes=[r_hb[par]])
                    p.dma("gpsimd", f"lw2{par}",
                          hb_v[par][:, 1024:2048].rearrange("p (a m) -> p a m", a=8),
                          w2t_i[ot].rearrange("(a p) m -> p a m", p=128), writes=[r_hb[par]])
                    for t2c in range(2):
                        for i in range(8):
                            p.op("tensor",
                                 lambda e, i=i, par=par, t2c=t2c, hoff=hoff: e.matmul(
                                     psg2[0],
                                     hb_v[par][:, 0:1024].rearrange("p (a m) -> p a m", a=8)[:, i, :],
                                     xnT_v[:, i, hoff + 512 * t2c:hoff + 512 * t2c + 512],
                                     start=(i == 0), stop=(i == 7)),
                                 reads=[r_hb[par], r_xnT], writes=[r_psg2[0]])
                        yield
                        for i in range(8):
                            p.op("tensor",
                                 lambda e, i=i, par=par, t2c=t2c, hoff=hoff: e.matmul(
                                     psg2[1],
                                     hb_v[par][:, 1024:2048].rearrange("p (a m) -> p a m", a=8)[:, i, :],
                                     xnT_v[:, i, hoff + 512 * t2c:hoff + 512 * t2c + 512],
                                     start=(i == 0), stop=(i == 7)),
                                 reads=[r_hb[par], r_xnT], writes=[r_psg2[1]])
                        p.op("scalar",
                             lambda e, t2c=t2c, ot=ot: e.activation(
                                 gsb[t2c], psg2[0], AF.Silu, scale=wsc["w1"][:, ot:ot + 1]),
                             reads=[r_psg2[0], r_wsc["w1"]], writes=[r_gs[t2c]])
                        dst = bigh_v[:, ot, 512 * t2c:512 * t2c + 512]
                        p.op("vector",
                             lambda e, t2c=t2c, ot=ot, dst=dst: e.scalar_tensor_tensor(
                                 dst, psg2[1], wsc["w2"][:, ot:ot + 1], gsb[t2c],
                                 ALU.mult, ALU.mult),
                             reads=[r_psg2[1], r_wsc["w2"], r_gs[t2c]], writes=[r_bigh])
                        yield
                for o2 in range(32):
                    par = o2 % 2
                    p.dma("gpsimd", f"lh{par}", hb_v[par].rearrange("p (c m) -> p c m", c=32),
                          h1t_i[o2].rearrange("(c p) m -> p c m", p=128), writes=[r_hb[par]])
                    for t2c in range(2):
                        qar = t2c % 2
                        for c in range(32):
                            p.op("tensor",
                                 lambda e, c=c, par=par, qar=qar, t2c=t2c: e.matmul(
                                     psg2[qar],
                                     hb_v[par].rearrange("p (c m) -> p c m", c=32)[:, c, :],
                                     bigh_v[:, c, 512 * t2c:512 * t2c + 512],
                                     start=(c == 0), stop=(c == 31)),
                                 reads=[r_hb[par], r_bigh], writes=[r_psg2[qar]])
                            if c % 8 == 7 and c < 31:
                                yield
                        p.op("scalar",
                             lambda e, qar=qar, o2=o2: e.activation(
                                 bst[qar], psg2[qar], AF.Gelu, scale=wsc["h1"][:, o2:o2 + 1]),
                             reads=[r_psg2[qar], r_wsc["h1"]], writes=[r_bst[qar]])
                        st2_stores.append(
                            p.dma("sync", "stb",
                                  bt_d[o2, :, hoff + 512 * t2c:hoff + 512 * t2c + 512],
                                  bst[qar], reads=[r_bst[qar]]))
                        yield
                p.dma("sync", "ldb2", bigh_v,
                      bt_d[:, :, hoff:hoff + 1024].rearrange("c p t -> p c t"),
                      writes=[r_bigh], extra_deps=list(st2_stores))
                yield
                for o3 in range(32):
                    par = o3 % 2
                    p.dma("gpsimd", f"lh{par}", hb_v[par].rearrange("p (c m) -> p c m", c=32),
                          h2t_i[o3].rearrange("(c p) m -> p c m", p=128), writes=[r_hb[par]])
                    for t2c in range(2):
                        qar = t2c % 2
                        for c in range(32):
                            p.op("tensor",
                                 lambda e, c=c, par=par, qar=qar, t2c=t2c: e.matmul(
                                     psg2[qar],
                                     hb_v[par].rearrange("p (c m) -> p c m", c=32)[:, c, :],
                                     bigh_v[:, c, 512 * t2c:512 * t2c + 512],
                                     start=(c == 0), stop=(c == 31)),
                                 reads=[r_hb[par], r_bigh], writes=[r_psg2[qar]])
                            if c % 8 == 7 and c < 31:
                                yield
                        p.op("scalar",
                             lambda e, qar=qar, o3=o3: e.activation(
                                 bst[qar], psg2[qar], AF.Gelu, scale=wsc["h2"][:, o3:o3 + 1]),
                             reads=[r_psg2[qar], r_wsc["h2"]], writes=[r_bst[qar]])
                        st3_stores.append(
                            p.dma("sync", "stb",
                                  ct_d[o3, :, hoff + 512 * t2c:hoff + 512 * t2c + 512],
                                  bst[qar], reads=[r_bst[qar]]))
                        yield

        gen = ffn_gen()
        NQUANTA = 1282
        TOTKC = 8 * 4 * 64
        q_done = 0
        kc_glob = 0
        gen_state = {"done": False}

        def step_gen(n=1):
            for _ in range(n):
                if gen_state["done"]:
                    return
                if next(gen, "END") == "END":
                    gen_state["done"] = True
                    return
        kt_stores = []
        for wi, win in ((0, wq_in), (1, wk_in)):
            p.dma("gpsimd", "ldw", wpb_v, win.rearrange("(a p) o -> p a o", p=128),
                  writes=[r_wp])
            for ft in range(8):
                for tc in range(4):
                    k4 = (ft * 4 + tc) % 4
                    for i in range(8):
                        p.op("tensor",
                             lambda e, i=i, ft=ft, tc=tc, k4=k4: e.matmul(
                                 pmm[k4], wpb_v[:, i, ft * 128:ft * 128 + 128],
                                 xnT_v[:, i, tc * 512:(tc + 1) * 512],
                                 start=(i == 0), stop=(i == 7)),
                             reads=[r_wp, r_xnT], writes=[r_pmm[k4]])
                    p.op("scalar", lambda e, k4=k4: e.activation(ev[k4], pmm[k4], AF.Copy),
                         reads=[r_pmm[k4]], writes=[r_ev[k4]])
                    dst_d = qt_d if wi == 0 else kt_c.rearrange("p c t -> p c t")
                    kt_stores.append(p.dma(
                        "sync", "stk", dst_d[:, ft, tc * 512:(tc + 1) * 512],
                        ev[k4], reads=[r_ev[k4]]))
                    step_gen(2)
        p.dma("gpsimd", "ldw", wpb_v, wv_in.rearrange("(a p) o -> p a o", p=128),
              writes=[r_wp])
        for tt in range(NT):
            for oc in range(2):
                k4 = (tt * 2 + oc) % 4
                for i in range(8):
                    p.op("tensor",
                         lambda e, i=i, tt=tt, oc=oc, k4=k4: e.matmul(
                             pmm[k4], xnT_v[:, i, tt * 128:(tt + 1) * 128],
                             wpb_v[:, i, oc * 512:oc * 512 + 512],
                             start=(i == 0), stop=(i == 7)),
                         reads=[r_wp, r_xnT], writes=[r_pmm[k4]])
                dst = vloc_v[:, 4 * oc:4 * oc + 4, tt, :] \
                    .rearrange("p a (h c) -> p a h c", h=2)[:, :, :, 0:64]
                p.op("scalar",
                     lambda e, dst=dst, k4=k4: e.activation(
                         dst, pmm[k4].rearrange("p (a h c) -> p a h c", a=4, h=2),
                         AF.Copy),
                     reads=[r_pmm[k4]], writes=[r_vloc])
                step_gen(1)
        step_gen(20)
        va_store = p.dma("sync", "stk", va_c, rg_vloc.view(BF16, 8 * NT * 130),
                         reads=[r_vloc])
        r_ktg, r_vag = R("ktg"), R("vag")
        cc1 = p.op("gpsimd",
                   lambda e: e.collective_compute(
                       "AllGather", ALU.bypass, replica_groups=RG,
                       ins=[kt_c.rearrange("p c t -> p (c t)")], outs=[kt_g]),
                   writes=[r_ktg], extra_deps=kt_stores)
        cc2 = p.op("gpsimd",
                   lambda e: e.collective_compute(
                       "AllGather", ALU.bypass, replica_groups=RG,
                       ins=[va_c], outs=[va_g]),
                   writes=[r_vag], extra_deps=[va_store])
        step_gen(120)
        rg_w1p.free()
        rg_vloc.free()
        rg_xnt1.free()
        for rg in rg_ev:
            rg.free()
        for rg in rg_pmm:
            rg.free()
        rg_pst.free()

        rg_kTp = A.alloc(16 * KB, "kTp")
        kTp2 = rg_kTp.view(BF16, S)
        rg_vaug = A.alloc(NT * 130 * 4 * 2, "vaug")
        vaug2 = rg_vaug.view(BF16, 64 * 130).rearrange("p (k c) -> p k c", k=64)
        rg_qT2 = A.alloc(4 * KB, "qT2")
        qT2 = rg_qT2.view(BF16, TOK)
        rg_aT2 = A.alloc(16 * KB, "aT2")
        aT2_v = rg_aT2.view(F32, 2 * TOK, parts=64).rearrange("p (c t) -> p c t", c=2)
        rg_pT = [A.alloc(4 * KB, f"pT{i}") for i in range(2)]
        pT = [rg.view(BF16, 2048) for rg in rg_pT]
        rg_b0c = A.alloc(64, "b0c")
        b0c = rg_b0c.view(F32, 1)
        rg_b0cb = A.alloc(64, "b0cb")
        b0cb = rg_b0cb.view(BF16, 1)
        rg_usb = A.alloc(4 * KB, "u_sb", side="top")
        u_sb = rg_usb.view(F32, D, parts=1)
        rg_linv = A.alloc(2 * KB, "linv")
        linv = rg_linv.view(F32, 512, parts=1)
        rg_pvsb = A.alloc(2 * KB, "pv_sb")
        pv_sb = rg_pvsb.view(F32, 512, parts=64)
        rg_grT2 = A.alloc(4 * KB, "grT2")
        grT2 = rg_grT2.view(F32, 8 * 128)
        rg_dT2 = A.alloc(2 * KB, "dT2")
        dT2 = rg_dT2.view(BF16, 8 * 128)
        dT2_v = dT2.rearrange("p (k m) -> p k m", k=8)
        rg_pss = [PA.alloc(4096, f"ps_s{i}") for i in range(2)]
        ps_s2 = [rg.view(F32, 1024) for rg in rg_pss]
        rg_pso = PA.alloc(4096, "ps_o")
        ps_o = rg_pso.view(F32, 1024)
        r_kTp, r_vaug, r_qT2, r_aT2 = rg_kTp.res, rg_vaug.res, rg_qT2.res, rg_aT2.res
        r_pT = [rg.res for rg in rg_pT]
        r_b0c, r_b0cb, r_usb = rg_b0c.res, rg_b0cb.res, rg_usb.res
        r_linv, r_pvsb = rg_linv.res, rg_pvsb.res
        r_grT2, r_dT2 = rg_grT2.res, rg_dT2.res
        r_ps_s2 = [rg.res for rg in rg_pss]
        r_ps_o = rg_pso.res

        p.dma("sync", "ldq", b0c, b0_i, writes=[r_b0c])
        p.dma("gpsimd", "ldb", b0cb, b0_i, writes=[r_b0cb])
        at_stores = []
        for s8 in range(8):
            for r in range(4):
                p.dma("sync", "ldk", kTp2[:, 2048 * r:2048 * (r + 1)],
                      kt_g[128 * r:128 * (r + 1), :]
                      .rearrange("p (c t) -> p c t", c=8)[:, s8, :],
                      writes=[r_kTp], extra_deps=[cc1])
                p.dma("sync", "ldv", vaug2[:, 16 * r:16 * (r + 1), :],
                      va_g[128 * r:128 * (r + 1), :]
                      .rearrange("p (a t c) -> p a t c", a=8, t=NT)[:, s8],
                      writes=[r_vaug], extra_deps=[cc2])
            p.dma("sync", "ldq", qT2, qt_d[:, s8, :], writes=[r_qT2],
                  extra_deps=kt_stores)
            for kc in range(64):
                p.op("tensor",
                     lambda e, kc=kc: e.matmul(ps_s2[0][0:1, 0:130], b0cb,
                                               vaug2[:, kc, :],
                                               start=(kc == 0), stop=(kc == 63)),
                     reads=[r_b0cb, r_vaug], writes=[r_ps_s2[0]])
            dst = u_sb[0:1, 128 * s8:128 * s8 + 128].rearrange("a (h c) -> a h c", h=2)
            src = ps_s2[0][0:1, 0:130].rearrange("a (h c) -> a h c", h=2)[:, :, 0:64]
            p.op("vector", lambda e, dst=dst, src=src: e.tensor_copy(dst, src),
                 reads=[r_ps_s2[0]], writes=[r_usb])
            assert rg_pss[1].start == rg_pss[0].start + 4096
            ps_pair = rg_pss[0].view(F32, 2048)

            def emit_pv(jj):
                pb = jj % 2
                for sub in range(4):
                    kc = 2 * jj + (sub >> 1)
                    hi = sub & 1
                    p.op("tensor",
                         lambda e, kc=kc, hi=hi, pb=pb, sub=sub: e.matmul(
                             ps_o[0:65, 512 * hi:512 * hi + 512],
                             vaug2[:, kc, 65 * hi:65 * hi + 65],
                             pT[pb][:, 512 * sub:512 * sub + 512],
                             start=(kc == 0), stop=(kc == 63)),
                         reads=[r_vaug, r_pT[pb]], writes=[r_ps_o])

            for qm in range(4):
                for jj in range(32):
                    for half2 in range(2):
                        kc = 2 * jj + half2
                        for hi in range(2):
                            lo = 64 * hi
                            p.op("tensor",
                                 lambda e, kc=kc, hi=hi, lo=lo, qm=qm, half2=half2: e.matmul(
                                     ps_s2[half2][:, 512 * hi:512 * hi + 512],
                                     kTp2[lo:lo + 64, 128 * kc:128 * kc + 128],
                                     qT2[lo:lo + 64, 512 * qm:512 * qm + 512],
                                     start=True, stop=True),
                                 reads=[r_kTp, r_qT2], writes=[r_ps_s2[half2]])
                    pb = jj % 2
                    p.op("scalar",
                         lambda e, pb=pb: e.activation(pT[pb], ps_pair, AF.Exp, scale=SCALE),
                         reads=[r_ps_s2[0], r_ps_s2[1]], writes=[r_pT[pb]])
                    if jj > 0:
                        emit_pv(jj - 1)
                    kc_glob += 2
                    while (not gen_state["done"]) and q_done < (kc_glob * NQUANTA) // TOTKC:
                        step_gen(1)
                        q_done += 1
                emit_pv(31)
                for hi in range(2):
                    p.op("vector",
                         lambda e, hi=hi: e.reciprocal(linv, ps_o[64:65, 512 * hi:512 * hi + 512]),
                         reads=[r_ps_o], writes=[r_linv])
                    p.op("scalar",
                         lambda e, hi=hi: e.activation(pv_sb, ps_o[0:64, 512 * hi:512 * hi + 512], AF.Copy),
                         reads=[r_ps_o], writes=[r_pvsb])
                    p.op("tensor",
                         lambda e: e.matmul(ps_s2[0][0:64, 0:512], onesf_t[0:1, 0:64], linv,
                                            start=True, stop=True),
                         reads=[r_onesf, r_linv], writes=[r_ps_s2[0]])
                    dst = aT2_v[:, hi, 512 * qm:512 * qm + 512]
                    p.op("vector",
                         lambda e, dst=dst: e.tensor_mul(dst, pv_sb, ps_s2[0][0:64, 0:512]),
                         reads=[r_pvsb, r_ps_s2[0]], writes=[r_aT2])
            for j, (a, l) in enumerate(PAIRS):
                if a >> 1 != s8:
                    continue
                hi = a & 1
                for cg in range(8):
                    p.dma("sync", "ldg", grT2.rearrange("p (c m) -> p c m", c=8),
                          grT_i[j, 1024 * cg:1024 * cg + 1024].rearrange("(c p) m -> p c m", p=128),
                          writes=[r_grT2])
                    p.op("vector",
                         lambda e: e.tensor_scalar(dT2, grT2, b0c, None, ALU.subtract),
                         reads=[r_grT2, r_b0c], writes=[r_dT2])
                    for c in range(8):
                        p.op("tensor",
                             lambda e, c=c, cg=cg, hi=hi: e.matmul(
                                 ps_o[0:64, 0:128],
                                 vaug2[:, 8 * cg + c, 65 * hi:65 * hi + 64],
                                 dT2_v[:, c, :],
                                 start=(cg == 0 and c == 0), stop=(cg == 7 and c == 7)),
                             reads=[r_vaug, r_dT2], writes=[r_ps_o])
                dst = aT2_v[:, hi, :].rearrange("p (t0 t1) -> p t0 t1", t1=16)[:, :, l]
                p.op("vector",
                     lambda e, dst=dst: e.tensor_tensor(dst, dst, ps_o[0:64, 0:128], ALU.add),
                     reads=[r_ps_o, r_aT2], writes=[r_aT2])
            at_stores.append(p.dma("sync", "sta", attnT_d[:, 2 * s8:2 * s8 + 2, :],
                                   aT2_v, reads=[r_aT2]))
        for _ in gen:
            pass
        rg_kTp.free()
        rg_vaug.free()
        rg_qT2.free()
        rg_aT2.free()
        for rg in rg_pT:
            rg.free()
        rg_b0c.free()
        rg_b0cb.free()
        rg_linv.free()
        rg_pvsb.free()
        rg_grT2.free()
        rg_dT2.free()
        for rg in rg_pss:
            rg.free()
        rg_pso.free()
        rg_xnT.free()
        rg_bigh.free()
        rg_pstr = PA.alloc(4096, "ps_tr")
        ps_tr = rg_pstr.view(F32, D)
        rg_psbc = PA.alloc(4096, "ps_bc")
        ps_bc = rg_psbc.view(F32, D)
        r_pstr, r_psbc = rg_pstr.res, rg_psbc.res

        rg_rowsA = A.alloc(4 * KB, "rowsA")
        rowsA = rg_rowsA.view(F32, D, parts=65)
        g_sb, be_sb = rowsA[32:33, :], rowsA[64:65, :]
        w3s_sb = rowsA[0:1, :]
        rg_ubc = A.alloc(4 * KB, "u_bc")
        u_bc = rg_ubc.view(F32, D)
        rg_gbc = A.alloc(4 * KB, "g_bc")
        g_bc = rg_gbc.view(F32, D)
        rg_bbc = A.alloc(4 * KB, "b_bc")
        b_bc = rg_bbc.view(F32, D)
        rg_w3sbc = A.alloc(4 * KB, "w3s_bc")
        w3s_bc = rg_w3sbc.view(F32, D)
        rg_abuf = A.alloc(4 * KB, "abuf")
        abuf = rg_abuf.view(F32, D)
        rg_t2 = A.alloc(4 * KB, "t2")
        t2 = rg_t2.view(F32, D)
        rg_asl = [A.alloc(8 * KB, f"asl{i}") for i in range(2)]
        asl = [rg.view(F32, NH * 128, parts=64) for rg in rg_asl]
        rg_xtb = [A.alloc(4 * KB, f"xtb{i}") for i in range(2)]
        xtb = [rg.view(F32, D) for rg in rg_xtb]
        r_rowsA = rg_rowsA.res
        r_rowsB = r_rowsA
        r_ubc, r_gbc, r_bbc = rg_ubc.res, rg_gbc.res, rg_bbc.res
        r_w3sbc = rg_w3sbc.res
        r_abuf, r_t2 = rg_abuf.res, rg_t2.res
        r_asl = [rg.res for rg in rg_asl]
        r_xtb = [rg.res for rg in rg_xtb]
        tmpA = {"sq": t2, "c0": c0, "c1": c1, "c2": c2}
        r_tmpA = {"sq": r_t2, "c0": r_c0, "c1": r_c1, "c2": r_c2}
        p.dma("sync", "lds", rowsA[32:33, :], gamma_i, writes=[r_rowsA])
        p.dma("sync", "lds", rowsA[64:65, :], beta_i, writes=[r_rowsA])
        p.dma("sync", "lds", w3s_sb, w3s_i, writes=[r_rowsB])

        def build_bc(src_sb, base, r_src, dst, r_dst):
            for half in range(2):
                p.op("tensor",
                     lambda e, src_sb=src_sb, half=half, base=base: e.matmul(
                         ps_bc[:, 512 * half:512 * half + 512],
                         onesf_t[base:base + 1, :],
                         src_sb[0:1, 512 * half:512 * half + 512], start=True, stop=True),
                     reads=[r_onesf, r_src], writes=[r_psbc])
            p.op("vector", lambda e, dst=dst: e.tensor_copy(dst, ps_bc[:]),
                 reads=[r_psbc], writes=[r_dst])

        build_bc(u_sb, 0, r_usb, u_bc, r_ubc)
        build_bc(g_sb, 32, r_rowsA, g_bc, r_gbc)
        build_bc(be_sb, 64, r_rowsA, b_bc, r_bbc)
        build_bc(w3s_sb, 0, r_rowsB, w3s_bc, r_w3sbc)
        rg_w3 = A.alloc(64 * KB, "w3t")
        w3t_v = rg_w3.view(BF16, 32 * D).rearrange("p (c m) -> p c m", c=32)
        r_w3 = rg_w3.res
        p.dma("gpsimd", "lw3", w3t_v, w3t_i.rearrange("(c p) m -> p c m", p=128),
              writes=[r_w3])
        rg_xtb4 = [A.alloc(4 * KB, f"xtb4_{i}") for i in range(2)]
        xtb4 = [rg.view(F32, D) for rg in rg_xtb4]
        r_xtb4 = [rg.res for rg in rg_xtb4]
        rg_abuf4 = A.alloc(4 * KB, "abuf4")
        abuf4 = rg_abuf4.view(F32, D)
        r_abuf4 = rg_abuf4.res
        x1_stores = []
        xa_stores = []
        for tt in range(NT):
            par = tt % 2
            p.dma("sync", f"lda{par}",
                  asl[par].rearrange("p (h t) -> p h t", h=NH),
                  attnT_d[:, :, 128 * tt:128 * tt + 128], writes=[r_asl[par]],
                  extra_deps=at_stores)
            p.dma("sync", f"ldxa{par}", xtb[par],
                  x_in.rearrange("(t p) d -> p t d", p=128)[:, tt, :],
                  writes=[r_xtb[par]])
            pstA = ps_tr if par == 0 else ps_bc
            r_pstA = r_pstr if par == 0 else r_psbc
            for h in range(NH):
                p.op("tensor",
                     lambda e, h=h, par=par, pstA=pstA: e.transpose(
                         pstA[:, 64 * h:64 * h + 64],
                         asl[par].rearrange("p (h t) -> p h t", h=NH)[:, h, :],
                         idf[0:64, 0:64]),
                     reads=[r_asl[par], r_idf], writes=[r_pstA])
            p.op("vector", lambda e, pstA=pstA: e.tensor_add(abuf, pstA[:], u_bc),
                 reads=[r_pstA, r_ubc], writes=[r_abuf])
            ln_tile(p, abuf, t2, r_abuf, r_t2, tmpA, r_tmpA, EPS1)
            p.op("vector", lambda e: e.tensor_mul(t2, t2, g_bc),
                 reads=[r_t2, r_gbc], writes=[r_t2])
            p.op("vector", lambda e: e.tensor_add(t2, t2, b_bc),
                 reads=[r_t2, r_bbc], writes=[r_t2])
            p.op("vector", lambda e, par=par: e.tensor_add(t2, t2, xtb[par]),
                 reads=[r_t2, r_xtb[par]], writes=[r_t2])
            xa_stores.append(p.dma("sync", "stxa", x1tmp[:, tt * D:(tt + 1) * D], t2,
                                   reads=[r_t2]))
            p.dma("sync", f"lc{par}", hb_v[par].rearrange("p (c m) -> p c m", c=32),
                  ct_d[:, :, 128 * tt:128 * tt + 128].rearrange("c p m -> p c m"),
                  writes=[r_hb[par]], extra_deps=st3_stores)
            p.dma("sync", f"ldx4{par}", xtb4[par], x1tmp[:, tt * D:(tt + 1) * D],
                  writes=[r_xtb4[par]], extra_deps=[xa_stores[tt]])
            for c in range(32):
                for half in range(2):
                    p.op("tensor",
                         lambda e, c=c, par=par, half=half, pstA=pstA: e.matmul(
                             pstA[:, 512 * half:512 * half + 512],
                             hb_v[par].rearrange("p (c m) -> p c m", c=32)[:, c, :],
                             w3t_v[:, c, 512 * half:512 * half + 512],
                             start=(c == 0), stop=(c == 31)),
                         reads=[r_hb[par], r_w3], writes=[r_pstA])
            p.op("vector", lambda e, pstA=pstA: e.tensor_mul(abuf4, pstA[:], w3s_bc),
                 reads=[r_pstA, r_w3sbc], writes=[r_abuf4])
            p.op("vector", lambda e, par=par: e.tensor_add(abuf4, abuf4, xtb4[par]),
                 reads=[r_abuf4, r_xtb4[par]], writes=[r_abuf4])
            x1_stores.append(p.dma("sync", "sty", x1_d[:, tt, :], abuf4, reads=[r_abuf4]))
        for rg in rg_asl:
            rg.free()
        rg_rowsA.free()
        rg_gbc.free()
        rg_bbc.free()
        rg_t2.free()
        rg_usb.free()
        rg_w3.free()
        for rg in rg_hb + rg_bst + rg_xtb + rg_xtb4:
            rg.free()
        rg_ubc.free()
        rg_w3sbc.free()
        rg_abuf.free()
        rg_abuf4.free()
        for rg in rg_psg2:
            rg.free()
        rg_pstr.free()
        rg_psbc.free()

        rg_xn1T = A.alloc(32 * KB, "xn1T", side="top")
        xn1T_v = rg_xn1T.view(BF16, 8 * TOK).rearrange("p (c t) -> p c t", c=8)
        rg_wqkv = A.alloc(48 * KB, "wqkv")
        wq4_v = rg_wqkv.view(BF16, 8 * 3 * D).rearrange("p (a o) -> p a o", a=8)
        rg_biasS = A.alloc(8 * KB, "biasS")
        biasS = rg_biasS.view(F32, 2 * D)
        rg_biasv = A.alloc(4 * KB, "bias_v")
        bias_v = rg_biasv.view(F32, D)
        rg_brow = A.alloc(12 * KB, "brow")
        brow = rg_brow.view(F32, 3 * D, parts=1)
        rg_srow = A.alloc(64, "srow")
        srow = rg_srow.view(F32, 1, parts=1)
        rg_x4 = [A.alloc(4 * KB, f"x4_{i}") for i in range(2)]
        x4b = [rg.view(F32, D) for rg in rg_x4]
        rg_xnt4 = A.alloc(2 * KB, "xnt4")
        xn_t4 = rg_xnt4.view(BF16, D)
        rg_qs = A.alloc(2 * KB, "qs")
        qs = rg_qs.view(F32, 512)
        rg_m0 = A.alloc(2 * KB, "m0")
        m0 = rg_m0.view(F32, 512)
        rg_e0 = A.alloc(2 * KB, "e0")
        e0 = rg_e0.view(F32, 512)
        rg_r0 = A.alloc(2 * KB, "r0")
        r0 = rg_r0.view(F32, 512)
        rg_q1st = A.alloc(4 * KB, "q1st")
        q1st = rg_q1st.view(F32, D)
        rg_k1st = A.alloc(4 * KB, "k1st")
        k1st = rg_k1st.view(F32, D)
        rg_k1bf = A.alloc(2 * KB, "k1bf")
        k1bf = rg_k1bf.view(BF16, D)
        rg_kvst = A.alloc(2 * KB, "kvst")
        kvst = rg_kvst.view(BF16, D)
        rg_totst = A.alloc(8 * KB, "totst")
        totst = rg_totst.view(F32, 2 * D, parts=1)
        r_xn1T, r_wq4 = rg_xn1T.res, rg_wqkv.res
        r_biasS, r_biasv, r_brow, r_srow = rg_biasS.res, rg_biasv.res, rg_brow.res, rg_srow.res
        r_x4 = [rg.res for rg in rg_x4]
        r_xnt4 = rg_xnt4.res
        r_qs, r_m0, r_e0, r_r0 = rg_qs.res, rg_m0.res, rg_e0.res, rg_r0.res
        r_q1st, r_k1st, r_k1bf, r_kvst, r_totst = (rg_q1st.res, rg_k1st.res,
                                                   rg_k1bf.res, rg_kvst.res, rg_totst.res)
        rg_pmm4 = [PA.alloc(2048, f"pmm4{i}") for i in range(2)]
        pmm4 = [rg.view(F32, 512) for rg in rg_pmm4]
        r_pmm4 = [rg.res for rg in rg_pmm4]
        rg_ptp4 = PA.alloc(2048, "ptp4")
        ptp4 = rg_ptp4.view(BF16, D)
        r_ptp4 = rg_ptp4.res
        rg_ptotk = PA.alloc(4096, "ptotk")
        ptotk = rg_ptotk.view(F32, D, parts=1)
        r_ptotk = rg_ptotk.res
        rg_ptotkv = PA.alloc(4096, "ptotkv")
        ptotkv = rg_ptotkv.view(F32, D, parts=1)
        r_ptotkv = rg_ptotkv.res

        p.dma("gpsimd", "ldw4", wq4_v, wqkvT_i.rearrange("(a p) o -> p a o", p=128),
              writes=[r_wq4])
        p.dma("sync", "ld4", brow, bqkv_i, writes=[r_brow])
        p.dma("sync", "ld4", srow, scale_i, writes=[r_srow])
        p.op("tensor", lambda e: e.matmul(pmm4[0][:, 0:1], onesf_t[0:1, :], srow,
                                          start=True, stop=True),
             reads=[r_onesf, r_srow], writes=[r_pmm4[0]])
        p.op("vector", lambda e: e.tensor_copy(scol[:], pmm4[0][:, 0:1]),
             reads=[r_pmm4[0]], writes=[r_scol])
        for jc in range(6):
            par = jc % 2
            p.op("tensor",
                 lambda e, jc=jc, par=par: e.matmul(
                     pmm4[par], onesf_t[0:1, :], brow[0:1, 512 * jc:512 * jc + 512],
                     start=True, stop=True),
                 reads=[r_onesf, r_brow], writes=[r_pmm4[par]])
            if jc < 4:
                p.op("vector",
                     lambda e, jc=jc, par=par: e.tensor_scalar_mul(
                         biasS[:, 512 * jc:512 * jc + 512], pmm4[par], scol[:]),
                     reads=[r_pmm4[par], r_scol], writes=[r_biasS])
            else:
                p.op("vector",
                     lambda e, jc=jc, par=par: e.tensor_copy(
                         bias_v[:, 512 * (jc - 4):512 * (jc - 4) + 512], pmm4[par]),
                     reads=[r_pmm4[par]], writes=[r_biasv])

        st4_stores = []
        for tt in range(NT):
            xpar = tt % 2
            p.dma("sync", f"lx4{xpar}", x4b[xpar], x1_d[:, tt, :],
                  writes=[r_x4[xpar]], extra_deps=x1_stores)
            ln_tile(p, x4b[xpar], xn_t4, r_x4[xpar], r_xnt4, tmp, r_tmp, 1e-5)
            for c in range(8):
                p.op("tensor",
                     lambda e, c=c: e.transpose(ptp4[:, c * 128:(c + 1) * 128],
                                                xn_t4[:, c * 128:(c + 1) * 128], idb[:]),
                     reads=[r_xnt4, r_idb], writes=[r_ptp4])
            dst = xn1T_v[:, :, tt * 128:(tt + 1) * 128]
            src = ptp4.rearrange("p (c t) -> p c t", c=8)
            p.op("vector", lambda e, dst=dst, src=src: e.tensor_copy(dst, src),
                 reads=[r_ptp4], writes=[r_xn1T])

            def elu1(jc, par, dst_ap, r_dst):
                p.op("vector",
                     lambda e, jc=jc, par=par: e.scalar_tensor_tensor(
                         qs, pmm4[par], scol[:], biasS[:, 512 * jc:512 * jc + 512],
                         ALU.mult, ALU.add),
                     reads=[r_pmm4[par], r_scol, r_biasS], writes=[r_qs])
                p.op("vector", lambda e: e.tensor_scalar_min(m0, qs, 0.0),
                     reads=[r_qs], writes=[r_m0])
                p.op("scalar", lambda e: e.activation(e0, m0, AF.Exp),
                     reads=[r_m0], writes=[r_e0])
                p.op("vector",
                     lambda e, dst_ap=dst_ap: e.scalar_tensor_tensor(
                         dst_ap, qs, 0.0, e0, ALU.max, ALU.add),
                     reads=[r_qs, r_e0], writes=[r_dst])

            for jc in range(6):
                par = jc % 2
                for i in range(8):
                    p.op("tensor",
                         lambda e, i=i, jc=jc, par=par, tt=tt: e.matmul(
                             pmm4[par],
                             xn1T_v[:, i, tt * 128:(tt + 1) * 128],
                             wq4_v[:, i, 512 * jc:512 * jc + 512],
                             start=(i == 0), stop=(i == 7)),
                         reads=[r_xn1T, r_wq4], writes=[r_pmm4[par]])
                half = jc % 2
                if jc < 2:
                    elu1(jc, par, q1st[:, 512 * half:512 * half + 512], r_q1st)
                elif jc < 4:
                    elu1(jc, par, k1st[:, 512 * half:512 * half + 512], r_k1st)
                else:
                    p.op("vector",
                         lambda e, par=par, half=half: e.tensor_add(
                             m0, pmm4[par], bias_v[:, 512 * half:512 * half + 512]),
                         reads=[r_pmm4[par], r_biasv], writes=[r_m0])
                    p.op("vector",
                         lambda e, half=half: e.tensor_mul(
                             kvst[:, 512 * half:512 * half + 512], m0,
                             k1st[:, 512 * half:512 * half + 512]),
                         reads=[r_m0, r_k1st], writes=[r_kvst])
            p.op("vector", lambda e: e.tensor_copy(k1bf, k1st),
                 reads=[r_k1st], writes=[r_k1bf])
            for half in range(2):
                p.op("tensor",
                     lambda e, half=half, tt=tt: e.matmul(
                         ptotk[0:1, 512 * half:512 * half + 512], onesb[:],
                         k1bf[:, 512 * half:512 * half + 512],
                         start=(tt == 0), stop=(tt == NT - 1)),
                     reads=[r_onesb, r_k1bf], writes=[r_ptotk])
                p.op("tensor",
                     lambda e, half=half, tt=tt: e.matmul(
                         ptotkv[0:1, 512 * half:512 * half + 512], onesb[:],
                         kvst[:, 512 * half:512 * half + 512],
                         start=(tt == 0), stop=(tt == NT - 1)),
                     reads=[r_onesb, r_kvst], writes=[r_ptotkv])
            st4_stores.append(p.dma("sync", "st4", q1_d[:, tt, :], q1st, reads=[r_q1st]))
            st4_stores.append(p.dma("sync", "st4", k1_d[:, tt, :], k1bf, reads=[r_k1bf]))
            st4_stores.append(p.dma("sync", "st4", kv1_d[:, tt, :], kvst, reads=[r_kvst]))
        p.op("vector", lambda e: e.tensor_copy(totst[0:1, 0:D], ptotk),
             reads=[r_ptotk], writes=[r_totst])
        p.op("vector", lambda e: e.tensor_copy(totst[0:1, D:2 * D], ptotkv),
             reads=[r_ptotkv], writes=[r_totst])
        tot_store = p.dma("sync", "st4", tot_c, totst, reads=[r_totst])
        r_totg = R("totg")
        cc3 = p.op("gpsimd",
                   lambda e: e.collective_compute(
                       "AllGather", ALU.bypass, replica_groups=RG,
                       ins=[tot_c], outs=[tot_g]),
                   writes=[r_totg], extra_deps=[tot_store])
        rg_wqkv.free()
        rg_biasS.free()
        rg_biasv.free()
        rg_brow.free()
        rg_srow.free()
        for rg in rg_x4:
            rg.free()
        rg_xnt4.free()
        rg_qs.free()
        rg_m0.free()
        rg_e0.free()
        rg_r0.free()
        rg_q1st.free()
        rg_k1st.free()
        rg_k1bf.free()
        rg_kvst.free()
        rg_totst.free()
        for rg in rg_pmm4:
            rg.free()
        rg_ptp4.free()
        rg_ptotk.free()
        rg_ptotkv.free()

        EPS = 1e-3
        rg_x1a = A.alloc(64 * KB, "x1a", side="top")
        x1a = rg_x1a.view(F32, NT * D)
        r_x1a = rg_x1a.res
        rg_wp5 = A.alloc(16 * KB, "wp5")
        wp5_v = rg_wp5.view(BF16, 8 * D).rearrange("p (a o) -> p a o", a=8)
        rg_bpbc = A.alloc(4 * KB, "bp_bc")
        bp_bc = rg_bpbc.view(F32, D)
        rg_tri = A.alloc(256, "tri")
        tri5 = rg_tri.view(BF16, 128)
        rg_mask = A.alloc(64, "mask")
        mask5 = rg_mask.view(F32, 1, parts=4)
        rg_totsg = A.alloc(8 * KB, "totsg")
        totsg = rg_totsg.view(F32, 2 * D, parts=4)
        rg_ck = A.alloc(4 * KB, "carry_k")
        carry_k = rg_ck.view(F32, D, parts=1)
        rg_ckv = A.alloc(4 * KB, "carry_kv")
        carry_kv = rg_ckv.view(F32, D, parts=1)
        rg_bprow = A.alloc(4 * KB, "bprow")
        bprow = rg_bprow.view(F32, D, parts=1)
        rg_q5 = [A.alloc(4 * KB, f"q5_{i}") for i in range(2)]
        q5 = [rg.view(F32, D) for rg in rg_q5]
        rg_k5 = [A.alloc(2 * KB, f"k5_{i}") for i in range(2)]
        k5 = [rg.view(BF16, D) for rg in rg_k5]
        rg_v5 = [A.alloc(2 * KB, f"v5_{i}") for i in range(2)]
        v5 = [rg.view(BF16, D) for rg in rg_v5]
        rg_x5 = [A.alloc(4 * KB, f"x5_{i}") for i in range(2)]
        x5 = [rg.view(F32, D) for rg in rg_x5]
        rg_dq = A.alloc(2 * KB, "dq")
        dq = rg_dq.view(F32, 512)
        rg_den = A.alloc(64, "den")
        den = rg_den.view(F32, 16)
        rg_dinv = A.alloc(64, "dinv")
        dinv = rg_dinv.view(F32, 16)
        rg_y0 = A.alloc(2 * KB, "y0")
        y0 = rg_y0.view(F32, 512)
        rg_ybf = A.alloc(2 * KB, "ybf")
        y_bf = rg_ybf.view(BF16, D)
        rg_yT = A.alloc(2 * KB, "yT")
        yT = rg_yT.view(BF16, 8 * 128)
        r_wp5, r_bpbc, r_tri, r_mask, r_totsg = (rg_wp5.res, rg_bpbc.res, rg_tri.res,
                                                 rg_mask.res, rg_totsg.res)
        r_ck, r_ckv, r_bprow = rg_ck.res, rg_ckv.res, rg_bprow.res
        r_q5 = [rg.res for rg in rg_q5]
        r_k5 = [rg.res for rg in rg_k5]
        r_v5 = [rg.res for rg in rg_v5]
        r_x5 = [rg.res for rg in rg_x5]
        r_dq, r_den, r_dinv, r_y0, r_ybf, r_yT = (rg_dq.res, rg_den.res, rg_dinv.res,
                                                  rg_y0.res, rg_ybf.res, rg_yT.res)
        rg_pcs = PA.alloc(2048, "ps_cs")
        ps_cs = rg_pcs.view(F32, 512)
        rg_pkv = PA.alloc(2048, "ps_kv")
        ps_kv = rg_pkv.view(F32, 512)
        rg_prk = PA.alloc(4096, "ps_rk")
        ps_rk = rg_prk.view(F32, D, parts=1)
        rg_prkv = PA.alloc(4096, "ps_rkv")
        ps_rkv = rg_prkv.view(F32, D, parts=1)
        rg_ptp5 = PA.alloc(2048, "ptp5")
        ptp5 = rg_ptp5.view(BF16, D)
        rg_ppr = PA.alloc(2048, "ps_pr")
        ps_pr = rg_ppr.view(F32, 512)
        r_pcs, r_pkv, r_prk, r_prkv = rg_pcs.res, rg_pkv.res, rg_prk.res, rg_prkv.res
        r_ptp5, r_ppr = rg_ptp5.res, rg_ppr.res

        p.dma("gpsimd", "ldw5", wp5_v, wpT_i.rearrange("(a p) o -> p a o", p=128),
              writes=[r_wp5])
        p.dma("gpsimd", "ldw5", tri5, tri_i, writes=[r_tri])
        p.dma("sync", "ld5", mask5, mask_i, writes=[r_mask])
        p.dma("sync", "ld5", totsg, tot_g, writes=[r_totsg], extra_deps=[cc3])
        p.dma("sync", "ld5", bprow, bp_i, writes=[r_bprow])
        for half in range(2):
            p.op("tensor",
                 lambda e, half=half: e.matmul(
                     ps_pr, onesf_t[0:1, :], bprow[0:1, 512 * half:512 * half + 512],
                     start=True, stop=True),
                 reads=[r_onesf, r_bprow], writes=[r_ppr])
            p.op("vector",
                 lambda e, half=half: e.tensor_copy(bp_bc[:, 512 * half:512 * half + 512], ps_pr),
                 reads=[r_ppr], writes=[r_bpbc])
        for half in range(2):
            p.op("tensor",
                 lambda e, half=half: e.matmul(
                     ps_rk[0:1, 512 * half:512 * half + 512], mask5,
                     totsg[:, 512 * half:512 * half + 512], start=True, stop=False),
                 reads=[r_mask, r_totsg], writes=[r_prk])
            p.op("tensor",
                 lambda e, half=half: e.matmul(
                     ps_rkv[0:1, 512 * half:512 * half + 512], mask5,
                     totsg[:, 1024 + 512 * half:1024 + 512 * half + 512], start=True, stop=False),
                 reads=[r_mask, r_totsg], writes=[r_prkv])
        p.op("vector", lambda e: e.tensor_copy(carry_k, ps_rk),
             reads=[r_prk], writes=[r_ck])
        p.op("vector", lambda e: e.tensor_copy(carry_kv, ps_rkv),
             reads=[r_prkv], writes=[r_ckv])

        for tt in range(NT):
            par = tt % 2
            p.dma("sync", f"lq{par}", q5[par], q1_d[:, tt, :], writes=[r_q5[par]],
                  extra_deps=st4_stores)
            p.dma("sync", f"lk{par}", k5[par], k1_d[:, tt, :], writes=[r_k5[par]],
                  extra_deps=st4_stores)
            p.dma("sync", f"lv{par}", v5[par], kv1_d[:, tt, :], writes=[r_v5[par]],
                  extra_deps=st4_stores)
            p.dma("sync", f"lx5{par}", x5[par], x1_d[:, tt, :], writes=[r_x5[par]])
            for half in range(2):
                sl = slice(512 * half, 512 * half + 512)
                p.op("tensor",
                     lambda e, par=par, sl=sl: e.matmul(ps_cs, tri5, k5[par][:, sl],
                                                        start=True, stop=False),
                     reads=[r_tri, r_k5[par]], writes=[r_pcs])
                p.op("tensor",
                     lambda e, sl=sl: e.matmul(ps_cs, onesf_t[0:1, :], carry_k[0:1, sl],
                                               start=False, stop=True),
                     reads=[r_onesf, r_ck], writes=[r_pcs])
                p.op("tensor",
                     lambda e, par=par, sl=sl: e.matmul(ps_kv, tri5, v5[par][:, sl],
                                                        start=True, stop=False),
                     reads=[r_tri, r_v5[par]], writes=[r_pkv])
                p.op("tensor",
                     lambda e, sl=sl: e.matmul(ps_kv, onesf_t[0:1, :], carry_kv[0:1, sl],
                                               start=False, stop=True),
                     reads=[r_onesf, r_ckv], writes=[r_pkv])
                p.op("vector",
                     lambda e, par=par, sl=sl: e.tensor_mul(dq, q5[par][:, sl], ps_cs),
                     reads=[r_q5[par], r_pcs], writes=[r_dq])
                dsl = slice(8 * half, 8 * half + 8)
                p.op("vector",
                     lambda e, dsl=dsl: e.tensor_reduce(
                         den[:, dsl].unsqueeze(2),
                         dq.rearrange("p (a b) -> p a b", a=8), AX.X, ALU.add),
                     reads=[r_dq], writes=[r_den])
                p.op("vector",
                     lambda e, dsl=dsl: e.tensor_scalar_max(den[:, dsl], den[:, dsl], EPS),
                     reads=[r_den], writes=[r_den])
                p.op("vector",
                     lambda e, dsl=dsl: e.reciprocal(dinv[:, dsl], den[:, dsl]),
                     reads=[r_den], writes=[r_dinv])
                p.op("vector",
                     lambda e, par=par, sl=sl: e.tensor_mul(y0, q5[par][:, sl], ps_kv),
                     reads=[r_q5[par], r_pkv], writes=[r_y0])
                bview = dinv[:, dsl].unsqueeze(2).broadcast_to([128, 8, 64])
                p.op("vector",
                     lambda e, sl=sl, bview=bview: e.tensor_tensor(
                         y_bf[:, sl].rearrange("p (a b) -> p a b", a=8),
                         y0.rearrange("p (a b) -> p a b", a=8), bview, ALU.mult),
                     reads=[r_y0, r_dinv], writes=[r_ybf])
            if tt < NT - 1:
                for half in range(2):
                    sl = slice(512 * half, 512 * half + 512)
                    p.op("tensor",
                         lambda e, par=par, sl=sl: e.matmul(
                             ps_rk[0:1, sl], onesb[:], k5[par][:, sl],
                             start=False, stop=False),
                         reads=[r_onesb, r_k5[par]], writes=[r_prk])
                    p.op("tensor",
                         lambda e, par=par, sl=sl: e.matmul(
                             ps_rkv[0:1, sl], onesb[:], v5[par][:, sl],
                             start=False, stop=False),
                         reads=[r_onesb, r_v5[par]], writes=[r_prkv])
                p.op("vector", lambda e: e.tensor_copy(carry_k, ps_rk),
                     reads=[r_prk], writes=[r_ck])
                p.op("vector", lambda e: e.tensor_copy(carry_kv, ps_rkv),
                     reads=[r_prkv], writes=[r_ckv])
            for c in range(8):
                p.op("tensor",
                     lambda e, c=c: e.transpose(ptp5[:, c * 128:(c + 1) * 128],
                                                y_bf[:, c * 128:(c + 1) * 128], idb[:]),
                     reads=[r_ybf, r_idb], writes=[r_ptp5])
            p.op("vector", lambda e: e.tensor_copy(yT, ptp5),
                 reads=[r_ptp5], writes=[r_yT])
            xt = x1a[:, tt * D:(tt + 1) * D]
            for half in range(2):
                sl = slice(512 * half, 512 * half + 512)
                for c in range(8):
                    p.op("tensor",
                         lambda e, c=c, sl=sl: e.matmul(
                             ps_pr, yT.rearrange("p (c t) -> p c t", c=8)[:, c, :],
                             wp5_v[:, c, sl], start=(c == 0), stop=(c == 7)),
                         reads=[r_yT, r_wp5], writes=[r_ppr])
                p.op("vector",
                     lambda e, sl=sl, xt=xt: e.tensor_add(xt[:, sl], ps_pr, bp_bc[:, sl]),
                     reads=[r_ppr, r_bpbc], writes=[r_x1a])
            p.op("vector", lambda e, xt=xt, par=par: e.tensor_add(xt, xt, x5[par]),
                 reads=[r_x5[par], r_x1a], writes=[r_x1a])
        rg_wp5.free()
        rg_bpbc.free()
        rg_tri.free()
        rg_mask.free()
        rg_totsg.free()
        rg_ck.free()
        rg_ckv.free()
        rg_bprow.free()
        for rg in rg_q5 + rg_k5 + rg_v5 + rg_x5:
            rg.free()
        rg_dq.free()
        rg_den.free()
        rg_dinv.free()
        rg_y0.free()
        rg_ybf.free()
        rg_yT.free()
        rg_pcs.free()
        rg_pkv.free()
        rg_prk.free()
        rg_prkv.free()
        rg_ptp5.free()
        rg_ppr.free()

        rg_hT = A.alloc(64 * KB, "hT")
        hT_v = rg_hT.view(BF16, 32 * 1024).rearrange("p (c t) -> p c t", c=32)
        rg_wd = [A.alloc(16 * KB, f"wd_res{i}") for i in range(2)]
        wd_v2 = [rg.view(BF16, 8 * D).rearrange("p (h m) -> p h m", h=8) for rg in rg_wd]
        rg_wgb = [A.alloc(2 * KB, f"wgb{i}") for i in range(2)]
        wgb = [rg.view(BF16, 8 * 128) for rg in rg_wgb]
        rg_wub = [A.alloc(2 * KB, f"wub{i}") for i in range(2)]
        wub = [rg.view(BF16, 8 * 128) for rg in rg_wub]
        gs6 = sqt[:, 0:1024]
        r_hT_q = [Res(f"hTq{i}") for i in range(4)]
        r_wd2 = [rg.res for rg in rg_wd]
        r_wgb = [rg.res for rg in rg_wgb]
        r_wub = [rg.res for rg in rg_wub]
        r_gs6 = r_sq
        rg_pg6 = [PA.alloc(2048, f"ps_g6{i}") for i in range(2)]
        ps_g6 = [rg.view(F32, 512) for rg in rg_pg6]
        rg_pu6 = [PA.alloc(2048, f"ps_u6{i}") for i in range(2)]
        ps_u6 = [rg.view(F32, 512) for rg in rg_pu6]
        rg_pd6 = [PA.alloc(4096, f"ps_d6{i}") for i in range(2)]
        ps_d6 = [rg.view(F32, D) for rg in rg_pd6]
        r_pg6 = [rg.res for rg in rg_pg6]
        r_pu6 = [rg.res for rg in rg_pu6]
        r_psd6 = [rg.res for rg in rg_pd6]

        dq_queue = []
        ud_ctr = [0]

        def make_unit(half, hg, t8, wpar):
            tt = 8 * half + t8

            def emit_unit():
                ud = ud_ctr[0] % 2
                ud_ctr[0] += 1
                for hl in range(8):
                    hid = 8 * hg + hl
                    for q2 in range(2):
                        p.op("tensor",
                             lambda e, hid=hid, hl=hl, q2=q2, t8=t8, ud=ud, wpar=wpar: e.matmul(
                                 ps_d6[ud][:, 512 * q2:512 * q2 + 512],
                                 hT_v[:, hid, 128 * t8:128 * t8 + 128],
                                 wd_v2[wpar][:, hl, 512 * q2:512 * q2 + 512],
                                 start=(hl == 0), stop=(hl == 7)),
                             reads=[r_hT_q[hg], r_wd2[wpar]], writes=[r_psd6[ud]])
                xt = x1a[:, tt * D:(tt + 1) * D]
                p.op("vector",
                     lambda e, xt=xt, ud=ud: e.tensor_add(xt, xt, ps_d6[ud][:]),
                     reads=[r_psd6[ud], r_x1a], writes=[r_x1a])
            return emit_unit

        for e_ in range(8):
            for half in range(2):
                hoff = 1024 * half
                for hid in range(32):
                    par = hid % 2
                    p.dma("gpsimd", f"lg{par}", wgb[par].rearrange("p (a m) -> p a m", a=8),
                          wg_i[e_, hid].rearrange("(a p) m -> p a m", p=128),
                          writes=[r_wgb[par]])
                    p.dma("gpsimd", f"lu{par}", wub[par].rearrange("p (a m) -> p a m", a=8),
                          wu_i[e_, hid].rearrange("(a p) m -> p a m", p=128),
                          writes=[r_wub[par]])
                    for q2 in range(2):
                        for i in range(8):
                            p.op("tensor",
                                 lambda e, i=i, q2=q2, par=par, hoff=hoff: e.matmul(
                                     ps_g6[q2],
                                     wgb[par].rearrange("p (a m) -> p a m", a=8)[:, i, :],
                                     xn1T_v[:, i, hoff + 512 * q2:hoff + 512 * q2 + 512],
                                     start=(i == 0), stop=(i == 7)),
                                 reads=[r_wgb[par], r_xn1T], writes=[r_pg6[q2]])
                        for i in range(8):
                            p.op("tensor",
                                 lambda e, i=i, q2=q2, par=par, hoff=hoff: e.matmul(
                                     ps_u6[q2],
                                     wub[par].rearrange("p (a m) -> p a m", a=8)[:, i, :],
                                     xn1T_v[:, i, hoff + 512 * q2:hoff + 512 * q2 + 512],
                                     start=(i == 0), stop=(i == 7)),
                                 reads=[r_wub[par], r_xn1T], writes=[r_pu6[q2]])
                        p.op("scalar",
                             lambda e, q2=q2: e.activation(gs6[:, 512 * q2:512 * q2 + 512],
                                                           ps_g6[q2], AF.Silu),
                             reads=[r_pg6[q2]], writes=[r_gs6])
                        p.op("vector",
                             lambda e, hid=hid, q2=q2: e.tensor_mul(
                                 hT_v[:, hid, 512 * q2:512 * q2 + 512],
                                 gs6[:, 512 * q2:512 * q2 + 512], ps_u6[q2]),
                             reads=[r_gs6, r_pu6[q2]], writes=[r_hT_q[hid >> 3]])
                    if hid % 8 == 7:
                        hg = hid >> 3
                        wpar = hg % 2
                        p.dma("gpsimd", "lwd", wd_v2[wpar],
                              wd_i[e_, 8 * hg:8 * hg + 8].rearrange("h p m -> p h m"),
                              writes=[r_wd2[wpar]])
                        for t8 in range(8):
                            dq_queue.append(make_unit(half, hg, t8, wpar))
                    for _ in range(2):
                        if dq_queue:
                            dq_queue.pop(0)()
        while dq_queue:
            dq_queue.pop(0)()
        y_store = p.dma("sync", "stfin", y_o, x1a.rearrange("p (t d) -> p t d", t=NT),
                        reads=[r_x1a])
        p.op("sync", lambda e: e.nop(), extra_deps=[y_store])
        p.build(es)
    return nc


def kernel_fused_maps(inputs_np):
    (x, wq, wk, wv, RT, gamma, beta, w1w, w1s, w2w, w2s, h1w, h1s, h2w, h2s,
     w3w, w3s, Wqkv, bqkv, Wproj, bproj, scale, Wg, Wu, Wd) = inputs_np
    iden = np.eye(128, dtype=np.float32)
    tri = np.triu(np.ones((128, 128), dtype=np.float32))
    com = dict(
        wq_in=wq, wk_in=wk, wv_in=wv, iden_in=iden, b0_i=stage_b0(RT),
        gamma_i=gamma[None], beta_i=beta[None],
        w1t_i=_pretile(w1w.T), w2t_i=_pretile(w2w.T),
        w1s_i=_scale128(w1s), w2s_i=_scale128(w2s),
        h1t_i=_pretile(h1w.T), h2t_i=_pretile(h2w.T),
        h1s_i=_scale128(h1s), h2s_i=_scale128(h2s),
        w3t_i=np.ascontiguousarray(w3w.T), w3s_i=w3s[None],
        wqkvT_i=np.ascontiguousarray(Wqkv.T), bqkv_i=bqkv[None],
        scale_i=scale.reshape(1, 1),
        wpT_i=np.ascontiguousarray(Wproj.T), bp_i=bproj[None], tri_i=tri,
        wg_i=np.ascontiguousarray(Wg.reshape(8, D, 32, 128).transpose(0, 2, 1, 3)),
        wu_i=np.ascontiguousarray(Wu.reshape(8, D, 32, 128).transpose(0, 2, 1, 3)),
        wd_i=np.ascontiguousarray(Wd.reshape(8, 32, 128, D)))
    grT = [stage_grT(RT, pos) for pos in range(4)]
    maps = []
    for c in range(NCORE):
        b, pos = c // 4, c % 4
        mask = np.zeros((4, 1), dtype=np.float32)
        mask[:pos] = 1.0
        maps.append(dict(x_in=np.ascontiguousarray(x[b, 2048 * pos:2048 * (pos + 1)]),
                         grT_i=grT[pos], mask_i=mask, **com))
    return maps


def kernel(L0_attn_x, L0_attn_Wq, L0_attn_Wk, L0_attn_Wv, L0_attn_rel_table,
           L0_attn_gamma, L0_attn_beta, L0_ffn_w1_w, L0_ffn_w1_s, L0_ffn_w2_w,
           L0_ffn_w2_s, L0_ffn_h1_w, L0_ffn_h1_s, L0_ffn_h2_w, L0_ffn_h2_s,
           L0_ffn_w3_w, L0_ffn_w3_s, L1_attn_Wqkv, L1_attn_bqkv, L1_attn_Wproj,
           L1_attn_bproj, L1_attn_scale, L1_ffn_Wg, L1_ffn_Wu, L1_ffn_Wd):
    args = [_f32(a) for a in (
        L0_attn_x, L0_attn_Wq, L0_attn_Wk, L0_attn_Wv, L0_attn_rel_table,
        L0_attn_gamma, L0_attn_beta, L0_ffn_w1_w, L0_ffn_w1_s, L0_ffn_w2_w,
        L0_ffn_w2_s, L0_ffn_h1_w, L0_ffn_h1_s, L0_ffn_h2_w, L0_ffn_h2_s,
        L0_ffn_w3_w, L0_ffn_w3_s, L1_attn_Wqkv, L1_attn_bqkv, L1_attn_Wproj,
        L1_attn_bproj, L1_attn_scale, L1_ffn_Wg, L1_ffn_Wu, L1_ffn_Wd)]
    maps = kernel_fused_maps(args)
    nc = _get("fused", build_fused)
    res = run_bass_kernel_spmd(nc, maps, core_ids=list(range(NCORE)))
    out = np.empty((2, S, D), dtype=np.float32)
    for c in range(NCORE):
        b, pos = c // 4, c % 4
        out[b, 2048 * pos:2048 * (pos + 1)] = _untile(res.results[c]["y_o"])
    return out


# version 14 (final)
# speedup vs baseline: 1.0100x; 1.0100x over previous
import contextlib
import numpy as np
import ml_dtypes
import concourse.bass as bass
import concourse.mybir as mybir
from concourse.bass_utils import run_bass_kernel_spmd

F32 = mybir.dt.float32
BF16 = mybir.dt.bfloat16
AF = mybir.ActivationFunctionType
ALU = mybir.AluOpType
AX = mybir.AxisListType
BF = ml_dtypes.bfloat16

NCORE = 8
TOK = 2048
NT = TOK // 128
S = 8192
D = 1024
NH = 16
HD = 64


ENGS = ("gpsimd", "sync", "tensor", "scalar", "vector")


class Res:
    __slots__ = ("w", "rs", "name")

    def __init__(self, name=""):
        self.w = None
        self.rs = []
        self.name = name


class Op:
    def __init__(self, eng, emit, deps, chan, is_dma):
        self.eng = eng
        self.emit = emit
        self.deps = deps
        self.chan = chan
        self.is_dma = is_dma
        self.gi = -1
        self.ei = -1
        self.marked = False
        self._waits = []


class Chan:
    __slots__ = ("sem", "ops", "name")

    def __init__(self, name):
        self.sem = None
        self.ops = []
        self.name = name


class Prog:
    def __init__(self, nc):
        self.nc = nc
        self.ops = []
        self.chans = {}
        self._es = None

    def chan(self, name):
        if name not in self.chans:
            self.chans[name] = Chan(name)
        return self.chans[name]

    def op(self, eng, emit, reads=(), writes=(), chan=None, extra_deps=()):
        is_dma = chan is not None
        if is_dma:
            chan = self.chan(chan)
        deps = []
        for r in reads:
            if r.w is not None:
                deps.append(r.w)
        for w in writes:
            if w.w is not None:
                deps.append(w.w)
            deps.extend(w.rs)
        deps.extend(extra_deps)
        o = Op(eng, emit, deps, chan, is_dma)
        o.gi = len(self.ops)
        self.ops.append(o)
        if is_dma:
            chan.ops.append(o)
        for r in reads:
            r.rs.append(o)
        for w in writes:
            w.w = o
            w.rs = []
        return o

    def dma(self, eng, chan, out_ap, in_ap, reads=(), writes=(), extra_deps=(), **kw):
        return self.op(
            eng,
            lambda e, _o=out_ap, _i=in_ap, _kw=kw: e.dma_start(_o, _i, **_kw),
            reads=reads, writes=writes, chan=chan, extra_deps=extra_deps)

    def build(self, es):
        nc = self.nc
        eng_sems = {}
        for en in ENGS:
            eng_sems[en] = es.enter_context(nc.semaphore(f"sem_{en}"))
        for ch in self.chans.values():
            ch.sem = es.enter_context(nc.semaphore(f"ch_{ch.name}"))
        eidx = {en: 0 for en in ENGS}
        for o in self.ops:
            o.ei = eidx[o.eng]
            eidx[o.eng] += 1
            for d in o.deps:
                d.marked = True
        inc_cum = {en: [] for en in ENGS}
        cnt = {en: 0 for en in ENGS}
        for o in self.ops:
            if (not o.is_dma) and o.marked:
                cnt[o.eng] += 1
            inc_cum[o.eng].append(cnt[o.eng])
        import bisect
        chan_gis = {id(ch): sorted(co.gi for co in ch.ops) for ch in self.chans.values()}

        def chan_count_before(ch, gi):
            return bisect.bisect_left(chan_gis[id(ch)], gi)

        watermark = {}
        for o in self.ops:
            waits = []
            for d in o.deps:
                if d.is_dma:
                    sem = d.chan.sem
                    val = 16 * chan_count_before(d.chan, o.gi)
                else:
                    if d.eng == o.eng and o.eng in ("tensor", "sync"):
                        continue
                    if d.eng == o.eng and d.gi == o.gi:
                        continue
                    sem = eng_sems[d.eng]
                    val = inc_cum[d.eng][d.ei]
                key = (o.eng, id(sem))
                if watermark.get(key, -1) >= val:
                    continue
                waits.append((sem, val, key))
            best = {}
            for sem, val, key in waits:
                if key not in best or best[key][1] < val:
                    best[key] = (sem, val)
            o._waits = [best[k] for k in best]
            for k, (sem, val) in best.items():
                watermark[k] = max(watermark.get(k, -1), val)
        blk = es.enter_context(nc.Block())
        prog = self

        def run_stream(eng_handle, en):
            for o in prog.ops:
                if o.eng != en:
                    continue
                for sem, val in o._waits:
                    eng_handle.wait_ge(sem, val)
                instr = o.emit(eng_handle)
                if o.is_dma:
                    instr.then_inc(o.chan.sem, 16)
                elif o.marked:
                    instr.then_inc(eng_sems[en], 1)

        blk.gpsimd(lambda e: run_stream(e, "gpsimd"))
        blk.sync(lambda e: run_stream(e, "sync"))
        blk.tensor(lambda e: run_stream(e, "tensor"))
        blk.scalar(lambda e: run_stream(e, "scalar"))
        blk.vector(lambda e: run_stream(e, "vector"))


class Arena:

    def __init__(self, tensor, nbytes):
        self.t = tensor
        self.nbytes = nbytes
        self.free_list = [(0, nbytes)]
        self.history = []

    def alloc(self, nbytes, name="", side="bottom"):
        nbytes = (nbytes + 63) & ~63
        idxs = range(len(self.free_list))
        if side == "top":
            idxs = reversed(list(idxs))
        for i in idxs:
            (s, e) = self.free_list[i]
            if e - s >= nbytes:
                if side == "top":
                    s = e - nbytes
                    self.free_list[i] = (self.free_list[i][0], s)
                else:
                    self.free_list[i] = (s + nbytes, e)
                if self.free_list[i][0] == self.free_list[i][1]:
                    self.free_list.pop(i)
                res = Res(name)
                prime = []
                for (hs, he, hres) in self.history:
                    if hs < s + nbytes and he > s:
                        if hres.w is not None:
                            prime.append(hres.w)
                        prime.extend(hres.rs)
                res.rs = prime
                self.history.append((s, s + nbytes, res))
                return _Region(self, s, nbytes, res)
        raise MemoryError(f"arena: cannot alloc {nbytes} B for {name}")

    def free(self, region):
        s, e = region.start, region.start + region.size
        self.free_list.append((s, e))
        self.free_list.sort()
        merged = []
        for iv in self.free_list:
            if merged and merged[-1][1] >= iv[0]:
                merged[-1] = (merged[-1][0], max(merged[-1][1], iv[1]))
            else:
                merged.append(list(iv))
        self.free_list = [tuple(m) for m in merged]


class _Region:
    def __init__(self, arena, start, size, res):
        self.arena = arena
        self.start = start
        self.size = size
        self.res = res

    def view(self, dtype, cols, offset_bytes=0, parts=128):
        isz = {F32: 4, BF16: 2}[dtype]
        off = (self.start + offset_bytes) // isz
        ap = self.arena.t[:].bitcast(dtype)[0:parts, off:off + cols]
        return ap

    def free(self):
        self.arena.free(self)



def ln_tile(p, x_ap, out_ap, r_x, r_out, tmp, r_tmp, eps):
    sq, c0, c1, c2 = tmp["sq"], tmp["c0"], tmp["c1"], tmp["c2"]
    r_sq, r_c0, r_c1, r_c2 = r_tmp["sq"], r_tmp["c0"], r_tmp["c1"], r_tmp["c2"]
    n = x_ap.shape[-1]
    p.op("vector", lambda e: e.reduce_sum(c0[:], x_ap, axis=AX.X),
         reads=[r_x], writes=[r_c0])
    p.op("vector", lambda e: e.tensor_scalar_mul(c0[:], c0[:], -1.0 / n),
         reads=[r_c0], writes=[r_c0])
    p.op("scalar", lambda e: e.activation(sq[:], x_ap, AF.Square, bias=c0[:]),
         reads=[r_x, r_c0], writes=[r_sq])
    p.op("vector", lambda e: e.reduce_sum(c1[:], sq[:], axis=AX.X),
         reads=[r_sq], writes=[r_c1])
    p.op("vector", lambda e: e.tensor_scalar(c1[:], c1[:], 1.0 / n, eps, ALU.mult, ALU.add),
         reads=[r_c1], writes=[r_c1])
    p.op("scalar", lambda e: e.activation(c1[:], c1[:], AF.Sqrt),
         reads=[r_c1], writes=[r_c1])
    p.op("vector", lambda e: e.reciprocal(c2[:], c1[:]),
         reads=[r_c1], writes=[r_c2])
    p.op("vector", lambda e: e.tensor_scalar(out_ap, x_ap, c0[:], c2[:], ALU.add, ALU.mult),
         reads=[r_x, r_c0, r_c2], writes=[r_out])



def build_pre0():
    nc = bass.Bass("TRN2", target_bir_lowering=False)
    x_in = nc.dram_tensor("x_in", [TOK, D], F32, kind="ExternalInput").ap()
    wq_in = nc.dram_tensor("wq_in", [D, D], F32, kind="ExternalInput").ap()
    wk_in = nc.dram_tensor("wk_in", [D, D], F32, kind="ExternalInput").ap()
    wv_in = nc.dram_tensor("wv_in", [D, D], F32, kind="ExternalInput").ap()
    iden_in = nc.dram_tensor("iden_in", [128, 128], F32, kind="ExternalInput").ap()
    xnT_o = nc.dram_tensor("xnT_o", [128, 8, TOK], BF16, kind="ExternalOutput").ap()
    qT_o = nc.dram_tensor("qT_o", [128, 8, TOK], BF16, kind="ExternalOutput").ap()
    kT_o = nc.dram_tensor("kT_o", [128, 8, TOK], BF16, kind="ExternalOutput").ap()
    v_o = nc.dram_tensor("v_o", [128, NT, D], BF16, kind="ExternalOutput").ap()

    with contextlib.ExitStack() as es:
        sb = lambda name, shape, dt: es.enter_context(nc.sbuf_tensor(name, shape, dt))
        ps = lambda name, shape, dt: es.enter_context(nc.psum_tensor(name, shape, dt))
        xs = sb("xs", [128, NT * D], F32)
        xnT = sb("xnT", [128, 8 * TOK], BF16)
        wq = sb("wq", [128, 8 * D], BF16)
        wk = sb("wk", [128, 8 * D], BF16)
        wv = sb("wv", [128, 8 * D], BF16)
        idb = sb("idb", [128, 128], BF16)
        c0 = sb("c0", [128, 1], F32)
        c1 = sb("c1", [128, 1], F32)
        c2 = sb("c2", [128, 1], F32)
        sq = sb("sq", [128, D], F32)
        xn_t = sb("xn_t", [128, D], BF16)
        ev = [sb(f"ev{i}", [128, 512], BF16) for i in range(4)]
        pst = ps("pst", [128, D], BF16)
        pmm = [ps(f"pmm{i}", [128, 512], F32) for i in range(4)]

        p = Prog(nc)
        r_xs = Res("xs")
        r_xnT = Res("xnT")
        r_w = {0: Res("wq"), 1: Res("wk"), 2: Res("wv")}
        r_id = Res("id")
        r_tmp = {"sq": Res(), "c0": Res(), "c1": Res(), "c2": Res()}
        tmp = {"sq": sq, "c0": c0, "c1": c1, "c2": c2}
        r_xn_t = Res("xn_t")
        r_ev = [Res(f"ev{i}") for i in range(4)]
        r_pst = Res("pst")
        r_pmm = [Res(f"pmm{i}") for i in range(4)]

        p.dma("sync", "ldx", xs[:].rearrange("p (t d) -> p t d", t=NT),
              x_in.rearrange("(t p) d -> p t d", p=128), writes=[r_xs])
        for wsb, win, rw in ((wq, wq_in, r_w[0]), (wk, wk_in, r_w[1]), (wv, wv_in, r_w[2])):
            p.dma("gpsimd", "ldw", wsb[:].rearrange("p (a o) -> p a o", a=8),
                  win.rearrange("(a p) o -> p a o", p=128), writes=[rw])
        p.dma("gpsimd", "ldw", idb[:], iden_in, writes=[r_id])

        for tt in range(NT):
            x_ap = xs[:, tt * D:(tt + 1) * D]
            ln_tile(p, x_ap, xn_t[:], r_xs, r_xn_t, tmp, r_tmp, 1e-5)
            for c in range(8):
                p.op("tensor",
                     lambda e, c=c: e.transpose(pst[:, c * 128:(c + 1) * 128],
                                                xn_t[:, c * 128:(c + 1) * 128], idb[:]),
                     reads=[r_xn_t, r_id], writes=[r_pst])
            dst = xnT[:].rearrange("p (c t) -> p c t", c=8)[:, :, tt * 128:(tt + 1) * 128]
            src = pst[:].rearrange("p (c t) -> p c t", c=8)
            p.op("vector", lambda e, dst=dst, src=src: e.tensor_copy(dst, src),
                 reads=[r_pst], writes=[r_xnT])

        p.dma("sync", "st", xnT_o[:, :, :], xnT[:].rearrange("p (c t) -> p c t", c=8),
              reads=[r_xnT])
        for wi, (wsb, out_dram) in enumerate(((wq, qT_o), (wk, kT_o))):
            for ft in range(8):
                for tc in range(4):
                    bank = r_pmm[(ft * 4 + tc) % 4]
                    pb = pmm[(ft * 4 + tc) % 4]
                    for i in range(8):
                        p.op("tensor",
                             lambda e, i=i, ft=ft, tc=tc, pb=pb, wsb=wsb: e.matmul(
                                 pb[:],
                                 wsb[:, i * D + ft * 128: i * D + ft * 128 + 128],
                                 xnT[:].rearrange("p (c t) -> p c t", c=8)[:, i, tc * 512:(tc + 1) * 512],
                                 start=(i == 0), stop=(i == 7)),
                             reads=[r_w[wi], r_xnT], writes=[bank])
                    eb = r_ev[(ft * 4 + tc) % 4]
                    esb = ev[(ft * 4 + tc) % 4]
                    p.op("scalar", lambda e, pb=pb, esb=esb: e.activation(esb[:], pb[:], AF.Copy),
                         reads=[bank], writes=[eb])
                    p.dma("sync", "st", out_dram[:, ft, tc * 512:(tc + 1) * 512], esb[:],
                          reads=[eb])
        for tt in range(NT):
            for oc in range(2):
                k = (tt * 2 + oc) % 4
                pb, bank, esb, eb = pmm[k], r_pmm[k], ev[k], r_ev[k]
                for i in range(8):
                    p.op("tensor",
                         lambda e, i=i, tt=tt, oc=oc, pb=pb: e.matmul(
                             pb[:],
                             xnT[:].rearrange("p (c t) -> p c t", c=8)[:, i, tt * 128:(tt + 1) * 128],
                             wv[:, i * D + oc * 512: i * D + oc * 512 + 512],
                             start=(i == 0), stop=(i == 7)),
                         reads=[r_w[2], r_xnT], writes=[bank])
                p.op("scalar", lambda e, pb=pb, esb=esb: e.activation(esb[:], pb[:], AF.Copy),
                     reads=[bank], writes=[eb])
                p.dma("sync", "st", v_o[:, tt, oc * 512:(oc + 1) * 512], esb[:],
                      reads=[eb])
        st_ops = list(p.chan("st").ops)
        p.op("sync", lambda e: e.nop(), extra_deps=st_ops)
        p.build(es)
    return nc



PAIRS = [(a, l) for a in range(16) for l in range(16) if a - l >= 12]


def build_attn0():
    nc = bass.Bass("TRN2", target_bir_lowering=False)
    qT_i = nc.dram_tensor("qT_i", [128, 8, TOK], BF16, kind="ExternalInput").ap()
    kT_i = nc.dram_tensor("kT_i", [128, 8, S], BF16, kind="ExternalInput").ap()
    vaug_i = nc.dram_tensor("vaug_i", [128, 4, 64, 260], BF16, kind="ExternalInput").ap()
    grT_i = nc.dram_tensor("grT_i", [len(PAIRS), S, 128], F32, kind="ExternalInput").ap()
    b0_i = nc.dram_tensor("b0_i", [128, 1], F32, kind="ExternalInput").ap()
    attnT_o = nc.dram_tensor("attnT_o", [64, NH, TOK], F32, kind="ExternalOutput").ap()
    u_o = nc.dram_tensor("u_o", [1, D], F32, kind="ExternalOutput").ap()

    SCALE = 1.0 / 32.0

    with contextlib.ExitStack() as es:
        sb = lambda name, shape, dt: es.enter_context(nc.sbuf_tensor(name, shape, dt))
        qT = sb("qT", [128, 8 * TOK], BF16)
        kTp = sb("kTp", [128, 2 * S], BF16)
        vaug = sb("vaug", [128, 64 * 260], BF16)
        aT_sl = sb("aT_sl", [64, 4 * TOK], F32)
        pT = [sb(f"pT{i}", [128, 2048], BF16) for i in range(2)]
        b0c = sb("b0c", [128, 1], F32)
        b0cb = sb("b0cb", [128, 1], BF16)
        u_out = sb("u_out", [1, D], F32)
        linv = sb("linv", [1, 512], F32)
        pv_sb = sb("pv_sb", [64, 512], F32)
        grT_sb = sb("grT_sb", [128, 64 * 128], F32)
        dT = sb("dT", [128, 64 * 128], BF16)
        onesf = sb("onesf", [1, 64], F32)
        ps_s = es.enter_context(nc.psum_tensor("ps_s", [128, 2048], F32))
        ps_o = es.enter_context(nc.psum_tensor("ps_o", [128, 2048], F32))

        p = Prog(nc)
        r_qT, r_kTp, r_vaug, r_aT = Res("qT"), Res("kTp"), Res("vaug"), Res("aT")
        r_pT = [Res("pT0"), Res("pT1")]
        r_b0c, r_b0cb, r_uout = Res(), Res(), Res()
        r_linv, r_pvsb, r_grT, r_dT, r_onesf = Res(), Res(), Res(), Res(), Res()
        r_ps_s, r_ps_o = Res("ps_s"), Res("ps_o")

        kTp_v = kTp[:].rearrange("p (c t) -> p c t", c=2)
        qT_v = qT[:].rearrange("p (c t) -> p c t", c=8)
        vaug_v = vaug[:].rearrange("p (k c) -> p k c", k=64)
        aT_v = aT_sl[:].rearrange("p (c t) -> p c t", c=4)
        dT_v = dT[:].rearrange("p (k m) -> p k m", k=64)

        p.dma("sync", "ldq", qT[:].rearrange("p (c t) -> p c t", c=8), qT_i,
              writes=[r_qT])
        p.dma("sync", "ldq", b0c[:], b0_i, writes=[r_b0c])
        p.dma("gpsimd", "ldb", b0cb[:], b0_i, writes=[r_b0cb])
        p.op("vector", lambda e: e.memset(onesf[:], 1.0), writes=[r_onesf])

        for hp in range(4):
            p.dma("sync", "ldk", kTp_v, kT_i[:, 2 * hp:2 * hp + 2, :], writes=[r_kTp])
            p.dma("sync", "ldv", vaug_v, vaug_i[:, hp], writes=[r_vaug])
            for kc in range(64):
                p.op("tensor",
                     lambda e, kc=kc: e.matmul(ps_s[0:1, 0:260], b0cb[:],
                                               vaug_v[:, kc, :],
                                               start=(kc == 0), stop=(kc == 63)),
                     reads=[r_b0cb, r_vaug], writes=[r_ps_s])
            dst = u_out[0:1, 256 * hp:256 * hp + 256].rearrange("a (h c) -> a h c", h=4)
            src = ps_s[0:1, 0:260].rearrange("a (h c) -> a h c", h=4)[:, :, 0:64]
            p.op("vector", lambda e, dst=dst, src=src: e.tensor_copy(dst, src),
                 reads=[r_ps_s], writes=[r_uout])

            for qm in range(4):
                for kc in range(64):
                    for hh in range(4):
                        lo = 64 * (hh & 1)
                        p.op("tensor",
                             lambda e, kc=kc, hh=hh, lo=lo, hp=hp, qm=qm: e.matmul(
                                 ps_s[:, 512 * hh:512 * hh + 512],
                                 kTp_v[lo:lo + 64, hh >> 1, 128 * kc:128 * kc + 128],
                                 qT_v[lo:lo + 64, 2 * hp + (hh >> 1), 512 * qm:512 * qm + 512],
                                 start=True, stop=True),
                             reads=[r_kTp, r_qT], writes=[r_ps_s])
                    pb = kc % 2
                    p.op("scalar",
                         lambda e, pb=pb: e.activation(pT[pb][:], ps_s[:], AF.Exp, scale=SCALE),
                         reads=[r_ps_s], writes=[r_pT[pb]])
                    for hh in range(4):
                        p.op("tensor",
                             lambda e, kc=kc, hh=hh, pb=pb: e.matmul(
                                 ps_o[0:65, 512 * hh:512 * hh + 512],
                                 vaug_v[:, kc, 65 * hh:65 * hh + 65],
                                 pT[pb][:, 512 * hh:512 * hh + 512],
                                 start=(kc == 0), stop=(kc == 63)),
                             reads=[r_vaug, r_pT[pb]], writes=[r_ps_o])
                for hh in range(4):
                    p.op("vector",
                         lambda e, hh=hh: e.reciprocal(linv[:], ps_o[64:65, 512 * hh:512 * hh + 512]),
                         reads=[r_ps_o], writes=[r_linv])
                    p.op("scalar",
                         lambda e, hh=hh: e.activation(pv_sb[:], ps_o[0:64, 512 * hh:512 * hh + 512], AF.Copy),
                         reads=[r_ps_o], writes=[r_pvsb])
                    p.op("tensor",
                         lambda e: e.matmul(ps_s[0:64, 0:512], onesf[:], linv[:],
                                            start=True, stop=True),
                         reads=[r_onesf, r_linv], writes=[r_ps_s])
                    dst = aT_v[:, hh, 512 * qm:512 * qm + 512]
                    p.op("vector",
                         lambda e, dst=dst: e.tensor_mul(dst, pv_sb[:], ps_s[0:64, 0:512]),
                         reads=[r_pvsb, r_ps_s], writes=[r_aT])
            if hp < 3:
                p.dma("sync", "sta", attnT_o[:, 4 * hp:4 * hp + 4, :], aT_v,
                      reads=[r_aT])

        for j, (a, l) in enumerate(PAIRS):
            p.dma("sync", "ldg", grT_sb[:].rearrange("p (c m) -> p c m", c=64),
                  grT_i[j].rearrange("(c p) m -> p c m", p=128), writes=[r_grT])
            p.op("vector",
                 lambda e: e.tensor_scalar(dT[:], grT_sb[:], b0c[:], None, ALU.subtract),
                 reads=[r_grT, r_b0c], writes=[r_dT])
            hh = a - 12
            for kc in range(64):
                p.op("tensor",
                     lambda e, kc=kc, hh=hh: e.matmul(
                         ps_o[0:64, 0:128],
                         vaug_v[:, kc, 65 * hh:65 * hh + 64],
                         dT_v[:, kc, :],
                         start=(kc == 0), stop=(kc == 63)),
                     reads=[r_vaug, r_dT], writes=[r_ps_o])
            dst = aT_v[:, a - 12, :].rearrange("p (t0 t1) -> p t0 t1", t1=16)[:, :, l]
            p.op("vector",
                 lambda e, dst=dst: e.tensor_tensor(dst, dst, ps_o[0:64, 0:128], ALU.add),
                 reads=[r_ps_o, r_aT], writes=[r_aT])
        p.dma("sync", "sta", attnT_o[:, 12:16, :], aT_v, reads=[r_aT])
        p.dma("sync", "sta", u_o, u_out[:], reads=[r_uout])
        st_ops = list(p.chan("sta").ops)
        p.op("sync", lambda e: e.nop(), extra_deps=st_ops)
        p.build(es)
    return nc



def build_ffn0():
    nc = bass.Bass("TRN2", target_bir_lowering=False)
    DH = 4096
    x_in = nc.dram_tensor("x_in", [TOK, D], F32, kind="ExternalInput").ap()
    attnT_i = nc.dram_tensor("attnT_i", [64, NH, TOK], F32, kind="ExternalInput").ap()
    u_i = nc.dram_tensor("u_i", [1, D], F32, kind="ExternalInput").ap()
    gamma_i = nc.dram_tensor("gamma_i", [1, D], F32, kind="ExternalInput").ap()
    beta_i = nc.dram_tensor("beta_i", [1, D], F32, kind="ExternalInput").ap()
    xnT_i = nc.dram_tensor("xnT_i", [128, 8, TOK], BF16, kind="ExternalInput").ap()
    w1t_i = nc.dram_tensor("w1t_i", [32, D, 128], F32, kind="ExternalInput").ap()
    w2t_i = nc.dram_tensor("w2t_i", [32, D, 128], F32, kind="ExternalInput").ap()
    w1s_i = nc.dram_tensor("w1s_i", [128, 32], F32, kind="ExternalInput").ap()
    w2s_i = nc.dram_tensor("w2s_i", [128, 32], F32, kind="ExternalInput").ap()
    h1t_i = nc.dram_tensor("h1t_i", [32, DH, 128], F32, kind="ExternalInput").ap()
    h2t_i = nc.dram_tensor("h2t_i", [32, DH, 128], F32, kind="ExternalInput").ap()
    h1s_i = nc.dram_tensor("h1s_i", [128, 32], F32, kind="ExternalInput").ap()
    h2s_i = nc.dram_tensor("h2s_i", [128, 32], F32, kind="ExternalInput").ap()
    w3t_i = nc.dram_tensor("w3t_i", [DH, D], F32, kind="ExternalInput").ap()
    w3s_i = nc.dram_tensor("w3s_i", [1, D], F32, kind="ExternalInput").ap()
    iden_i = nc.dram_tensor("iden_i", [128, 128], F32, kind="ExternalInput").ap()
    x1_o = nc.dram_tensor("x1_o", [128, NT, D], F32, kind="ExternalOutput").ap()
    bt_d = nc.dram_tensor("bt_d", [32, 128, TOK], BF16, kind="Internal").ap()
    ct_d = nc.dram_tensor("ct_d", [32, 128, TOK], BF16, kind="Internal").ap()
    x1tmp = nc.dram_tensor("x1tmp", [128, NT * D], F32, kind="Internal").ap()

    with contextlib.ExitStack() as es:
        sb = lambda name, shape, dt: es.enter_context(nc.sbuf_tensor(name, shape, dt))
        big = sb("big", [128, 32 * TOK], BF16)
        xs_v = big[:].bitcast(F32)[:, 0:NT * D]
        aT_v = big[:].rearrange("p (c t) -> p c t", c=32)
        w3t_v = big[:, 0:32 * D].rearrange("p (c m) -> p c m", c=32)
        xnT = sb("xnT", [128, 8 * TOK], BF16)
        xnT_v = xnT[:].rearrange("p (c t) -> p c t", c=8)
        x1b = [xnT[:].bitcast(F32)[:, i * D:(i + 1) * D] for i in range(2)]
        abuf = sb("abuf", [128, D], F32)
        t2 = sb("t2", [128, D], F32)
        rowsA = sb("rowsA", [65, D], F32)
        rowsB = sb("rowsB", [1, D], F32)
        u_sb, g_sb, be_sb = rowsA[0:1, :], rowsA[32:33, :], rowsA[64:65, :]
        w3s_sb = rowsB[0:1, :]
        u_bc = sb("u_bc", [128, D], F32)
        g_bc = sb("g_bc", [128, D], F32)
        b_bc = sb("b_bc", [128, D], F32)
        rg_w3sbc = A.alloc(4 * KB, "w3s_bc")
        w3s_bc = rg_w3sbc.view(F32, D)
        onesf_t = sb("onesf", [65, 128], F32)
        onesf = onesf_t[0:1, :]
        idf = sb("idf", [128, 128], F32)
        c0 = sb("c0", [128, 1], F32)
        c1 = sb("c1", [128, 1], F32)
        c2 = sb("c2", [128, 1], F32)
        hbuf = [sb(f"hbuf{i}", [128, 32 * 128], BF16) for i in range(2)]
        w1b = [h[:, 0:1024] for h in hbuf]
        w2b = [h[:, 1024:2048] for h in hbuf]
        gsb = [abuf[:, 0:512], abuf[:, 512:1024]]
        asl = [h[:].bitcast(F32)[0:64, 0:NH * 128] for h in hbuf]
        ctb = hbuf
        bst = [t2[:].bitcast(BF16)[:, 0:512], t2[:].bitcast(BF16)[:, 512:1024]]
        w1s = sb("w1s", [128, 32], F32)
        w2s = sb("w2s", [128, 32], F32)
        h1s = sb("h1s", [128, 32], F32)
        h2s = sb("h2s", [128, 32], F32)
        ps_tr = es.enter_context(nc.psum_tensor("ps_tr", [128, D], F32))
        ps_bc = es.enter_context(nc.psum_tensor("ps_bc", [128, D], F32))
        ps_g = [es.enter_context(nc.psum_tensor(f"ps_g{i}", [128, 512], F32)) for i in range(2)]
        ps_u = [es.enter_context(nc.psum_tensor(f"ps_u{i}", [128, 512], F32)) for i in range(2)]

        p = Prog(nc)
        R = lambda n="": Res(n)
        r_big, r_xnT, r_abuf, r_t2 = R("big"), R("xnT"), R(), R()
        r_rows = R("rows")
        r_usb = r_gsb_ = r_besb = r_w3ssb = r_rows
        r_ubc, r_gbc, r_bbc = R(), R(), R()
        r_w3sbc = rg_w3sbc.res
        r_onesf, r_idf = R(), R()
        r_tmp = {"sq": r_t2, "c0": R(), "c1": R(), "c2": R()}
        tmp = {"sq": t2, "c0": c0, "c1": c1, "c2": c2}
        r_hbuf = [R(), R()]
        r_w1b = r_hbuf
        r_w2b = r_hbuf
        r_gs = [r_abuf, r_abuf]
        r_bst = [r_t2, r_t2]
        r_ctb = r_hbuf
        r_asl = r_hbuf
        r_x1b = [r_xnT, r_xnT]
        r_scales = {"w1": R(), "w2": R(), "h1": R(), "h2": R()}
        r_pstr, r_psbc = R("pstr"), R("psbc")
        r_psg = [R(), R()]
        r_psu = [R(), R()]

        p.dma("sync", "ld0", xs_v.rearrange("p (t d) -> p t d", t=NT),
              x_in.rearrange("(t p) d -> p t d", p=128), writes=[r_big])
        p.dma("sync", "ld0", xnT_v, xnT_i, writes=[r_xnT])
        p.dma("sync", "ld0", u_sb[:], u_i, writes=[r_usb])
        p.dma("sync", "ld0", g_sb[:], gamma_i, writes=[r_gsb_])
        p.dma("sync", "ld0", be_sb[:], beta_i, writes=[r_besb])
        p.dma("sync", "ld0", w3s_sb[:], w3s_i, writes=[r_w3ssb])
        p.dma("sync", "ld0", idf[:], iden_i, writes=[r_idf])
        for nm, ssb, sin in (("w1", w1s, w1s_i), ("w2", w2s, w2s_i),
                             ("h1", h1s, h1s_i), ("h2", h2s, h2s_i)):
            p.dma("sync", "ld0", ssb[:], sin, writes=[r_scales[nm]])
        p.op("vector", lambda e: e.memset(onesf_t[:], 1.0), writes=[r_onesf])

        def build_bc(src_sb, base, r_src, dst, r_dst):
            for half in range(2):
                p.op("tensor",
                     lambda e, src_sb=src_sb, half=half, base=base: e.matmul(
                         ps_bc[:, 512 * half:512 * half + 512],
                         onesf_t[base:base + 1, :],
                         src_sb[0:1, 512 * half:512 * half + 512], start=True, stop=True),
                     reads=[r_onesf, r_src], writes=[r_psbc])
            p.op("vector", lambda e, dst=dst: e.tensor_copy(dst[:], ps_bc[:]),
                 reads=[r_psbc], writes=[r_dst])

        build_bc(u_sb, 0, r_usb, u_bc, r_ubc)
        build_bc(g_sb, 32, r_gsb_, g_bc, r_gbc)
        build_bc(be_sb, 64, r_besb, b_bc, r_bbc)

        for tt in range(NT):
            par = tt % 2
            p.dma("sync", f"lda{par}",
                  asl[par].rearrange("p (h t) -> p h t", h=NH),
                  attnT_i[:, :, 128 * tt:128 * tt + 128], writes=[r_asl[par]])
            pst = ps_tr if par == 0 else ps_bc
            r_pst = r_pstr if par == 0 else r_psbc
            for h in range(NH):
                p.op("tensor",
                     lambda e, h=h, par=par, pst=pst: e.transpose(
                         pst[:, 64 * h:64 * h + 64],
                         asl[par].rearrange("p (h t) -> p h t", h=NH)[:, h, :],
                         idf[0:64, 0:64]),
                     reads=[r_asl[par], r_idf], writes=[r_pst])
            p.op("vector", lambda e, pst=pst: e.tensor_add(abuf[:], pst[:], u_bc[:]),
                 reads=[r_pst, r_ubc], writes=[r_abuf])
            ln_tile(p, abuf[:], t2[:], r_abuf, r_t2, tmp, r_tmp, 1e-3)
            p.op("vector", lambda e: e.tensor_mul(t2[:], t2[:], g_bc[:]),
                 reads=[r_t2, r_gbc], writes=[r_t2])
            p.op("vector", lambda e: e.tensor_add(t2[:], t2[:], b_bc[:]),
                 reads=[r_t2, r_bbc], writes=[r_t2])
            xt = xs_v[:, tt * D:(tt + 1) * D]
            p.op("vector", lambda e, xt=xt: e.tensor_add(xt, xt, t2[:]),
                 reads=[r_t2, r_big], writes=[r_big])
        stx_op = p.dma("sync", "stx", x1tmp, xs_v, reads=[r_big])

        for ot in range(32):
            par = ot % 2
            p.dma("gpsimd", f"lw1{par}", w1b[par].rearrange("p (a m) -> p a m", a=8),
                  w1t_i[ot].rearrange("(a p) m -> p a m", p=128), writes=[r_w1b[par]])
            p.dma("gpsimd", f"lw2{par}", w2b[par].rearrange("p (a m) -> p a m", a=8),
                  w2t_i[ot].rearrange("(a p) m -> p a m", p=128), writes=[r_w2b[par]])
            for tch in range(4):
                qar = tch % 2
                for i in range(8):
                    p.op("tensor",
                         lambda e, i=i, par=par, qar=qar, tch=tch: e.matmul(
                             ps_g[qar][:],
                             w1b[par].rearrange("p (a m) -> p a m", a=8)[:, i, :],
                             xnT_v[:, i, 512 * tch:512 * tch + 512],
                             start=(i == 0), stop=(i == 7)),
                         reads=[r_w1b[par], r_xnT], writes=[r_psg[qar]])
                for i in range(8):
                    p.op("tensor",
                         lambda e, i=i, par=par, qar=qar, tch=tch: e.matmul(
                             ps_u[qar][:],
                             w2b[par].rearrange("p (a m) -> p a m", a=8)[:, i, :],
                             xnT_v[:, i, 512 * tch:512 * tch + 512],
                             start=(i == 0), stop=(i == 7)),
                         reads=[r_w2b[par], r_xnT], writes=[r_psu[qar]])
                p.op("scalar",
                     lambda e, qar=qar, ot=ot: e.activation(
                         gsb[qar], ps_g[qar][:], AF.Silu, scale=w1s[:, ot:ot + 1]),
                     reads=[r_psg[qar], r_scales["w1"]], writes=[r_gs[qar]])
                dst = aT_v[:, ot, 512 * tch:512 * tch + 512]
                p.op("vector",
                     lambda e, qar=qar, ot=ot, dst=dst: e.scalar_tensor_tensor(
                         dst, ps_u[qar][:], w2s[:, ot:ot + 1], gsb[qar],
                         ALU.mult, ALU.mult),
                     reads=[r_psu[qar], r_scales["w2"], r_gs[qar]], writes=[r_big])

        build_bc(w3s_sb, 0, r_w3ssb, w3s_bc, r_w3sbc)
        st2_stores = []
        for o2 in range(32):
            par = o2 % 2
            p.dma("gpsimd", f"lh{par}", hbuf[par][:].rearrange("p (c m) -> p c m", c=32),
                  h1t_i[o2].rearrange("(c p) m -> p c m", p=128), writes=[r_hbuf[par]])
            for tch in range(4):
                qar = tch % 2
                for c in range(32):
                    p.op("tensor",
                         lambda e, c=c, par=par, qar=qar, tch=tch: e.matmul(
                             ps_g[qar][:],
                             hbuf[par][:].rearrange("p (c m) -> p c m", c=32)[:, c, :],
                             aT_v[:, c, 512 * tch:512 * tch + 512],
                             start=(c == 0), stop=(c == 31)),
                         reads=[r_hbuf[par], r_big], writes=[r_psg[qar]])
                p.op("scalar",
                     lambda e, qar=qar, o2=o2: e.activation(
                         bst[qar], ps_g[qar][:], AF.Gelu, scale=h1s[:, o2:o2 + 1]),
                     reads=[r_psg[qar], r_scales["h1"]], writes=[r_bst[qar]])
                st2_stores.append(
                    p.dma("sync", "stb", bt_d[o2, :, 512 * tch:512 * tch + 512],
                          bst[qar], reads=[r_bst[qar]]))

        p.dma("sync", "ldb2", big[:].rearrange("p (c t) -> p c t", c=32),
              bt_d.rearrange("c p t -> p c t"), writes=[r_big],
              extra_deps=st2_stores)
        st3_stores = []
        for o3 in range(32):
            par = o3 % 2
            p.dma("gpsimd", f"lh{par}", hbuf[par][:].rearrange("p (c m) -> p c m", c=32),
                  h2t_i[o3].rearrange("(c p) m -> p c m", p=128), writes=[r_hbuf[par]])
            for tch in range(4):
                qar = tch % 2
                for c in range(32):
                    p.op("tensor",
                         lambda e, c=c, par=par, qar=qar, tch=tch: e.matmul(
                             ps_g[qar][:],
                             hbuf[par][:].rearrange("p (c m) -> p c m", c=32)[:, c, :],
                             aT_v[:, c, 512 * tch:512 * tch + 512],
                             start=(c == 0), stop=(c == 31)),
                         reads=[r_hbuf[par], r_big], writes=[r_psg[qar]])
                p.op("scalar",
                     lambda e, qar=qar, o3=o3: e.activation(
                         bst[qar], ps_g[qar][:], AF.Gelu, scale=h2s[:, o3:o3 + 1]),
                     reads=[r_psg[qar], r_scales["h2"]], writes=[r_bst[qar]])
                st3_stores.append(
                    p.dma("sync", "stb", ct_d[o3, :, 512 * tch:512 * tch + 512],
                          bst[qar], reads=[r_bst[qar]]))

        p.dma("gpsimd", "lw3", w3t_v, w3t_i.rearrange("(c p) m -> p c m", p=128),
              writes=[r_big])
        for tt in range(NT):
            par = tt % 2
            p.dma("sync", f"lc{par}", ctb[par][:].rearrange("p (c m) -> p c m", c=32),
                  ct_d[:, :, 128 * tt:128 * tt + 128].rearrange("c p m -> p c m"),
                  writes=[r_ctb[par]], extra_deps=st3_stores)
            p.dma("sync", f"lx{par}", x1b[par], x1tmp[:, tt * D:(tt + 1) * D],
                  writes=[r_x1b[par]], extra_deps=[stx_op])
            pst = ps_tr if par == 0 else ps_bc
            r_pst = r_pstr if par == 0 else r_psbc
            for c in range(32):
                for half in range(2):
                    p.op("tensor",
                         lambda e, c=c, par=par, half=half, pst=pst: e.matmul(
                             pst[:, 512 * half:512 * half + 512],
                             ctb[par][:].rearrange("p (c m) -> p c m", c=32)[:, c, :],
                             w3t_v[:, c, 512 * half:512 * half + 512],
                             start=(c == 0), stop=(c == 31)),
                         reads=[r_ctb[par], r_big], writes=[r_pst])
            p.op("vector", lambda e, pst=pst: e.tensor_mul(abuf[:], pst[:], w3s_bc[:]),
                 reads=[r_pst, r_w3sbc], writes=[r_abuf])
            p.op("vector", lambda e, par=par: e.tensor_add(abuf[:], abuf[:], x1b[par]),
                 reads=[r_abuf, r_x1b[par]], writes=[r_abuf])
            p.dma("sync", "sty", x1_o[:, tt, :], abuf[:], reads=[r_abuf])
        st_ops = list(p.chan("sty").ops)
        p.op("sync", lambda e: e.nop(), extra_deps=st_ops)
        p.build(es)
    return nc



def build_pre1():
    nc = bass.Bass("TRN2", target_bir_lowering=False)
    x1_i = nc.dram_tensor("x1_i", [128, NT, D], F32, kind="ExternalInput").ap()
    wqkvT_i = nc.dram_tensor("wqkvT_i", [D, 3 * D], F32, kind="ExternalInput").ap()
    bqkv_i = nc.dram_tensor("bqkv_i", [1, 3 * D], F32, kind="ExternalInput").ap()
    scale_i = nc.dram_tensor("scale_i", [1, 1], F32, kind="ExternalInput").ap()
    iden_i = nc.dram_tensor("iden_i", [128, 128], F32, kind="ExternalInput").ap()
    xn1T_o = nc.dram_tensor("xn1T_o", [128, 8, TOK], BF16, kind="ExternalOutput").ap()
    q1_o = nc.dram_tensor("q1_o", [128, NT, D], F32, kind="ExternalOutput").ap()
    k1_o = nc.dram_tensor("k1_o", [128, NT, D], BF16, kind="ExternalOutput").ap()
    kv1_o = nc.dram_tensor("kv1_o", [128, NT, D], BF16, kind="ExternalOutput").ap()
    tot_o = nc.dram_tensor("tot_o", [1, 2 * D], F32, kind="ExternalOutput").ap()

    with contextlib.ExitStack() as es:
        sb = lambda name, shape, dt: es.enter_context(nc.sbuf_tensor(name, shape, dt))
        x1s = sb("x1s", [128, NT * D], F32)
        xn1T = sb("xn1T", [128, 8 * TOK], BF16)
        xn1T_v = xn1T[:].rearrange("p (c t) -> p c t", c=8)
        wq = sb("wq", [128, 8 * 3 * D], BF16)
        wq_v = wq[:].rearrange("p (a o) -> p a o", a=8)
        biasS = sb("biasS", [128, 2 * D], F32)
        bias_v = sb("bias_v", [128, D], F32)
        brow = sb("brow", [1, 3 * D], F32)
        srow = sb("srow", [1, 1], F32)
        scol = sb("scol", [128, 1], F32)
        onesf_t = sb("onesf", [1, 128], F32)
        onesb = sb("onesb", [128, 1], BF16)
        idb = sb("idb", [128, 128], BF16)
        xn_t = sb("xn_t", [128, D], BF16)
        qs = sb("qs", [128, 512], F32)
        m0 = sb("m0", [128, 512], F32)
        e0 = sb("e0", [128, 512], F32)
        r0 = sb("r0", [128, 512], F32)
        q1st = sb("q1st", [128, D], F32)
        k1st = sb("k1st", [128, D], F32)
        k1bf = sb("k1bf", [128, D], BF16)
        kvst = sb("kvst", [128, D], BF16)
        totst = sb("totst", [1, 2 * D], F32)
        c0 = sb("c0", [128, 1], F32)
        c1 = sb("c1", [128, 1], F32)
        c2 = sb("c2", [128, 1], F32)
        sq = sb("sq", [128, D], F32)
        ps_mm = [es.enter_context(nc.psum_tensor(f"ps_mm{i}", [128, 512], F32)) for i in range(2)]
        ps_tp = es.enter_context(nc.psum_tensor("ps_tp", [128, D], BF16))
        ps_tot = es.enter_context(nc.psum_tensor("ps_tot", [1, 2 * D], F32))

        p = Prog(nc)
        R = lambda n="": Res(n)
        r_x1s, r_xn1T, r_wq = R(), R(), R()
        r_biasS, r_biasv, r_brow, r_srow, r_scol = R(), R(), R(), R(), R()
        r_onesf, r_onesb, r_idb, r_xnt = R(), R(), R(), R()
        r_qs, r_m0, r_e0, r_r0 = R(), R(), R(), R()
        r_q1st, r_k1st, r_k1bf, r_kvst, r_totst = R(), R(), R(), R(), R()
        r_tmp = {"sq": R(), "c0": R(), "c1": R(), "c2": R()}
        tmp = {"sq": sq, "c0": c0, "c1": c1, "c2": c2}
        r_ps = [R(), R()]
        r_pstp, r_pstot = R(), R()

        p.dma("sync", "ld0", x1s[:].rearrange("p (t d) -> p t d", t=NT), x1_i,
              writes=[r_x1s])
        p.dma("gpsimd", "ldw", wq_v, wqkvT_i.rearrange("(a p) o -> p a o", p=128),
              writes=[r_wq])
        p.dma("sync", "ld0", brow[:], bqkv_i, writes=[r_brow])
        p.dma("sync", "ld0", srow[:], scale_i, writes=[r_srow])
        p.dma("gpsimd", "ldw", idb[:], iden_i, writes=[r_idb])
        p.op("vector", lambda e: e.memset(onesf_t[:], 1.0), writes=[r_onesf])
        p.op("vector", lambda e: e.memset(onesb[:], 1.0), writes=[r_onesb])

        p.op("tensor", lambda e: e.matmul(ps_mm[0][:, 0:1], onesf_t[:], srow[:],
                                          start=True, stop=True),
             reads=[r_onesf, r_srow], writes=[r_ps[0]])
        p.op("vector", lambda e: e.tensor_copy(scol[:], ps_mm[0][:, 0:1]),
             reads=[r_ps[0]], writes=[r_scol])
        for jc in range(6):
            par = jc % 2
            p.op("tensor",
                 lambda e, jc=jc, par=par: e.matmul(
                     ps_mm[par][:], onesf_t[:], brow[0:1, 512 * jc:512 * jc + 512],
                     start=True, stop=True),
                 reads=[r_onesf, r_brow], writes=[r_ps[par]])
            if jc < 4:
                p.op("vector",
                     lambda e, jc=jc, par=par: e.tensor_scalar_mul(
                         biasS[:, 512 * jc:512 * jc + 512], ps_mm[par][:], scol[:]),
                     reads=[r_ps[par], r_scol], writes=[r_biasS])
            else:
                p.op("vector",
                     lambda e, jc=jc, par=par: e.tensor_copy(
                         bias_v[:, 512 * (jc - 4):512 * (jc - 4) + 512], ps_mm[par][:]),
                     reads=[r_ps[par]], writes=[r_biasv])

        for tt in range(NT):
            x_ap = x1s[:, tt * D:(tt + 1) * D]
            ln_tile(p, x_ap, xn_t[:], r_x1s, r_xnt, tmp, r_tmp, 1e-5)
            for c in range(8):
                p.op("tensor",
                     lambda e, c=c: e.transpose(ps_tp[:, c * 128:(c + 1) * 128],
                                                xn_t[:, c * 128:(c + 1) * 128], idb[:]),
                     reads=[r_xnt, r_idb], writes=[r_pstp])
            dst = xn1T_v[:, :, tt * 128:(tt + 1) * 128]
            src = ps_tp[:].rearrange("p (c t) -> p c t", c=8)
            p.op("vector", lambda e, dst=dst, src=src: e.tensor_copy(dst, src),
                 reads=[r_pstp], writes=[r_xn1T])

            def elu1(jc, par, dst_ap, r_dst):
                p.op("vector",
                     lambda e, jc=jc, par=par: e.scalar_tensor_tensor(
                         qs[:], ps_mm[par][:], scol[:], biasS[:, 512 * jc:512 * jc + 512],
                         ALU.mult, ALU.add),
                     reads=[r_ps[par], r_scol, r_biasS], writes=[r_qs])
                p.op("vector", lambda e: e.tensor_scalar_min(m0[:], qs[:], 0.0),
                     reads=[r_qs], writes=[r_m0])
                p.op("scalar", lambda e: e.activation(e0[:], m0[:], AF.Exp),
                     reads=[r_m0], writes=[r_e0])
                p.op("vector", lambda e: e.tensor_scalar_max(r0[:], qs[:], 0.0),
                     reads=[r_qs], writes=[r_r0])
                p.op("vector", lambda e, dst_ap=dst_ap: e.tensor_add(dst_ap, e0[:], r0[:]),
                     reads=[r_e0, r_r0], writes=[r_dst])

            for jc in range(6):
                par = jc % 2
                for i in range(8):
                    p.op("tensor",
                         lambda e, i=i, jc=jc, par=par, tt=tt: e.matmul(
                             ps_mm[par][:],
                             xn1T_v[:, i, tt * 128:(tt + 1) * 128],
                             wq_v[:, i, 512 * jc:512 * jc + 512],
                             start=(i == 0), stop=(i == 7)),
                         reads=[r_xn1T, r_wq], writes=[r_ps[par]])
                half = jc % 2
                if jc < 2:
                    elu1(jc, par, q1st[:, 512 * half:512 * half + 512], r_q1st)
                elif jc < 4:
                    elu1(jc, par, k1st[:, 512 * half:512 * half + 512], r_k1st)
                else:
                    p.op("vector",
                         lambda e, par=par, half=half: e.tensor_add(
                             m0[:], ps_mm[par][:], bias_v[:, 512 * half:512 * half + 512]),
                         reads=[r_ps[par], r_biasv], writes=[r_m0])
                    p.op("vector",
                         lambda e, half=half: e.tensor_mul(
                             kvst[:, 512 * half:512 * half + 512], m0[:],
                             k1st[:, 512 * half:512 * half + 512]),
                         reads=[r_m0, r_k1st], writes=[r_kvst])
            p.op("vector", lambda e: e.tensor_copy(k1bf[:], k1st[:]),
                 reads=[r_k1st], writes=[r_k1bf])
            for half in range(2):
                p.op("tensor",
                     lambda e, half=half, tt=tt: e.matmul(
                         ps_tot[0:1, 512 * half:512 * half + 512], onesb[:],
                         k1bf[:, 512 * half:512 * half + 512],
                         start=(tt == 0), stop=(tt == NT - 1)),
                     reads=[r_onesb, r_k1bf], writes=[r_pstot])
                p.op("tensor",
                     lambda e, half=half, tt=tt: e.matmul(
                         ps_tot[0:1, 1024 + 512 * half:1024 + 512 * half + 512], onesb[:],
                         kvst[:, 512 * half:512 * half + 512],
                         start=(tt == 0), stop=(tt == NT - 1)),
                     reads=[r_onesb, r_kvst], writes=[r_pstot])
            p.dma("sync", "st", q1_o[:, tt, :], q1st[:], reads=[r_q1st])
            p.dma("sync", "st", k1_o[:, tt, :], k1bf[:], reads=[r_k1bf])
            p.dma("sync", "st", kv1_o[:, tt, :], kvst[:], reads=[r_kvst])
        p.op("vector", lambda e: e.tensor_copy(totst[:], ps_tot[:]),
             reads=[r_pstot], writes=[r_totst])
        p.dma("sync", "st", tot_o, totst[:], reads=[r_totst])
        p.dma("sync", "st", xn1T_o, xn1T_v, reads=[r_xn1T])
        st_ops = list(p.chan("st").ops)
        p.op("sync", lambda e: e.nop(), extra_deps=st_ops)
        p.build(es)
    return nc



def build_attn1():
    nc = bass.Bass("TRN2", target_bir_lowering=False)
    q1_i = nc.dram_tensor("q1_i", [128, NT, D], F32, kind="ExternalInput").ap()
    k1_i = nc.dram_tensor("k1_i", [128, NT, D], BF16, kind="ExternalInput").ap()
    kv1_i = nc.dram_tensor("kv1_i", [128, NT, D], BF16, kind="ExternalInput").ap()
    tots_i = nc.dram_tensor("tots_i", [3, 2 * D], F32, kind="ExternalInput").ap()
    tri_i = nc.dram_tensor("tri_i", [128, 128], F32, kind="ExternalInput").ap()
    wpT_i = nc.dram_tensor("wpT_i", [D, D], F32, kind="ExternalInput").ap()
    bp_i = nc.dram_tensor("bp_i", [1, D], F32, kind="ExternalInput").ap()
    x1_i = nc.dram_tensor("x1_i", [128, NT, D], F32, kind="ExternalInput").ap()
    iden_i = nc.dram_tensor("iden_i", [128, 128], F32, kind="ExternalInput").ap()
    x1a_o = nc.dram_tensor("x1a_o", [128, NT, D], F32, kind="ExternalOutput").ap()
    EPS = 1e-3

    with contextlib.ExitStack() as es:
        sb = lambda name, shape, dt: es.enter_context(nc.sbuf_tensor(name, shape, dt))
        wp = sb("wp", [128, 8 * D], BF16)
        wp_v = wp[:].rearrange("p (a o) -> p a o", a=8)
        bp_bc = sb("bp_bc", [128, D], F32)
        tri = sb("tri", [128, 128], BF16)
        idb = sb("idb", [128, 128], BF16)
        tots = sb("tots", [3, 2 * D], F32)
        carry_k = sb("carry_k", [1, D], F32)
        carry_kv = sb("carry_kv", [1, D], F32)
        onesf_t = sb("onesf", [3, 128], F32)
        onesb = sb("onesb", [128, 1], BF16)
        q1b = [sb(f"q1b{i}", [128, D], F32) for i in range(2)]
        k1b = [sb(f"k1b{i}", [128, D], BF16) for i in range(2)]
        kvb = [sb(f"kvb{i}", [128, D], BF16) for i in range(2)]
        x1b = [sb(f"x1b{i}", [128, D], F32) for i in range(2)]
        dq = sb("dq", [128, 512], F32)
        den = sb("den", [128, 16], F32)
        dinv = sb("dinv", [128, 16], F32)
        y0 = sb("y0", [128, 512], F32)
        y_bf = sb("y_bf", [128, D], BF16)
        yT = sb("yT", [128, 8 * 128], BF16)
        ot1 = sb("ot1", [128, D], F32)
        bprow = sb("bprow", [1, D], F32)
        ps_cs = es.enter_context(nc.psum_tensor("ps_cs", [128, 512], F32))
        ps_kv = es.enter_context(nc.psum_tensor("ps_kv", [128, 512], F32))
        ps_rk = es.enter_context(nc.psum_tensor("ps_rk", [1, D], F32))
        ps_rkv = es.enter_context(nc.psum_tensor("ps_rkv", [1, D], F32))
        ps_tp = es.enter_context(nc.psum_tensor("ps_tp", [128, D], BF16))
        ps_pr = es.enter_context(nc.psum_tensor("ps_pr", [128, 512], F32))

        p = Prog(nc)
        R = lambda n="": Res(n)
        r_wp, r_bpbc, r_tri, r_idb, r_tots = R(), R(), R(), R(), R()
        r_ck, r_ckv, r_onesf, r_bprow, r_onesb = R(), R(), R(), R(), R()
        r_q1b, r_k1b, r_kvb, r_x1b = [R(), R()], [R(), R()], [R(), R()], [R(), R()]
        r_dq, r_den, r_dinv, r_y0, r_ybf, r_yT, r_ot1 = R(), R(), R(), R(), R(), R(), R()
        r_pscs, r_pskv, r_psrk, r_psrkv, r_pstp, r_pspr = R(), R(), R(), R(), R(), R()

        p.dma("gpsimd", "ldw", wp_v, wpT_i.rearrange("(a p) o -> p a o", p=128),
              writes=[r_wp])
        p.dma("gpsimd", "ldw", tri[:], tri_i, writes=[r_tri])
        p.dma("gpsimd", "ldw", idb[:], iden_i, writes=[r_idb])
        p.dma("sync", "ld0", tots[:], tots_i, writes=[r_tots])
        p.dma("sync", "ld0", bprow[:], bp_i, writes=[r_bprow])
        p.op("vector", lambda e: e.memset(onesf_t[:], 1.0), writes=[r_onesf])
        p.op("vector", lambda e: e.memset(onesb[:], 1.0), writes=[r_onesb])

        for half in range(2):
            p.op("tensor",
                 lambda e, half=half: e.matmul(
                     ps_pr[:], onesf_t[0:1, :], bprow[0:1, 512 * half:512 * half + 512],
                     start=True, stop=True),
                 reads=[r_onesf, r_bprow], writes=[r_pspr])
            p.op("vector",
                 lambda e, half=half: e.tensor_copy(bp_bc[:, 512 * half:512 * half + 512], ps_pr[:]),
                 reads=[r_pspr], writes=[r_bpbc])
        for half in range(2):
            p.op("tensor",
                 lambda e, half=half: e.matmul(
                     ps_rk[0:1, 512 * half:512 * half + 512], onesf_t[:, 0:1],
                     tots[:, 512 * half:512 * half + 512], start=True, stop=True),
                 reads=[r_onesf, r_tots], writes=[r_psrk])
            p.op("tensor",
                 lambda e, half=half: e.matmul(
                     ps_rkv[0:1, 512 * half:512 * half + 512], onesf_t[:, 0:1],
                     tots[:, 1024 + 512 * half:1024 + 512 * half + 512], start=True, stop=True),
                 reads=[r_onesf, r_tots], writes=[r_psrkv])
        p.op("vector", lambda e: e.tensor_copy(carry_k[:], ps_rk[:]),
             reads=[r_psrk], writes=[r_ck])
        p.op("vector", lambda e: e.tensor_copy(carry_kv[:], ps_rkv[:]),
             reads=[r_psrkv], writes=[r_ckv])

        for tt in range(NT):
            par = tt % 2
            p.dma("sync", f"lq{par}", q1b[par][:], q1_i[:, tt, :], writes=[r_q1b[par]])
            p.dma("sync", f"lk{par}", k1b[par][:], k1_i[:, tt, :], writes=[r_k1b[par]])
            p.dma("sync", f"lv{par}", kvb[par][:], kv1_i[:, tt, :], writes=[r_kvb[par]])
            p.dma("sync", f"lx{par}", x1b[par][:], x1_i[:, tt, :], writes=[r_x1b[par]])
            for half in range(2):
                sl = slice(512 * half, 512 * half + 512)
                p.op("tensor",
                     lambda e, par=par, sl=sl: e.matmul(ps_cs[:], tri[:], k1b[par][:, sl],
                                                        start=True, stop=False),
                     reads=[r_tri, r_k1b[par]], writes=[r_pscs])
                p.op("tensor",
                     lambda e, sl=sl: e.matmul(ps_cs[:], onesf_t[0:1, :], carry_k[0:1, sl],
                                               start=False, stop=True),
                     reads=[r_onesf, r_ck], writes=[r_pscs])
                p.op("tensor",
                     lambda e, par=par, sl=sl: e.matmul(ps_kv[:], tri[:], kvb[par][:, sl],
                                                        start=True, stop=False),
                     reads=[r_tri, r_kvb[par]], writes=[r_pskv])
                p.op("tensor",
                     lambda e, sl=sl: e.matmul(ps_kv[:], onesf_t[0:1, :], carry_kv[0:1, sl],
                                               start=False, stop=True),
                     reads=[r_onesf, r_ckv], writes=[r_pskv])
                p.op("vector",
                     lambda e, par=par, sl=sl: e.tensor_mul(dq[:], q1b[par][:, sl], ps_cs[:]),
                     reads=[r_q1b[par], r_pscs], writes=[r_dq])
                dsl = slice(8 * half, 8 * half + 8)
                p.op("vector",
                     lambda e, dsl=dsl: e.tensor_reduce(
                         den[:, dsl].unsqueeze(2),
                         dq[:].rearrange("p (a b) -> p a b", a=8), AX.X, ALU.add),
                     reads=[r_dq], writes=[r_den])
                p.op("vector",
                     lambda e, dsl=dsl: e.tensor_scalar_max(den[:, dsl], den[:, dsl], EPS),
                     reads=[r_den], writes=[r_den])
                p.op("vector",
                     lambda e, dsl=dsl: e.reciprocal(dinv[:, dsl], den[:, dsl]),
                     reads=[r_den], writes=[r_dinv])
                p.op("vector",
                     lambda e, par=par, sl=sl: e.tensor_mul(y0[:], q1b[par][:, sl], ps_kv[:]),
                     reads=[r_q1b[par], r_pskv], writes=[r_y0])
                bview = dinv[:, dsl].unsqueeze(2).broadcast_to([128, 8, 64])
                p.op("vector",
                     lambda e, sl=sl, bview=bview: e.tensor_tensor(
                         y_bf[:, sl].rearrange("p (a b) -> p a b", a=8),
                         y0[:].rearrange("p (a b) -> p a b", a=8), bview, ALU.mult),
                     reads=[r_y0, r_dinv], writes=[r_ybf])
            if tt < NT - 1:
                for half in range(2):
                    sl = slice(512 * half, 512 * half + 512)
                    p.op("tensor",
                         lambda e, par=par, sl=sl: e.matmul(
                             ps_rk[0:1, sl], onesb[:], k1b[par][:, sl],
                             start=False, stop=False),
                         reads=[r_onesb, r_k1b[par]], writes=[r_psrk])
                    p.op("tensor",
                         lambda e, par=par, sl=sl: e.matmul(
                             ps_rkv[0:1, sl], onesb[:], kvb[par][:, sl],
                             start=False, stop=False),
                         reads=[r_onesb, r_kvb[par]], writes=[r_psrkv])
                p.op("vector", lambda e: e.tensor_copy(carry_k[:], ps_rk[:]),
                     reads=[r_psrk], writes=[r_ck])
                p.op("vector", lambda e: e.tensor_copy(carry_kv[:], ps_rkv[:]),
                     reads=[r_psrkv], writes=[r_ckv])
            for c in range(8):
                p.op("tensor",
                     lambda e, c=c: e.transpose(ps_tp[:, c * 128:(c + 1) * 128],
                                                y_bf[:, c * 128:(c + 1) * 128], idb[:]),
                     reads=[r_ybf, r_idb], writes=[r_pstp])
            p.op("vector", lambda e: e.tensor_copy(yT[:], ps_tp[:]),
                 reads=[r_pstp], writes=[r_yT])
            for half in range(2):
                sl = slice(512 * half, 512 * half + 512)
                for c in range(8):
                    p.op("tensor",
                         lambda e, c=c, sl=sl: e.matmul(
                             ps_pr[:], yT[:].rearrange("p (c t) -> p c t", c=8)[:, c, :],
                             wp_v[:, c, sl], start=(c == 0), stop=(c == 7)),
                         reads=[r_yT, r_wp], writes=[r_pspr])
                p.op("vector",
                     lambda e, sl=sl: e.tensor_add(ot1[:, sl], ps_pr[:], bp_bc[:, sl]),
                     reads=[r_pspr, r_bpbc], writes=[r_ot1])
            p.op("vector", lambda e, par=par: e.tensor_add(ot1[:], ot1[:], x1b[par][:]),
                 reads=[r_ot1, r_x1b[par]], writes=[r_ot1])
            p.dma("sync", "st", x1a_o[:, tt, :], ot1[:], reads=[r_ot1])
        st_ops = list(p.chan("st").ops)
        p.op("sync", lambda e: e.nop(), extra_deps=st_ops)
        p.build(es)
    return nc



def build_ffn1():
    nc = bass.Bass("TRN2", target_bir_lowering=False)
    xn1T_i = nc.dram_tensor("xn1T_i", [128, 8, TOK], BF16, kind="ExternalInput").ap()
    wg_i = nc.dram_tensor("wg_i", [8, 32, D, 128], F32, kind="ExternalInput").ap()
    wu_i = nc.dram_tensor("wu_i", [8, 32, D, 128], F32, kind="ExternalInput").ap()
    wd_i = nc.dram_tensor("wd_i", [8, 32, 128, D], F32, kind="ExternalInput").ap()
    x1a_i = nc.dram_tensor("x1a_i", [128, NT, D], F32, kind="ExternalInput").ap()
    y_o = nc.dram_tensor("y_o", [128, NT, D], F32, kind="ExternalOutput").ap()

    with contextlib.ExitStack() as es:
        sb = lambda name, shape, dt: es.enter_context(nc.sbuf_tensor(name, shape, dt))
        x1a = sb("x1a", [128, NT * D], F32)
        xn1T = sb("xn1T", [128, 8 * TOK], BF16)
        xn1T_v = xn1T[:].rearrange("p (c t) -> p c t", c=8)
        hT = sb("hT", [128, 32 * 1024], BF16)
        hT_v = hT[:].rearrange("p (c t) -> p c t", c=32)
        wd_res = sb("wd_res", [128, 16 * D], BF16)
        wd_v = wd_res[:].rearrange("p (h m) -> p h m", h=16)
        wgb = [sb(f"wgb{i}", [128, 8 * 128], BF16) for i in range(2)]
        wub = [sb(f"wub{i}", [128, 8 * 128], BF16) for i in range(2)]
        gs = sb("gs", [128, 1024], F32)
        ps_g = es.enter_context(nc.psum_tensor("ps_g", [128, 1024], F32))
        ps_u = es.enter_context(nc.psum_tensor("ps_u", [128, 1024], F32))
        ps_d = [es.enter_context(nc.psum_tensor(f"ps_d{i}", [128, 1024], F32)) for i in range(2)]

        p = Prog(nc)
        R = lambda n="": Res(n)
        r_x1a, r_xn1T, r_hT, r_gs, r_wd = R("x1a"), R(), R("hT"), R(), R("wd")
        r_wgb, r_wub = [R(), R()], [R(), R()]
        r_psg, r_psu = R(), R()
        r_psd = [R(), R()]

        p.dma("sync", "ld0", x1a[:].rearrange("p (t d) -> p t d", t=NT), x1a_i,
              writes=[r_x1a])
        p.dma("sync", "ld0", xn1T_v, xn1T_i, writes=[r_xn1T])

        for e_ in range(8):
            for half in range(2):
                hoff = 1024 * half
                for hid in range(32):
                    par = hid % 2
                    p.dma("gpsimd", f"lg{par}", wgb[par][:].rearrange("p (a m) -> p a m", a=8),
                          wg_i[e_, hid].rearrange("(a p) m -> p a m", p=128),
                          writes=[r_wgb[par]])
                    p.dma("gpsimd", f"lu{par}", wub[par][:].rearrange("p (a m) -> p a m", a=8),
                          wu_i[e_, hid].rearrange("(a p) m -> p a m", p=128),
                          writes=[r_wub[par]])
                    for i in range(8):
                        for q2 in range(2):
                            p.op("tensor",
                                 lambda e, i=i, q2=q2, par=par, hoff=hoff: e.matmul(
                                     ps_g[:, 512 * q2:512 * q2 + 512],
                                     wgb[par][:].rearrange("p (a m) -> p a m", a=8)[:, i, :],
                                     xn1T_v[:, i, hoff + 512 * q2:hoff + 512 * q2 + 512],
                                     start=(i == 0), stop=(i == 7)),
                                 reads=[r_wgb[par], r_xn1T], writes=[r_psg])
                    for i in range(8):
                        for q2 in range(2):
                            p.op("tensor",
                                 lambda e, i=i, q2=q2, par=par, hoff=hoff: e.matmul(
                                     ps_u[:, 512 * q2:512 * q2 + 512],
                                     wub[par][:].rearrange("p (a m) -> p a m", a=8)[:, i, :],
                                     xn1T_v[:, i, hoff + 512 * q2:hoff + 512 * q2 + 512],
                                     start=(i == 0), stop=(i == 7)),
                                 reads=[r_wub[par], r_xn1T], writes=[r_psu])
                    p.op("scalar", lambda e: e.activation(gs[:], ps_g[:], AF.Silu),
                         reads=[r_psg], writes=[r_gs])
                    p.op("vector",
                         lambda e, hid=hid: e.tensor_mul(hT_v[:, hid, :], gs[:], ps_u[:]),
                         reads=[r_gs, r_psu], writes=[r_hT])
                for hg in range(2):
                    p.dma("gpsimd", "lwd", wd_v,
                          wd_i[e_, 16 * hg:16 * hg + 16].rearrange("h p m -> p h m"),
                          writes=[r_wd])
                    for t8 in range(8):
                        tpar = t8 % 2
                        tt = 8 * half + t8
                        for hl in range(16):
                            hid = 16 * hg + hl
                            for q2 in range(2):
                                p.op("tensor",
                                     lambda e, hid=hid, hl=hl, q2=q2, t8=t8, tpar=tpar: e.matmul(
                                         ps_d[tpar][:, 512 * q2:512 * q2 + 512],
                                         hT_v[:, hid, 128 * t8:128 * t8 + 128],
                                         wd_v[:, hl, 512 * q2:512 * q2 + 512],
                                         start=(hl == 0), stop=(hl == 15)),
                                     reads=[r_hT, r_wd], writes=[r_psd[tpar]])
                        xt = x1a[:, tt * D:(tt + 1) * D]
                        p.op("vector",
                             lambda e, xt=xt, tpar=tpar: e.tensor_add(xt, xt, ps_d[tpar][:]),
                             reads=[r_psd[tpar], r_x1a], writes=[r_x1a])
        p.dma("sync", "st", y_o, x1a[:].rearrange("p (t d) -> p t d", t=NT),
              reads=[r_x1a])
        st_ops = list(p.chan("st").ops)
        p.op("sync", lambda e: e.nop(), extra_deps=st_ops)
        p.build(es)
    return nc

def _f32(a):
    return np.ascontiguousarray(a, dtype=np.float32)


def stage_kTf(kT_cores):
    return np.ascontiguousarray(np.concatenate(kT_cores, axis=2))


def stage_vaug(v_cores):
    v_all = np.concatenate(v_cores, axis=1)
    vaug = np.zeros((128, 4, 64, 260), dtype=BF)
    for hp in range(4):
        for hh in range(4):
            h = 4 * hp + hh
            vaug[:, hp, :, 65 * hh:65 * hh + 64] = v_all[:, :, 64 * h:64 * h + 64]
            vaug[:, hp, :, 65 * hh + 64] = np.ones((), dtype=BF)
    return vaug


def stage_grT(RT, pos):
    ks = np.arange(S)
    k16 = ks >> 4
    kl = ks & 15
    out = np.empty((len(PAIRS), S, 128), dtype=np.float32)
    for j, (a, l) in enumerate(PAIRS):
        C = 8191 - 512 * a + 512 * l
        for jj in range(128):
            r = C - (128 * pos + jj)
            out[j, :, jj] = RT[np.minimum(r + k16, 2046), kl]
    return out


def stage_b0(RT):
    return np.ascontiguousarray(RT[2046, np.arange(128) % 16, None].astype(np.float32))


def _tileize(x):
    return np.ascontiguousarray(x.reshape(NT, 128, D).transpose(1, 0, 2))


def _untile(x):
    return np.asarray(x).transpose(1, 0, 2).reshape(TOK, D)


def _pretile(wT):
    return np.ascontiguousarray(wT.reshape(wT.shape[0], 32, 128).transpose(1, 0, 2))


def _scale128(s):
    return np.ascontiguousarray(np.asarray(s, dtype=np.float32).reshape(-1, 128).T)


_BUILD_CACHE = {}


def _get(name, fn):
    if name not in _BUILD_CACHE:
        _BUILD_CACHE[name] = fn()
    return _BUILD_CACHE[name]


def kernel_unfused(L0_attn_x, L0_attn_Wq, L0_attn_Wk, L0_attn_Wv, L0_attn_rel_table,
           L0_attn_gamma, L0_attn_beta, L0_ffn_w1_w, L0_ffn_w1_s, L0_ffn_w2_w,
           L0_ffn_w2_s, L0_ffn_h1_w, L0_ffn_h1_s, L0_ffn_h2_w, L0_ffn_h2_s,
           L0_ffn_w3_w, L0_ffn_w3_s, L1_attn_Wqkv, L1_attn_bqkv, L1_attn_Wproj,
           L1_attn_bproj, L1_attn_scale, L1_ffn_Wg, L1_ffn_Wu, L1_ffn_Wd):
    x = _f32(L0_attn_x)
    wq, wk, wv = _f32(L0_attn_Wq), _f32(L0_attn_Wk), _f32(L0_attn_Wv)
    RT = _f32(L0_attn_rel_table)
    iden = np.eye(128, dtype=np.float32)
    tri = np.triu(np.ones((128, 128), dtype=np.float32))
    cores = [(c // 4, c % 4) for c in range(NCORE)]
    cid = list(range(NCORE))

    nc1 = _get("pre0", build_pre0)
    maps1 = [dict(x_in=np.ascontiguousarray(x[b, 2048 * p:2048 * (p + 1)]),
                  wq_in=wq, wk_in=wk, wv_in=wv, iden_in=iden)
             for (b, p) in cores]
    r1 = run_bass_kernel_spmd(nc1, maps1, core_ids=cid).results

    kTf = [stage_kTf([r1[4 * b + j]["kT_o"] for j in range(4)]) for b in range(2)]
    vaug = [stage_vaug([r1[4 * b + j]["v_o"] for j in range(4)]) for b in range(2)]
    grT = [stage_grT(RT, p) for p in range(4)]
    b0 = stage_b0(RT)
    nc2 = _get("attn0", build_attn0)
    maps2 = [dict(qT_i=np.asarray(r1[c]["qT_o"]), kT_i=kTf[b], vaug_i=vaug[b],
                  grT_i=grT[p], b0_i=b0)
             for c, (b, p) in enumerate(cores)]
    r2 = run_bass_kernel_spmd(nc2, maps2, core_ids=cid).results

    nc3 = _get("ffn0", build_ffn0)
    com3 = dict(
        gamma_i=_f32(L0_attn_gamma)[None], beta_i=_f32(L0_attn_beta)[None],
        w1t_i=_pretile(_f32(L0_ffn_w1_w).T), w2t_i=_pretile(_f32(L0_ffn_w2_w).T),
        w1s_i=_scale128(L0_ffn_w1_s), w2s_i=_scale128(L0_ffn_w2_s),
        h1t_i=_pretile(_f32(L0_ffn_h1_w).T), h2t_i=_pretile(_f32(L0_ffn_h2_w).T),
        h1s_i=_scale128(L0_ffn_h1_s), h2s_i=_scale128(L0_ffn_h2_s),
        w3t_i=np.ascontiguousarray(_f32(L0_ffn_w3_w).T), w3s_i=_f32(L0_ffn_w3_s)[None],
        iden_i=iden)
    maps3 = [dict(x_in=np.ascontiguousarray(x[b, 2048 * p:2048 * (p + 1)]),
                  attnT_i=np.asarray(r2[c]["attnT_o"]), u_i=np.asarray(r2[c]["u_o"]),
                  xnT_i=np.asarray(r1[c]["xnT_o"]), **com3)
             for c, (b, p) in enumerate(cores)]
    r3 = run_bass_kernel_spmd(nc3, maps3, core_ids=cid).results

    nc4 = _get("pre1", build_pre1)
    com4 = dict(wqkvT_i=np.ascontiguousarray(_f32(L1_attn_Wqkv).T),
                bqkv_i=_f32(L1_attn_bqkv)[None],
                scale_i=_f32(L1_attn_scale).reshape(1, 1), iden_i=iden)
    maps4 = [dict(x1_i=np.asarray(r3[c]["x1_o"]), **com4) for c in range(NCORE)]
    r4 = run_bass_kernel_spmd(nc4, maps4, core_ids=cid).results

    nc5 = _get("attn1", build_attn1)
    com5 = dict(tri_i=tri, wpT_i=np.ascontiguousarray(_f32(L1_attn_Wproj).T),
                bp_i=_f32(L1_attn_bproj)[None], iden_i=iden)
    maps5 = []
    for c, (b, p) in enumerate(cores):
        tots = np.zeros((3, 2 * D), dtype=np.float32)
        for j in range(p):
            tots[j] = np.asarray(r4[4 * b + j]["tot_o"])[0]
        maps5.append(dict(q1_i=np.asarray(r4[c]["q1_o"]), k1_i=np.asarray(r4[c]["k1_o"]),
                          kv1_i=np.asarray(r4[c]["kv1_o"]), tots_i=tots,
                          x1_i=np.asarray(r3[c]["x1_o"]), **com5))
    r5 = run_bass_kernel_spmd(nc5, maps5, core_ids=cid).results

    nc6 = _get("ffn1", build_ffn1)
    Wg, Wu, Wd = _f32(L1_ffn_Wg), _f32(L1_ffn_Wu), _f32(L1_ffn_Wd)
    com6 = dict(wg_i=np.ascontiguousarray(Wg.reshape(8, D, 32, 128).transpose(0, 2, 1, 3)),
                wu_i=np.ascontiguousarray(Wu.reshape(8, D, 32, 128).transpose(0, 2, 1, 3)),
                wd_i=np.ascontiguousarray(Wd.reshape(8, 32, 128, D)))
    maps6 = [dict(xn1T_i=np.asarray(r4[c]["xn1T_o"]), x1a_i=np.asarray(r5[c]["x1a_o"]),
                  **com6) for c in range(NCORE)]
    r6 = run_bass_kernel_spmd(nc6, maps6, core_ids=cid).results

    out = np.empty((2, S, D), dtype=np.float32)
    for c, (b, p) in enumerate(cores):
        out[b, 2048 * p:2048 * (p + 1)] = _untile(r6[c]["y_o"])
    return out



def build_fused():
    nc = bass.Bass("TRN2", target_bir_lowering=False)
    DH = 4096
    T = nc.dram_tensor
    x_in = T("x_in", [TOK, D], F32, kind="ExternalInput").ap()
    wq_in = T("wq_in", [D, D], F32, kind="ExternalInput").ap()
    wk_in = T("wk_in", [D, D], F32, kind="ExternalInput").ap()
    wv_in = T("wv_in", [D, D], F32, kind="ExternalInput").ap()
    iden_in = T("iden_in", [128, 128], F32, kind="ExternalInput").ap()
    grT_i = T("grT_i", [len(PAIRS), S, 128], F32, kind="ExternalInput").ap()
    b0_i = T("b0_i", [128, 1], F32, kind="ExternalInput").ap()
    gamma_i = T("gamma_i", [1, D], F32, kind="ExternalInput").ap()
    beta_i = T("beta_i", [1, D], F32, kind="ExternalInput").ap()
    w1t_i = T("w1t_i", [32, D, 128], F32, kind="ExternalInput").ap()
    w2t_i = T("w2t_i", [32, D, 128], F32, kind="ExternalInput").ap()
    w1s_i = T("w1s_i", [128, 32], F32, kind="ExternalInput").ap()
    w2s_i = T("w2s_i", [128, 32], F32, kind="ExternalInput").ap()
    h1t_i = T("h1t_i", [32, DH, 128], F32, kind="ExternalInput").ap()
    h2t_i = T("h2t_i", [32, DH, 128], F32, kind="ExternalInput").ap()
    h1s_i = T("h1s_i", [128, 32], F32, kind="ExternalInput").ap()
    h2s_i = T("h2s_i", [128, 32], F32, kind="ExternalInput").ap()
    w3t_i = T("w3t_i", [DH, D], F32, kind="ExternalInput").ap()
    w3s_i = T("w3s_i", [1, D], F32, kind="ExternalInput").ap()
    wqkvT_i = T("wqkvT_i", [D, 3 * D], F32, kind="ExternalInput").ap()
    bqkv_i = T("bqkv_i", [1, 3 * D], F32, kind="ExternalInput").ap()
    scale_i = T("scale_i", [1, 1], F32, kind="ExternalInput").ap()
    wpT_i = T("wpT_i", [D, D], F32, kind="ExternalInput").ap()
    bp_i = T("bp_i", [1, D], F32, kind="ExternalInput").ap()
    tri_i = T("tri_i", [128, 128], F32, kind="ExternalInput").ap()
    mask_i = T("mask_i", [4, 1], F32, kind="ExternalInput").ap()
    wg_i = T("wg_i", [8, 32, D, 128], F32, kind="ExternalInput").ap()
    wu_i = T("wu_i", [8, 32, D, 128], F32, kind="ExternalInput").ap()
    wd_i = T("wd_i", [8, 32, 128, D], F32, kind="ExternalInput").ap()
    y_o = T("y_o", [128, NT, D], F32, kind="ExternalOutput").ap()
    kt_c = T("kt_c", [128, 8, TOK], BF16, kind="Internal").ap()
    kt_g = T("kt_g", [512, 8 * TOK], BF16, kind="Internal").ap()
    va_c = T("va_c", [128, 8 * NT * 130], BF16, kind="Internal").ap()
    va_g = T("va_g", [512, 8 * NT * 130], BF16, kind="Internal").ap()
    qt_d = T("qt_d", [128, 8, TOK], BF16, kind="Internal").ap()
    attnT_d = T("attnT_d", [64, NH, TOK], F32, kind="Internal").ap()
    x1tmp = T("x1tmp", [128, NT * D], F32, kind="Internal").ap()
    bt_d = T("bt_d", [32, 128, TOK], BF16, kind="Internal").ap()
    ct_d = T("ct_d", [32, 128, TOK], BF16, kind="Internal").ap()
    x1_d = T("x1_d", [128, NT, D], F32, kind="Internal").ap()
    q1_d = T("q1_d", [128, NT, D], F32, kind="Internal").ap()
    k1_d = T("k1_d", [128, NT, D], BF16, kind="Internal").ap()
    kv1_d = T("kv1_d", [128, NT, D], BF16, kind="Internal").ap()
    tot_c = T("tot_c", [1, 2 * D], F32, kind="Internal").ap()
    tot_g = T("tot_g", [4, 2 * D], F32, kind="Internal").ap()

    SCALE = 1.0 / 32.0
    RG = [[0, 1, 2, 3], [4, 5, 6, 7]]
    KB = 1024
    EPS1 = 1e-3

    with contextlib.ExitStack() as es:
        sb = lambda name, shape, dt: es.enter_context(nc.sbuf_tensor(name, shape, dt))
        idb = sb("idb", [128, 128], BF16)
        idf = sb("idf", [128, 128], F32)
        onesf_t = sb("onesf", [65, 128], F32)
        onesb = sb("onesb", [128, 1], BF16)
        c0 = sb("c0", [128, 1], F32)
        c1 = sb("c1", [128, 1], F32)
        c2 = sb("c2", [128, 1], F32)
        scol = sb("scol", [128, 1], F32)
        wsc = {k: sb(f"wsc_{k}", [128, 32], F32) for k in ("w1", "w2", "h1", "h2")}
        sqt = sb("sqt", [128, D], F32)
        ARENA_B = 206784
        arena_t = sb("arena", [128, ARENA_B // 2], BF16)
        A = Arena(arena_t, ARENA_B)
        psum_t = es.enter_context(nc.psum_tensor("psar", [128, 4096], F32))
        PA = Arena(psum_t, 16384)

        p = Prog(nc)
        R = lambda n="": Res(n)
        r_idb, r_idf, r_onesf, r_onesb = R(), R(), R(), R()
        r_c0, r_c1, r_c2, r_scol, r_sq = R(), R(), R(), R(), R()
        r_wsc = {k: R() for k in wsc}
        tmp = {"sq": sqt, "c0": c0, "c1": c1, "c2": c2}
        r_tmp = {"sq": r_sq, "c0": r_c0, "c1": r_c1, "c2": r_c2}

        p.dma("gpsimd", "ldi", idb[:], iden_in, writes=[r_idb])
        p.dma("sync", "ldi", idf[:], iden_in, writes=[r_idf])
        p.op("vector", lambda e: e.memset(onesf_t[:], 1.0), writes=[r_onesf])
        p.op("vector", lambda e: e.memset(onesb[:], 1.0), writes=[r_onesb])

        rg_xs = A.alloc(64 * KB, "xs")
        xs = rg_xs.view(F32, NT * D)
        rg_xnT = A.alloc(32 * KB, "xnT", side="top")
        xnT_v = rg_xnT.view(BF16, 8 * TOK).rearrange("p (c t) -> p c t", c=8)
        rg_w1p = A.alloc(16 * KB, "wproj")
        wpb_v = rg_w1p.view(BF16, 8 * D).rearrange("p (a o) -> p a o", a=8)
        rg_vloc = A.alloc(8 * NT * 130 * 2, "vloc")
        vloc_v = rg_vloc.view(BF16, 8 * NT * 130).rearrange(
            "p (a t c) -> p a t c", a=8, t=NT)
        rg_xnt1 = A.alloc(2 * KB, "xn_t")
        xn_t = rg_xnt1.view(BF16, D)
        rg_ev = [A.alloc(KB, f"ev{i}") for i in range(4)]
        ev = [rg.view(BF16, 512) for rg in rg_ev]
        r_xs, r_xnT, r_wp = rg_xs.res, rg_xnT.res, rg_w1p.res
        r_vloc, r_xnt = rg_vloc.res, rg_xnt1.res
        r_ev = [rg.res for rg in rg_ev]
        rg_pmm = [PA.alloc(2048, f"pmm{i}") for i in range(4)]
        pmm = [rg.view(F32, 512) for rg in rg_pmm]
        r_pmm = [rg.res for rg in rg_pmm]
        rg_pst = PA.alloc(2048, "pst")
        pst = rg_pst.view(BF16, D)
        r_pst = rg_pst.res

        p.dma("sync", "ldx", xs.rearrange("p (t d) -> p t d", t=NT),
              x_in.rearrange("(t p) d -> p t d", p=128), writes=[r_xs])
        p.op("vector", lambda e: e.memset(vloc_v, 1.0), writes=[r_vloc])
        for tt in range(NT):
            x_ap = xs[:, tt * D:(tt + 1) * D]
            ln_tile(p, x_ap, xn_t, r_xs, r_xnt, tmp, r_tmp, 1e-5)
            for c in range(8):
                p.op("tensor",
                     lambda e, c=c: e.transpose(pst[:, c * 128:(c + 1) * 128],
                                                xn_t[:, c * 128:(c + 1) * 128], idb[:]),
                     reads=[r_xnt, r_idb], writes=[r_pst])
            dst = xnT_v[:, :, tt * 128:(tt + 1) * 128]
            src = pst.rearrange("p (c t) -> p c t", c=8)
            p.op("vector", lambda e, dst=dst, src=src: e.tensor_copy(dst, src),
                 reads=[r_pst], writes=[r_xnT])
        rg_xs.free()
        rg_psg2 = [PA.alloc(2048, f"psg2{i}", side="top") for i in range(2)]
        psg2 = [rg.view(F32, 512) for rg in rg_psg2]
        r_psg2 = [rg.res for rg in rg_psg2]
        rg_bigh = A.alloc(64 * KB, "bigh")
        bigh_v = rg_bigh.view(BF16, 32 * 1024).rearrange("p (c t) -> p c t", c=32)
        rg_hb = [A.alloc(8 * KB, f"hb{i}") for i in range(2)]
        hb_v = [rg.view(BF16, 32 * 128) for rg in rg_hb]
        gsb = [sqt[:, 0:512], sqt[:, 512:1024]]
        rg_bst = [A.alloc(KB, f"bst{i}") for i in range(2)]
        bst = [rg.view(BF16, 512) for rg in rg_bst]
        for nm, sin in (("w1", w1s_i), ("w2", w2s_i), ("h1", h1s_i), ("h2", h2s_i)):
            p.dma("sync", "lds", wsc[nm][:], sin, writes=[r_wsc[nm]])
        r_bigh = rg_bigh.res
        r_hb = [rg.res for rg in rg_hb]
        r_gs = [r_sq, r_sq]
        r_bst = [rg.res for rg in rg_bst]
        st2_stores = []
        st3_stores = []

        def ffn_gen():
            for half in range(2):
                hoff = 1024 * half
                for ot in range(32):
                    par = ot % 2
                    p.dma("gpsimd", f"lw1{par}",
                          hb_v[par][:, 0:1024].rearrange("p (a m) -> p a m", a=8),
                          w1t_i[ot].rearrange("(a p) m -> p a m", p=128), writes=[r_hb[par]])
                    p.dma("gpsimd", f"lw2{par}",
                          hb_v[par][:, 1024:2048].rearrange("p (a m) -> p a m", a=8),
                          w2t_i[ot].rearrange("(a p) m -> p a m", p=128), writes=[r_hb[par]])
                    for t2c in range(2):
                        for i in range(8):
                            p.op("tensor",
                                 lambda e, i=i, par=par, t2c=t2c, hoff=hoff: e.matmul(
                                     psg2[0],
                                     hb_v[par][:, 0:1024].rearrange("p (a m) -> p a m", a=8)[:, i, :],
                                     xnT_v[:, i, hoff + 512 * t2c:hoff + 512 * t2c + 512],
                                     start=(i == 0), stop=(i == 7)),
                                 reads=[r_hb[par], r_xnT], writes=[r_psg2[0]])
                        yield
                        for i in range(8):
                            p.op("tensor",
                                 lambda e, i=i, par=par, t2c=t2c, hoff=hoff: e.matmul(
                                     psg2[1],
                                     hb_v[par][:, 1024:2048].rearrange("p (a m) -> p a m", a=8)[:, i, :],
                                     xnT_v[:, i, hoff + 512 * t2c:hoff + 512 * t2c + 512],
                                     start=(i == 0), stop=(i == 7)),
                                 reads=[r_hb[par], r_xnT], writes=[r_psg2[1]])
                        p.op("scalar",
                             lambda e, t2c=t2c, ot=ot: e.activation(
                                 gsb[t2c], psg2[0], AF.Silu, scale=wsc["w1"][:, ot:ot + 1]),
                             reads=[r_psg2[0], r_wsc["w1"]], writes=[r_gs[t2c]])
                        dst = bigh_v[:, ot, 512 * t2c:512 * t2c + 512]
                        p.op("vector",
                             lambda e, t2c=t2c, ot=ot, dst=dst: e.scalar_tensor_tensor(
                                 dst, psg2[1], wsc["w2"][:, ot:ot + 1], gsb[t2c],
                                 ALU.mult, ALU.mult),
                             reads=[r_psg2[1], r_wsc["w2"], r_gs[t2c]], writes=[r_bigh])
                        yield
                for o2 in range(32):
                    par = o2 % 2
                    p.dma("gpsimd", f"lh{par}", hb_v[par].rearrange("p (c m) -> p c m", c=32),
                          h1t_i[o2].rearrange("(c p) m -> p c m", p=128), writes=[r_hb[par]])
                    for t2c in range(2):
                        qar = t2c % 2
                        for c in range(32):
                            p.op("tensor",
                                 lambda e, c=c, par=par, qar=qar, t2c=t2c: e.matmul(
                                     psg2[qar],
                                     hb_v[par].rearrange("p (c m) -> p c m", c=32)[:, c, :],
                                     bigh_v[:, c, 512 * t2c:512 * t2c + 512],
                                     start=(c == 0), stop=(c == 31)),
                                 reads=[r_hb[par], r_bigh], writes=[r_psg2[qar]])
                            if c % 8 == 7 and c < 31:
                                yield
                        p.op("scalar",
                             lambda e, qar=qar, o2=o2: e.activation(
                                 bst[qar], psg2[qar], AF.Gelu, scale=wsc["h1"][:, o2:o2 + 1]),
                             reads=[r_psg2[qar], r_wsc["h1"]], writes=[r_bst[qar]])
                        st2_stores.append(
                            p.dma("sync", "stb",
                                  bt_d[o2, :, hoff + 512 * t2c:hoff + 512 * t2c + 512],
                                  bst[qar], reads=[r_bst[qar]]))
                        yield
                p.dma("sync", "ldb2", bigh_v,
                      bt_d[:, :, hoff:hoff + 1024].rearrange("c p t -> p c t"),
                      writes=[r_bigh], extra_deps=list(st2_stores))
                yield
                for o3 in range(32):
                    par = o3 % 2
                    p.dma("gpsimd", f"lh{par}", hb_v[par].rearrange("p (c m) -> p c m", c=32),
                          h2t_i[o3].rearrange("(c p) m -> p c m", p=128), writes=[r_hb[par]])
                    for t2c in range(2):
                        qar = t2c % 2
                        for c in range(32):
                            p.op("tensor",
                                 lambda e, c=c, par=par, qar=qar, t2c=t2c: e.matmul(
                                     psg2[qar],
                                     hb_v[par].rearrange("p (c m) -> p c m", c=32)[:, c, :],
                                     bigh_v[:, c, 512 * t2c:512 * t2c + 512],
                                     start=(c == 0), stop=(c == 31)),
                                 reads=[r_hb[par], r_bigh], writes=[r_psg2[qar]])
                            if c % 8 == 7 and c < 31:
                                yield
                        p.op("scalar",
                             lambda e, qar=qar, o3=o3: e.activation(
                                 bst[qar], psg2[qar], AF.Gelu, scale=wsc["h2"][:, o3:o3 + 1]),
                             reads=[r_psg2[qar], r_wsc["h2"]], writes=[r_bst[qar]])
                        st3_stores.append(
                            p.dma("sync", "stb",
                                  ct_d[o3, :, hoff + 512 * t2c:hoff + 512 * t2c + 512],
                                  bst[qar], reads=[r_bst[qar]]))
                        yield

        gen = ffn_gen()
        NQUANTA = 1282
        TOTKC = 8 * 4 * 64
        q_done = 0
        kc_glob = 0
        gen_state = {"done": False}

        def step_gen(n=1):
            for _ in range(n):
                if gen_state["done"]:
                    return
                if next(gen, "END") == "END":
                    gen_state["done"] = True
                    return
        kt_stores = []
        qt_stores = []
        for wi, win in ((0, wq_in), (1, wk_in)):
            p.dma("gpsimd", "ldw", wpb_v, win.rearrange("(a p) o -> p a o", p=128),
                  writes=[r_wp])
            for ft in range(8):
                for tc in range(4):
                    k4 = (ft * 4 + tc) % 4
                    for i in range(8):
                        p.op("tensor",
                             lambda e, i=i, ft=ft, tc=tc, k4=k4: e.matmul(
                                 pmm[k4], wpb_v[:, i, ft * 128:ft * 128 + 128],
                                 xnT_v[:, i, tc * 512:(tc + 1) * 512],
                                 start=(i == 0), stop=(i == 7)),
                             reads=[r_wp, r_xnT], writes=[r_pmm[k4]])
                    p.op("scalar", lambda e, k4=k4: e.activation(ev[k4], pmm[k4], AF.Copy),
                         reads=[r_pmm[k4]], writes=[r_ev[k4]])
                    dst_d = qt_d if wi == 0 else kt_c.rearrange("p c t -> p c t")
                    (qt_stores if wi == 0 else kt_stores).append(p.dma(
                        "sync", "stk", dst_d[:, ft, tc * 512:(tc + 1) * 512],
                        ev[k4], reads=[r_ev[k4]]))
                    step_gen(2)
        r_ktg = R("ktg")
        cc1 = p.op("gpsimd",
                   lambda e: e.collective_compute(
                       "AllGather", ALU.bypass, replica_groups=RG,
                       ins=[kt_c.rearrange("p c t -> p (c t)")], outs=[kt_g]),
                   writes=[r_ktg], extra_deps=list(kt_stores))
        p.dma("gpsimd", "ldw", wpb_v, wv_in.rearrange("(a p) o -> p a o", p=128),
              writes=[r_wp])
        for tt in range(NT):
            for oc in range(2):
                k4 = (tt * 2 + oc) % 4
                for i in range(8):
                    p.op("tensor",
                         lambda e, i=i, tt=tt, oc=oc, k4=k4: e.matmul(
                             pmm[k4], xnT_v[:, i, tt * 128:(tt + 1) * 128],
                             wpb_v[:, i, oc * 512:oc * 512 + 512],
                             start=(i == 0), stop=(i == 7)),
                         reads=[r_wp, r_xnT], writes=[r_pmm[k4]])
                dst = vloc_v[:, 4 * oc:4 * oc + 4, tt, :] \
                    .rearrange("p a (h c) -> p a h c", h=2)[:, :, :, 0:64]
                p.op("scalar",
                     lambda e, dst=dst, k4=k4: e.activation(
                         dst, pmm[k4].rearrange("p (a h c) -> p a h c", a=4, h=2),
                         AF.Copy),
                     reads=[r_pmm[k4]], writes=[r_vloc])
                step_gen(1)
        step_gen(20)
        va_store = p.dma("sync", "stk", va_c, rg_vloc.view(BF16, 8 * NT * 130),
                         reads=[r_vloc])
        r_vag = R("vag")
        cc2 = p.op("gpsimd",
                   lambda e: e.collective_compute(
                       "AllGather", ALU.bypass, replica_groups=RG,
                       ins=[va_c], outs=[va_g]),
                   writes=[r_vag], extra_deps=[va_store])
        step_gen(120)
        rg_w1p.free()
        rg_vloc.free()
        rg_xnt1.free()
        for rg in rg_ev:
            rg.free()
        for rg in rg_pmm:
            rg.free()
        rg_pst.free()

        rg_kTp = A.alloc(16 * KB, "kTp")
        kTp2 = rg_kTp.view(BF16, S)
        rg_vaug = A.alloc(NT * 130 * 4 * 2, "vaug")
        vaug2 = rg_vaug.view(BF16, 64 * 130).rearrange("p (k c) -> p k c", k=64)
        rg_qT2 = A.alloc(4 * KB, "qT2")
        qT2 = rg_qT2.view(BF16, TOK)
        rg_aT2 = A.alloc(16 * KB, "aT2")
        aT2_v = rg_aT2.view(F32, 2 * TOK, parts=64).rearrange("p (c t) -> p c t", c=2)
        rg_pT = [A.alloc(4 * KB, f"pT{i}") for i in range(2)]
        pT = [rg.view(BF16, 2048) for rg in rg_pT]
        rg_b0c = A.alloc(64, "b0c")
        b0c = rg_b0c.view(F32, 1)
        rg_b0cb = A.alloc(64, "b0cb")
        b0cb = rg_b0cb.view(BF16, 1)
        rg_usb = A.alloc(4 * KB, "u_sb", side="top")
        u_sb = rg_usb.view(F32, D, parts=1)
        rg_linv = A.alloc(2 * KB, "linv")
        linv = rg_linv.view(F32, 512, parts=1)
        rg_pvsb = A.alloc(2 * KB, "pv_sb")
        pv_sb = rg_pvsb.view(F32, 512, parts=64)
        rg_grT2 = A.alloc(4 * KB, "grT2")
        grT2 = rg_grT2.view(F32, 8 * 128)
        rg_dT2 = A.alloc(2 * KB, "dT2")
        dT2 = rg_dT2.view(BF16, 8 * 128)
        dT2_v = dT2.rearrange("p (k m) -> p k m", k=8)
        rg_pss = [PA.alloc(4096, f"ps_s{i}") for i in range(2)]
        ps_s2 = [rg.view(F32, 1024) for rg in rg_pss]
        rg_pso = PA.alloc(4096, "ps_o")
        ps_o = rg_pso.view(F32, 1024)
        r_kTp, r_vaug, r_qT2, r_aT2 = rg_kTp.res, rg_vaug.res, rg_qT2.res, rg_aT2.res
        r_pT = [rg.res for rg in rg_pT]
        r_b0c, r_b0cb, r_usb = rg_b0c.res, rg_b0cb.res, rg_usb.res
        r_linv, r_pvsb = rg_linv.res, rg_pvsb.res
        r_grT2, r_dT2 = rg_grT2.res, rg_dT2.res
        r_ps_s2 = [rg.res for rg in rg_pss]
        r_ps_o = rg_pso.res

        p.dma("sync", "ldq", b0c, b0_i, writes=[r_b0c])
        p.dma("gpsimd", "ldb", b0cb, b0_i, writes=[r_b0cb])
        at_stores = []
        for s8 in range(8):
            for r in range(4):
                p.dma("sync", "ldk", kTp2[:, 2048 * r:2048 * (r + 1)],
                      kt_g[128 * r:128 * (r + 1), :]
                      .rearrange("p (c t) -> p c t", c=8)[:, s8, :],
                      writes=[r_kTp], extra_deps=[cc1])
                p.dma("sync", "ldv", vaug2[:, 16 * r:16 * (r + 1), :],
                      va_g[128 * r:128 * (r + 1), :]
                      .rearrange("p (a t c) -> p a t c", a=8, t=NT)[:, s8],
                      writes=[r_vaug], extra_deps=[cc2])
            p.dma("sync", "ldq", qT2, qt_d[:, s8, :], writes=[r_qT2],
                  extra_deps=qt_stores)
            for kc in range(64):
                p.op("tensor",
                     lambda e, kc=kc: e.matmul(ps_s2[0][0:1, 0:130], b0cb,
                                               vaug2[:, kc, :],
                                               start=(kc == 0), stop=(kc == 63)),
                     reads=[r_b0cb, r_vaug], writes=[r_ps_s2[0]])
            dst = u_sb[0:1, 128 * s8:128 * s8 + 128].rearrange("a (h c) -> a h c", h=2)
            src = ps_s2[0][0:1, 0:130].rearrange("a (h c) -> a h c", h=2)[:, :, 0:64]
            p.op("vector", lambda e, dst=dst, src=src: e.tensor_copy(dst, src),
                 reads=[r_ps_s2[0]], writes=[r_usb])
            assert rg_pss[1].start == rg_pss[0].start + 4096
            ps_pair = rg_pss[0].view(F32, 2048)

            def emit_pv(jj):
                pb = jj % 2
                for sub in range(4):
                    kc = 2 * jj + (sub >> 1)
                    hi = sub & 1
                    p.op("tensor",
                         lambda e, kc=kc, hi=hi, pb=pb, sub=sub: e.matmul(
                             ps_o[0:65, 512 * hi:512 * hi + 512],
                             vaug2[:, kc, 65 * hi:65 * hi + 65],
                             pT[pb][:, 512 * sub:512 * sub + 512],
                             start=(kc == 0), stop=(kc == 63)),
                         reads=[r_vaug, r_pT[pb]], writes=[r_ps_o])

            for qm in range(4):
                for jj in range(32):
                    for half2 in range(2):
                        kc = 2 * jj + half2
                        for hi in range(2):
                            lo = 64 * hi
                            p.op("tensor",
                                 lambda e, kc=kc, hi=hi, lo=lo, qm=qm, half2=half2: e.matmul(
                                     ps_s2[half2][:, 512 * hi:512 * hi + 512],
                                     kTp2[lo:lo + 64, 128 * kc:128 * kc + 128],
                                     qT2[lo:lo + 64, 512 * qm:512 * qm + 512],
                                     start=True, stop=True),
                                 reads=[r_kTp, r_qT2], writes=[r_ps_s2[half2]])
                    pb = jj % 2
                    p.op("scalar",
                         lambda e, pb=pb: e.activation(pT[pb], ps_pair, AF.Exp, scale=SCALE),
                         reads=[r_ps_s2[0], r_ps_s2[1]], writes=[r_pT[pb]])
                    if jj > 0:
                        emit_pv(jj - 1)
                    kc_glob += 2
                    while (not gen_state["done"]) and q_done < (kc_glob * NQUANTA) // TOTKC:
                        step_gen(1)
                        q_done += 1
                emit_pv(31)
                for hi in range(2):
                    p.op("vector",
                         lambda e, hi=hi: e.reciprocal(linv, ps_o[64:65, 512 * hi:512 * hi + 512]),
                         reads=[r_ps_o], writes=[r_linv])
                    p.op("scalar",
                         lambda e, hi=hi: e.activation(pv_sb, ps_o[0:64, 512 * hi:512 * hi + 512], AF.Copy),
                         reads=[r_ps_o], writes=[r_pvsb])
                    p.op("tensor",
                         lambda e: e.matmul(ps_s2[0][0:64, 0:512], onesf_t[0:1, 0:64], linv,
                                            start=True, stop=True),
                         reads=[r_onesf, r_linv], writes=[r_ps_s2[0]])
                    dst = aT2_v[:, hi, 512 * qm:512 * qm + 512]
                    p.op("vector",
                         lambda e, dst=dst: e.tensor_mul(dst, pv_sb, ps_s2[0][0:64, 0:512]),
                         reads=[r_pvsb, r_ps_s2[0]], writes=[r_aT2])
            for j, (a, l) in enumerate(PAIRS):
                if a >> 1 != s8:
                    continue
                hi = a & 1
                for cg in range(8):
                    p.dma("sync", "ldg", grT2.rearrange("p (c m) -> p c m", c=8),
                          grT_i[j, 1024 * cg:1024 * cg + 1024].rearrange("(c p) m -> p c m", p=128),
                          writes=[r_grT2])
                    p.op("vector",
                         lambda e: e.tensor_scalar(dT2, grT2, b0c, None, ALU.subtract),
                         reads=[r_grT2, r_b0c], writes=[r_dT2])
                    for c in range(8):
                        p.op("tensor",
                             lambda e, c=c, cg=cg, hi=hi: e.matmul(
                                 ps_o[0:64, 0:128],
                                 vaug2[:, 8 * cg + c, 65 * hi:65 * hi + 64],
                                 dT2_v[:, c, :],
                                 start=(cg == 0 and c == 0), stop=(cg == 7 and c == 7)),
                             reads=[r_vaug, r_dT2], writes=[r_ps_o])
                dst = aT2_v[:, hi, :].rearrange("p (t0 t1) -> p t0 t1", t1=16)[:, :, l]
                p.op("vector",
                     lambda e, dst=dst: e.tensor_tensor(dst, dst, ps_o[0:64, 0:128], ALU.add),
                     reads=[r_ps_o, r_aT2], writes=[r_aT2])
            at_stores.append(p.dma("sync", "sta", attnT_d[:, 2 * s8:2 * s8 + 2, :],
                                   aT2_v, reads=[r_aT2]))
        for _ in gen:
            pass
        rg_kTp.free()
        rg_vaug.free()
        rg_qT2.free()
        rg_aT2.free()
        for rg in rg_pT:
            rg.free()
        rg_b0c.free()
        rg_b0cb.free()
        rg_linv.free()
        rg_pvsb.free()
        rg_grT2.free()
        rg_dT2.free()
        for rg in rg_pss:
            rg.free()
        rg_pso.free()
        rg_xnT.free()
        rg_bigh.free()
        rg_pstr = PA.alloc(4096, "ps_tr")
        ps_tr = rg_pstr.view(F32, D)
        rg_psbc = PA.alloc(4096, "ps_bc")
        ps_bc = rg_psbc.view(F32, D)
        r_pstr, r_psbc = rg_pstr.res, rg_psbc.res

        rg_rowsA = A.alloc(4 * KB, "rowsA")
        rowsA = rg_rowsA.view(F32, D, parts=65)
        g_sb, be_sb = rowsA[32:33, :], rowsA[64:65, :]
        w3s_sb = rowsA[0:1, :]
        rg_ubc = A.alloc(4 * KB, "u_bc")
        u_bc = rg_ubc.view(F32, D)
        rg_gbc = A.alloc(4 * KB, "g_bc")
        g_bc = rg_gbc.view(F32, D)
        rg_bbc = A.alloc(4 * KB, "b_bc")
        b_bc = rg_bbc.view(F32, D)
        rg_w3sbc = A.alloc(4 * KB, "w3s_bc")
        w3s_bc = rg_w3sbc.view(F32, D)
        rg_abuf = A.alloc(4 * KB, "abuf")
        abuf = rg_abuf.view(F32, D)
        rg_t2 = A.alloc(4 * KB, "t2")
        t2 = rg_t2.view(F32, D)
        rg_asl = [A.alloc(8 * KB, f"asl{i}") for i in range(2)]
        asl = [rg.view(F32, NH * 128, parts=64) for rg in rg_asl]
        rg_xtb = [A.alloc(4 * KB, f"xtb{i}") for i in range(2)]
        xtb = [rg.view(F32, D) for rg in rg_xtb]
        r_rowsA = rg_rowsA.res
        r_rowsB = r_rowsA
        r_ubc, r_gbc, r_bbc = rg_ubc.res, rg_gbc.res, rg_bbc.res
        r_w3sbc = rg_w3sbc.res
        r_abuf, r_t2 = rg_abuf.res, rg_t2.res
        r_asl = [rg.res for rg in rg_asl]
        r_xtb = [rg.res for rg in rg_xtb]
        tmpA = {"sq": t2, "c0": c0, "c1": c1, "c2": c2}
        r_tmpA = {"sq": r_t2, "c0": r_c0, "c1": r_c1, "c2": r_c2}
        p.dma("sync", "lds", rowsA[32:33, :], gamma_i, writes=[r_rowsA])
        p.dma("sync", "lds", rowsA[64:65, :], beta_i, writes=[r_rowsA])
        p.dma("sync", "lds", w3s_sb, w3s_i, writes=[r_rowsB])

        def build_bc(src_sb, base, r_src, dst, r_dst):
            for half in range(2):
                p.op("tensor",
                     lambda e, src_sb=src_sb, half=half, base=base: e.matmul(
                         ps_bc[:, 512 * half:512 * half + 512],
                         onesf_t[base:base + 1, :],
                         src_sb[0:1, 512 * half:512 * half + 512], start=True, stop=True),
                     reads=[r_onesf, r_src], writes=[r_psbc])
            p.op("vector", lambda e, dst=dst: e.tensor_copy(dst, ps_bc[:]),
                 reads=[r_psbc], writes=[r_dst])

        build_bc(u_sb, 0, r_usb, u_bc, r_ubc)
        build_bc(g_sb, 32, r_rowsA, g_bc, r_gbc)
        build_bc(be_sb, 64, r_rowsA, b_bc, r_bbc)
        build_bc(w3s_sb, 0, r_rowsB, w3s_bc, r_w3sbc)
        rg_w3 = A.alloc(64 * KB, "w3t")
        w3t_v = rg_w3.view(BF16, 32 * D).rearrange("p (c m) -> p c m", c=32)
        r_w3 = rg_w3.res
        p.dma("gpsimd", "lw3", w3t_v, w3t_i.rearrange("(c p) m -> p c m", p=128),
              writes=[r_w3])
        rg_xtb4 = [A.alloc(4 * KB, f"xtb4_{i}") for i in range(2)]
        xtb4 = [rg.view(F32, D) for rg in rg_xtb4]
        r_xtb4 = [rg.res for rg in rg_xtb4]
        rg_abuf4 = A.alloc(4 * KB, "abuf4")
        abuf4 = rg_abuf4.view(F32, D)
        r_abuf4 = rg_abuf4.res
        x1_stores = []
        xa_stores = []
        for tt in range(NT):
            par = tt % 2
            p.dma("sync", f"lda{par}",
                  asl[par].rearrange("p (h t) -> p h t", h=NH),
                  attnT_d[:, :, 128 * tt:128 * tt + 128], writes=[r_asl[par]],
                  extra_deps=at_stores)
            p.dma("sync", f"ldxa{par}", xtb[par],
                  x_in.rearrange("(t p) d -> p t d", p=128)[:, tt, :],
                  writes=[r_xtb[par]])
            pstA = ps_tr if par == 0 else ps_bc
            r_pstA = r_pstr if par == 0 else r_psbc
            for h in range(NH):
                p.op("tensor",
                     lambda e, h=h, par=par, pstA=pstA: e.transpose(
                         pstA[:, 64 * h:64 * h + 64],
                         asl[par].rearrange("p (h t) -> p h t", h=NH)[:, h, :],
                         idf[0:64, 0:64]),
                     reads=[r_asl[par], r_idf], writes=[r_pstA])
            p.op("vector", lambda e, pstA=pstA: e.tensor_add(abuf, pstA[:], u_bc),
                 reads=[r_pstA, r_ubc], writes=[r_abuf])
            ln_tile(p, abuf, t2, r_abuf, r_t2, tmpA, r_tmpA, EPS1)
            p.op("vector", lambda e: e.tensor_mul(t2, t2, g_bc),
                 reads=[r_t2, r_gbc], writes=[r_t2])
            p.op("vector", lambda e: e.tensor_add(t2, t2, b_bc),
                 reads=[r_t2, r_bbc], writes=[r_t2])
            p.op("vector", lambda e, par=par: e.tensor_add(t2, t2, xtb[par]),
                 reads=[r_t2, r_xtb[par]], writes=[r_t2])
            xa_stores.append(p.dma("sync", "stxa", x1tmp[:, tt * D:(tt + 1) * D], t2,
                                   reads=[r_t2]))
            p.dma("sync", f"lc{par}", hb_v[par].rearrange("p (c m) -> p c m", c=32),
                  ct_d[:, :, 128 * tt:128 * tt + 128].rearrange("c p m -> p c m"),
                  writes=[r_hb[par]], extra_deps=st3_stores)
            p.dma("sync", f"ldx4{par}", xtb4[par], x1tmp[:, tt * D:(tt + 1) * D],
                  writes=[r_xtb4[par]], extra_deps=[xa_stores[tt]])
            for c in range(32):
                for half in range(2):
                    p.op("tensor",
                         lambda e, c=c, par=par, half=half, pstA=pstA: e.matmul(
                             pstA[:, 512 * half:512 * half + 512],
                             hb_v[par].rearrange("p (c m) -> p c m", c=32)[:, c, :],
                             w3t_v[:, c, 512 * half:512 * half + 512],
                             start=(c == 0), stop=(c == 31)),
                         reads=[r_hb[par], r_w3], writes=[r_pstA])
            p.op("vector", lambda e, pstA=pstA: e.tensor_mul(abuf4, pstA[:], w3s_bc),
                 reads=[r_pstA, r_w3sbc], writes=[r_abuf4])
            p.op("vector", lambda e, par=par: e.tensor_add(abuf4, abuf4, xtb4[par]),
                 reads=[r_abuf4, r_xtb4[par]], writes=[r_abuf4])
            x1_stores.append(p.dma("sync", "sty", x1_d[:, tt, :], abuf4, reads=[r_abuf4]))
        for rg in rg_asl:
            rg.free()
        rg_rowsA.free()
        rg_gbc.free()
        rg_bbc.free()
        rg_t2.free()
        rg_usb.free()
        rg_w3.free()
        for rg in rg_hb + rg_bst + rg_xtb + rg_xtb4:
            rg.free()
        rg_ubc.free()
        rg_w3sbc.free()
        rg_abuf.free()
        rg_abuf4.free()
        for rg in rg_psg2:
            rg.free()
        rg_pstr.free()
        rg_psbc.free()

        rg_xn1T = A.alloc(32 * KB, "xn1T", side="top")
        xn1T_v = rg_xn1T.view(BF16, 8 * TOK).rearrange("p (c t) -> p c t", c=8)
        rg_wqkv = A.alloc(48 * KB, "wqkv")
        wq4_v = rg_wqkv.view(BF16, 8 * 3 * D).rearrange("p (a o) -> p a o", a=8)
        rg_biasS = A.alloc(8 * KB, "biasS")
        biasS = rg_biasS.view(F32, 2 * D)
        rg_biasv = A.alloc(4 * KB, "bias_v")
        bias_v = rg_biasv.view(F32, D)
        rg_brow = A.alloc(12 * KB, "brow")
        brow = rg_brow.view(F32, 3 * D, parts=1)
        rg_srow = A.alloc(64, "srow")
        srow = rg_srow.view(F32, 1, parts=1)
        rg_x4 = [A.alloc(4 * KB, f"x4_{i}") for i in range(2)]
        x4b = [rg.view(F32, D) for rg in rg_x4]
        rg_xnt4 = A.alloc(2 * KB, "xnt4")
        xn_t4 = rg_xnt4.view(BF16, D)
        rg_qs = A.alloc(2 * KB, "qs")
        qs = rg_qs.view(F32, 512)
        rg_m0 = A.alloc(2 * KB, "m0")
        m0 = rg_m0.view(F32, 512)
        rg_e0 = A.alloc(2 * KB, "e0")
        e0 = rg_e0.view(F32, 512)
        rg_r0 = A.alloc(2 * KB, "r0")
        r0 = rg_r0.view(F32, 512)
        rg_q1st = A.alloc(4 * KB, "q1st")
        q1st = rg_q1st.view(F32, D)
        rg_k1st = A.alloc(4 * KB, "k1st")
        k1st = rg_k1st.view(F32, D)
        rg_k1bf = A.alloc(2 * KB, "k1bf")
        k1bf = rg_k1bf.view(BF16, D)
        rg_kvst = A.alloc(2 * KB, "kvst")
        kvst = rg_kvst.view(BF16, D)
        rg_totst = A.alloc(8 * KB, "totst")
        totst = rg_totst.view(F32, 2 * D, parts=1)
        r_xn1T, r_wq4 = rg_xn1T.res, rg_wqkv.res
        r_biasS, r_biasv, r_brow, r_srow = rg_biasS.res, rg_biasv.res, rg_brow.res, rg_srow.res
        r_x4 = [rg.res for rg in rg_x4]
        r_xnt4 = rg_xnt4.res
        r_qs, r_m0, r_e0, r_r0 = rg_qs.res, rg_m0.res, rg_e0.res, rg_r0.res
        r_q1st, r_k1st, r_k1bf, r_kvst, r_totst = (rg_q1st.res, rg_k1st.res,
                                                   rg_k1bf.res, rg_kvst.res, rg_totst.res)
        rg_pmm4 = [PA.alloc(2048, f"pmm4{i}") for i in range(2)]
        pmm4 = [rg.view(F32, 512) for rg in rg_pmm4]
        r_pmm4 = [rg.res for rg in rg_pmm4]
        rg_ptp4 = PA.alloc(2048, "ptp4")
        ptp4 = rg_ptp4.view(BF16, D)
        r_ptp4 = rg_ptp4.res
        rg_ptotk = PA.alloc(4096, "ptotk")
        ptotk = rg_ptotk.view(F32, D, parts=1)
        r_ptotk = rg_ptotk.res
        rg_ptotkv = PA.alloc(4096, "ptotkv")
        ptotkv = rg_ptotkv.view(F32, D, parts=1)
        r_ptotkv = rg_ptotkv.res

        p.dma("gpsimd", "ldw4", wq4_v, wqkvT_i.rearrange("(a p) o -> p a o", p=128),
              writes=[r_wq4])
        p.dma("sync", "ld4", brow, bqkv_i, writes=[r_brow])
        p.dma("sync", "ld4", srow, scale_i, writes=[r_srow])
        p.op("tensor", lambda e: e.matmul(pmm4[0][:, 0:1], onesf_t[0:1, :], srow,
                                          start=True, stop=True),
             reads=[r_onesf, r_srow], writes=[r_pmm4[0]])
        p.op("vector", lambda e: e.tensor_copy(scol[:], pmm4[0][:, 0:1]),
             reads=[r_pmm4[0]], writes=[r_scol])
        for jc in range(6):
            par = jc % 2
            p.op("tensor",
                 lambda e, jc=jc, par=par: e.matmul(
                     pmm4[par], onesf_t[0:1, :], brow[0:1, 512 * jc:512 * jc + 512],
                     start=True, stop=True),
                 reads=[r_onesf, r_brow], writes=[r_pmm4[par]])
            if jc < 4:
                p.op("vector",
                     lambda e, jc=jc, par=par: e.tensor_scalar_mul(
                         biasS[:, 512 * jc:512 * jc + 512], pmm4[par], scol[:]),
                     reads=[r_pmm4[par], r_scol], writes=[r_biasS])
            else:
                p.op("vector",
                     lambda e, jc=jc, par=par: e.tensor_copy(
                         bias_v[:, 512 * (jc - 4):512 * (jc - 4) + 512], pmm4[par]),
                     reads=[r_pmm4[par]], writes=[r_biasv])

        st4_stores = []
        for tt in range(NT):
            xpar = tt % 2
            p.dma("sync", f"lx4{xpar}", x4b[xpar], x1_d[:, tt, :],
                  writes=[r_x4[xpar]], extra_deps=x1_stores)
            ln_tile(p, x4b[xpar], xn_t4, r_x4[xpar], r_xnt4, tmp, r_tmp, 1e-5)
            for c in range(8):
                p.op("tensor",
                     lambda e, c=c: e.transpose(ptp4[:, c * 128:(c + 1) * 128],
                                                xn_t4[:, c * 128:(c + 1) * 128], idb[:]),
                     reads=[r_xnt4, r_idb], writes=[r_ptp4])
            dst = xn1T_v[:, :, tt * 128:(tt + 1) * 128]
            src = ptp4.rearrange("p (c t) -> p c t", c=8)
            p.op("vector", lambda e, dst=dst, src=src: e.tensor_copy(dst, src),
                 reads=[r_ptp4], writes=[r_xn1T])

            def elu1(jc, par, dst_ap, r_dst):
                p.op("vector",
                     lambda e, jc=jc, par=par: e.scalar_tensor_tensor(
                         qs, pmm4[par], scol[:], biasS[:, 512 * jc:512 * jc + 512],
                         ALU.mult, ALU.add),
                     reads=[r_pmm4[par], r_scol, r_biasS], writes=[r_qs])
                p.op("vector", lambda e: e.tensor_scalar_min(m0, qs, 0.0),
                     reads=[r_qs], writes=[r_m0])
                p.op("scalar", lambda e: e.activation(e0, m0, AF.Exp),
                     reads=[r_m0], writes=[r_e0])
                p.op("vector",
                     lambda e, dst_ap=dst_ap: e.scalar_tensor_tensor(
                         dst_ap, qs, 0.0, e0, ALU.max, ALU.add),
                     reads=[r_qs, r_e0], writes=[r_dst])

            for jc in range(6):
                par = jc % 2
                for i in range(8):
                    p.op("tensor",
                         lambda e, i=i, jc=jc, par=par, tt=tt: e.matmul(
                             pmm4[par],
                             xn1T_v[:, i, tt * 128:(tt + 1) * 128],
                             wq4_v[:, i, 512 * jc:512 * jc + 512],
                             start=(i == 0), stop=(i == 7)),
                         reads=[r_xn1T, r_wq4], writes=[r_pmm4[par]])
                half = jc % 2
                if jc < 2:
                    elu1(jc, par, q1st[:, 512 * half:512 * half + 512], r_q1st)
                elif jc < 4:
                    elu1(jc, par, k1st[:, 512 * half:512 * half + 512], r_k1st)
                else:
                    p.op("vector",
                         lambda e, par=par, half=half: e.tensor_add(
                             m0, pmm4[par], bias_v[:, 512 * half:512 * half + 512]),
                         reads=[r_pmm4[par], r_biasv], writes=[r_m0])
                    p.op("vector",
                         lambda e, half=half: e.tensor_mul(
                             kvst[:, 512 * half:512 * half + 512], m0,
                             k1st[:, 512 * half:512 * half + 512]),
                         reads=[r_m0, r_k1st], writes=[r_kvst])
            p.op("vector", lambda e: e.tensor_copy(k1bf, k1st),
                 reads=[r_k1st], writes=[r_k1bf])
            for half in range(2):
                p.op("tensor",
                     lambda e, half=half, tt=tt: e.matmul(
                         ptotk[0:1, 512 * half:512 * half + 512], onesb[:],
                         k1bf[:, 512 * half:512 * half + 512],
                         start=(tt == 0), stop=(tt == NT - 1)),
                     reads=[r_onesb, r_k1bf], writes=[r_ptotk])
                p.op("tensor",
                     lambda e, half=half, tt=tt: e.matmul(
                         ptotkv[0:1, 512 * half:512 * half + 512], onesb[:],
                         kvst[:, 512 * half:512 * half + 512],
                         start=(tt == 0), stop=(tt == NT - 1)),
                     reads=[r_onesb, r_kvst], writes=[r_ptotkv])
            st4_stores.append(p.dma("sync", "st4", q1_d[:, tt, :], q1st, reads=[r_q1st]))
            st4_stores.append(p.dma("sync", "st4", k1_d[:, tt, :], k1bf, reads=[r_k1bf]))
            st4_stores.append(p.dma("sync", "st4", kv1_d[:, tt, :], kvst, reads=[r_kvst]))
        p.op("vector", lambda e: e.tensor_copy(totst[0:1, 0:D], ptotk),
             reads=[r_ptotk], writes=[r_totst])
        p.op("vector", lambda e: e.tensor_copy(totst[0:1, D:2 * D], ptotkv),
             reads=[r_ptotkv], writes=[r_totst])
        tot_store = p.dma("sync", "st4", tot_c, totst, reads=[r_totst])
        r_totg = R("totg")
        cc3 = p.op("gpsimd",
                   lambda e: e.collective_compute(
                       "AllGather", ALU.bypass, replica_groups=RG,
                       ins=[tot_c], outs=[tot_g]),
                   writes=[r_totg], extra_deps=[tot_store])
        rg_wqkv.free()
        rg_biasS.free()
        rg_biasv.free()
        rg_brow.free()
        rg_srow.free()
        for rg in rg_x4:
            rg.free()
        rg_xnt4.free()
        rg_qs.free()
        rg_m0.free()
        rg_e0.free()
        rg_r0.free()
        rg_q1st.free()
        rg_k1st.free()
        rg_k1bf.free()
        rg_kvst.free()
        rg_totst.free()
        for rg in rg_pmm4:
            rg.free()
        rg_ptp4.free()
        rg_ptotk.free()
        rg_ptotkv.free()

        EPS = 1e-3
        rg_x1a = A.alloc(64 * KB, "x1a", side="top")
        x1a = rg_x1a.view(F32, NT * D)
        r_x1a = rg_x1a.res
        rg_wp5 = A.alloc(16 * KB, "wp5")
        wp5_v = rg_wp5.view(BF16, 8 * D).rearrange("p (a o) -> p a o", a=8)
        rg_bpbc = A.alloc(4 * KB, "bp_bc")
        bp_bc = rg_bpbc.view(F32, D)
        rg_tri = A.alloc(256, "tri")
        tri5 = rg_tri.view(BF16, 128)
        rg_mask = A.alloc(64, "mask")
        mask5 = rg_mask.view(F32, 1, parts=4)
        rg_totsg = A.alloc(8 * KB, "totsg")
        totsg = rg_totsg.view(F32, 2 * D, parts=4)
        rg_ck = A.alloc(4 * KB, "carry_k")
        carry_k = rg_ck.view(F32, D, parts=1)
        rg_ckv = A.alloc(4 * KB, "carry_kv")
        carry_kv = rg_ckv.view(F32, D, parts=1)
        rg_bprow = A.alloc(4 * KB, "bprow")
        bprow = rg_bprow.view(F32, D, parts=1)
        rg_q5 = [A.alloc(4 * KB, f"q5_{i}") for i in range(2)]
        q5 = [rg.view(F32, D) for rg in rg_q5]
        rg_k5 = [A.alloc(2 * KB, f"k5_{i}") for i in range(2)]
        k5 = [rg.view(BF16, D) for rg in rg_k5]
        rg_v5 = [A.alloc(2 * KB, f"v5_{i}") for i in range(2)]
        v5 = [rg.view(BF16, D) for rg in rg_v5]
        rg_x5 = [A.alloc(4 * KB, f"x5_{i}") for i in range(2)]
        x5 = [rg.view(F32, D) for rg in rg_x5]
        rg_dq = A.alloc(2 * KB, "dq")
        dq = rg_dq.view(F32, 512)
        rg_den = A.alloc(64, "den")
        den = rg_den.view(F32, 16)
        rg_dinv = A.alloc(64, "dinv")
        dinv = rg_dinv.view(F32, 16)
        rg_y0 = A.alloc(2 * KB, "y0")
        y0 = rg_y0.view(F32, 512)
        rg_ybf = A.alloc(2 * KB, "ybf")
        y_bf = rg_ybf.view(BF16, D)
        rg_yT = A.alloc(2 * KB, "yT")
        yT = rg_yT.view(BF16, 8 * 128)
        r_wp5, r_bpbc, r_tri, r_mask, r_totsg = (rg_wp5.res, rg_bpbc.res, rg_tri.res,
                                                 rg_mask.res, rg_totsg.res)
        r_ck, r_ckv, r_bprow = rg_ck.res, rg_ckv.res, rg_bprow.res
        r_q5 = [rg.res for rg in rg_q5]
        r_k5 = [rg.res for rg in rg_k5]
        r_v5 = [rg.res for rg in rg_v5]
        r_x5 = [rg.res for rg in rg_x5]
        r_dq, r_den, r_dinv, r_y0, r_ybf, r_yT = (rg_dq.res, rg_den.res, rg_dinv.res,
                                                  rg_y0.res, rg_ybf.res, rg_yT.res)
        rg_pcs = PA.alloc(2048, "ps_cs")
        ps_cs = rg_pcs.view(F32, 512)
        rg_pkv = PA.alloc(2048, "ps_kv")
        ps_kv = rg_pkv.view(F32, 512)
        rg_prk = PA.alloc(4096, "ps_rk")
        ps_rk = rg_prk.view(F32, D, parts=1)
        rg_prkv = PA.alloc(4096, "ps_rkv")
        ps_rkv = rg_prkv.view(F32, D, parts=1)
        rg_ptp5 = PA.alloc(2048, "ptp5")
        ptp5 = rg_ptp5.view(BF16, D)
        rg_ppr = PA.alloc(2048, "ps_pr")
        ps_pr = rg_ppr.view(F32, 512)
        r_pcs, r_pkv, r_prk, r_prkv = rg_pcs.res, rg_pkv.res, rg_prk.res, rg_prkv.res
        r_ptp5, r_ppr = rg_ptp5.res, rg_ppr.res

        p.dma("gpsimd", "ldw5", wp5_v, wpT_i.rearrange("(a p) o -> p a o", p=128),
              writes=[r_wp5])
        p.dma("gpsimd", "ldw5", tri5, tri_i, writes=[r_tri])
        p.dma("sync", "ld5", mask5, mask_i, writes=[r_mask])
        p.dma("sync", "ld5", totsg, tot_g, writes=[r_totsg], extra_deps=[cc3])
        p.dma("sync", "ld5", bprow, bp_i, writes=[r_bprow])
        for half in range(2):
            p.op("tensor",
                 lambda e, half=half: e.matmul(
                     ps_pr, onesf_t[0:1, :], bprow[0:1, 512 * half:512 * half + 512],
                     start=True, stop=True),
                 reads=[r_onesf, r_bprow], writes=[r_ppr])
            p.op("vector",
                 lambda e, half=half: e.tensor_copy(bp_bc[:, 512 * half:512 * half + 512], ps_pr),
                 reads=[r_ppr], writes=[r_bpbc])
        for half in range(2):
            p.op("tensor",
                 lambda e, half=half: e.matmul(
                     ps_rk[0:1, 512 * half:512 * half + 512], mask5,
                     totsg[:, 512 * half:512 * half + 512], start=True, stop=False),
                 reads=[r_mask, r_totsg], writes=[r_prk])
            p.op("tensor",
                 lambda e, half=half: e.matmul(
                     ps_rkv[0:1, 512 * half:512 * half + 512], mask5,
                     totsg[:, 1024 + 512 * half:1024 + 512 * half + 512], start=True, stop=False),
                 reads=[r_mask, r_totsg], writes=[r_prkv])
        p.op("vector", lambda e: e.tensor_copy(carry_k, ps_rk),
             reads=[r_prk], writes=[r_ck])
        p.op("vector", lambda e: e.tensor_copy(carry_kv, ps_rkv),
             reads=[r_prkv], writes=[r_ckv])

        for tt in range(NT):
            par = tt % 2
            p.dma("sync", f"lq{par}", q5[par], q1_d[:, tt, :], writes=[r_q5[par]],
                  extra_deps=st4_stores)
            p.dma("sync", f"lk{par}", k5[par], k1_d[:, tt, :], writes=[r_k5[par]],
                  extra_deps=st4_stores)
            p.dma("sync", f"lv{par}", v5[par], kv1_d[:, tt, :], writes=[r_v5[par]],
                  extra_deps=st4_stores)
            p.dma("sync", f"lx5{par}", x5[par], x1_d[:, tt, :], writes=[r_x5[par]])
            for half in range(2):
                sl = slice(512 * half, 512 * half + 512)
                p.op("tensor",
                     lambda e, par=par, sl=sl: e.matmul(ps_cs, tri5, k5[par][:, sl],
                                                        start=True, stop=False),
                     reads=[r_tri, r_k5[par]], writes=[r_pcs])
                p.op("tensor",
                     lambda e, sl=sl: e.matmul(ps_cs, onesf_t[0:1, :], carry_k[0:1, sl],
                                               start=False, stop=True),
                     reads=[r_onesf, r_ck], writes=[r_pcs])
                p.op("tensor",
                     lambda e, par=par, sl=sl: e.matmul(ps_kv, tri5, v5[par][:, sl],
                                                        start=True, stop=False),
                     reads=[r_tri, r_v5[par]], writes=[r_pkv])
                p.op("tensor",
                     lambda e, sl=sl: e.matmul(ps_kv, onesf_t[0:1, :], carry_kv[0:1, sl],
                                               start=False, stop=True),
                     reads=[r_onesf, r_ckv], writes=[r_pkv])
                p.op("vector",
                     lambda e, par=par, sl=sl: e.tensor_mul(dq, q5[par][:, sl], ps_cs),
                     reads=[r_q5[par], r_pcs], writes=[r_dq])
                dsl = slice(8 * half, 8 * half + 8)
                p.op("vector",
                     lambda e, dsl=dsl: e.tensor_reduce(
                         den[:, dsl].unsqueeze(2),
                         dq.rearrange("p (a b) -> p a b", a=8), AX.X, ALU.add),
                     reads=[r_dq], writes=[r_den])
                p.op("vector",
                     lambda e, dsl=dsl: e.tensor_scalar_max(den[:, dsl], den[:, dsl], EPS),
                     reads=[r_den], writes=[r_den])
                p.op("vector",
                     lambda e, dsl=dsl: e.reciprocal(dinv[:, dsl], den[:, dsl]),
                     reads=[r_den], writes=[r_dinv])
                p.op("vector",
                     lambda e, par=par, sl=sl: e.tensor_mul(y0, q5[par][:, sl], ps_kv),
                     reads=[r_q5[par], r_pkv], writes=[r_y0])
                bview = dinv[:, dsl].unsqueeze(2).broadcast_to([128, 8, 64])
                p.op("vector",
                     lambda e, sl=sl, bview=bview: e.tensor_tensor(
                         y_bf[:, sl].rearrange("p (a b) -> p a b", a=8),
                         y0.rearrange("p (a b) -> p a b", a=8), bview, ALU.mult),
                     reads=[r_y0, r_dinv], writes=[r_ybf])
            if tt < NT - 1:
                for half in range(2):
                    sl = slice(512 * half, 512 * half + 512)
                    p.op("tensor",
                         lambda e, par=par, sl=sl: e.matmul(
                             ps_rk[0:1, sl], onesb[:], k5[par][:, sl],
                             start=False, stop=False),
                         reads=[r_onesb, r_k5[par]], writes=[r_prk])
                    p.op("tensor",
                         lambda e, par=par, sl=sl: e.matmul(
                             ps_rkv[0:1, sl], onesb[:], v5[par][:, sl],
                             start=False, stop=False),
                         reads=[r_onesb, r_v5[par]], writes=[r_prkv])
                p.op("vector", lambda e: e.tensor_copy(carry_k, ps_rk),
                     reads=[r_prk], writes=[r_ck])
                p.op("vector", lambda e: e.tensor_copy(carry_kv, ps_rkv),
                     reads=[r_prkv], writes=[r_ckv])
            for c in range(8):
                p.op("tensor",
                     lambda e, c=c: e.transpose(ptp5[:, c * 128:(c + 1) * 128],
                                                y_bf[:, c * 128:(c + 1) * 128], idb[:]),
                     reads=[r_ybf, r_idb], writes=[r_ptp5])
            p.op("vector", lambda e: e.tensor_copy(yT, ptp5),
                 reads=[r_ptp5], writes=[r_yT])
            xt = x1a[:, tt * D:(tt + 1) * D]
            for half in range(2):
                sl = slice(512 * half, 512 * half + 512)
                for c in range(8):
                    p.op("tensor",
                         lambda e, c=c, sl=sl: e.matmul(
                             ps_pr, yT.rearrange("p (c t) -> p c t", c=8)[:, c, :],
                             wp5_v[:, c, sl], start=(c == 0), stop=(c == 7)),
                         reads=[r_yT, r_wp5], writes=[r_ppr])
                p.op("vector",
                     lambda e, sl=sl, xt=xt: e.tensor_add(xt[:, sl], ps_pr, bp_bc[:, sl]),
                     reads=[r_ppr, r_bpbc], writes=[r_x1a])
            p.op("vector", lambda e, xt=xt, par=par: e.tensor_add(xt, xt, x5[par]),
                 reads=[r_x5[par], r_x1a], writes=[r_x1a])
        rg_wp5.free()
        rg_bpbc.free()
        rg_tri.free()
        rg_mask.free()
        rg_totsg.free()
        rg_ck.free()
        rg_ckv.free()
        rg_bprow.free()
        for rg in rg_q5 + rg_k5 + rg_v5 + rg_x5:
            rg.free()
        rg_dq.free()
        rg_den.free()
        rg_dinv.free()
        rg_y0.free()
        rg_ybf.free()
        rg_yT.free()
        rg_pcs.free()
        rg_pkv.free()
        rg_prk.free()
        rg_prkv.free()
        rg_ptp5.free()
        rg_ppr.free()

        rg_hT = A.alloc(64 * KB, "hT")
        hT_v = rg_hT.view(BF16, 32 * 1024).rearrange("p (c t) -> p c t", c=32)
        rg_wd = [A.alloc(16 * KB, f"wd_res{i}") for i in range(2)]
        wd_v2 = [rg.view(BF16, 8 * D).rearrange("p (h m) -> p h m", h=8) for rg in rg_wd]
        rg_wgb = [A.alloc(2 * KB, f"wgb{i}") for i in range(2)]
        wgb = [rg.view(BF16, 8 * 128) for rg in rg_wgb]
        rg_wub = [A.alloc(2 * KB, f"wub{i}") for i in range(2)]
        wub = [rg.view(BF16, 8 * 128) for rg in rg_wub]
        gs6 = sqt[:, 0:1024]
        r_hT_q = [Res(f"hTq{i}") for i in range(4)]
        r_wd2 = [rg.res for rg in rg_wd]
        r_wgb = [rg.res for rg in rg_wgb]
        r_wub = [rg.res for rg in rg_wub]
        r_gs6 = r_sq
        rg_pg6 = [PA.alloc(2048, f"ps_g6{i}") for i in range(2)]
        ps_g6 = [rg.view(F32, 512) for rg in rg_pg6]
        rg_pu6 = [PA.alloc(2048, f"ps_u6{i}") for i in range(2)]
        ps_u6 = [rg.view(F32, 512) for rg in rg_pu6]
        rg_pd6 = [PA.alloc(4096, f"ps_d6{i}") for i in range(2)]
        ps_d6 = [rg.view(F32, D) for rg in rg_pd6]
        r_pg6 = [rg.res for rg in rg_pg6]
        r_pu6 = [rg.res for rg in rg_pu6]
        r_psd6 = [rg.res for rg in rg_pd6]

        dq_queue = []
        ud_ctr = [0]

        def make_unit(half, hg, t8, wpar):
            tt = 8 * half + t8

            def emit_unit():
                ud = ud_ctr[0] % 2
                ud_ctr[0] += 1
                for hl in range(8):
                    hid = 8 * hg + hl
                    for q2 in range(2):
                        p.op("tensor",
                             lambda e, hid=hid, hl=hl, q2=q2, t8=t8, ud=ud, wpar=wpar: e.matmul(
                                 ps_d6[ud][:, 512 * q2:512 * q2 + 512],
                                 hT_v[:, hid, 128 * t8:128 * t8 + 128],
                                 wd_v2[wpar][:, hl, 512 * q2:512 * q2 + 512],
                                 start=(hl == 0), stop=(hl == 7)),
                             reads=[r_hT_q[hg], r_wd2[wpar]], writes=[r_psd6[ud]])
                xt = x1a[:, tt * D:(tt + 1) * D]
                p.op("vector",
                     lambda e, xt=xt, ud=ud: e.tensor_add(xt, xt, ps_d6[ud][:]),
                     reads=[r_psd6[ud], r_x1a], writes=[r_x1a])
            return emit_unit

        for e_ in range(8):
            for half in range(2):
                hoff = 1024 * half
                for hid in range(32):
                    par = hid % 2
                    p.dma("gpsimd", f"lg{par}", wgb[par].rearrange("p (a m) -> p a m", a=8),
                          wg_i[e_, hid].rearrange("(a p) m -> p a m", p=128),
                          writes=[r_wgb[par]])
                    p.dma("gpsimd", f"lu{par}", wub[par].rearrange("p (a m) -> p a m", a=8),
                          wu_i[e_, hid].rearrange("(a p) m -> p a m", p=128),
                          writes=[r_wub[par]])
                    for q2 in range(2):
                        for i in range(8):
                            p.op("tensor",
                                 lambda e, i=i, q2=q2, par=par, hoff=hoff: e.matmul(
                                     ps_g6[q2],
                                     wgb[par].rearrange("p (a m) -> p a m", a=8)[:, i, :],
                                     xn1T_v[:, i, hoff + 512 * q2:hoff + 512 * q2 + 512],
                                     start=(i == 0), stop=(i == 7)),
                                 reads=[r_wgb[par], r_xn1T], writes=[r_pg6[q2]])
                        for i in range(8):
                            p.op("tensor",
                                 lambda e, i=i, q2=q2, par=par, hoff=hoff: e.matmul(
                                     ps_u6[q2],
                                     wub[par].rearrange("p (a m) -> p a m", a=8)[:, i, :],
                                     xn1T_v[:, i, hoff + 512 * q2:hoff + 512 * q2 + 512],
                                     start=(i == 0), stop=(i == 7)),
                                 reads=[r_wub[par], r_xn1T], writes=[r_pu6[q2]])
                        p.op("scalar",
                             lambda e, q2=q2: e.activation(gs6[:, 512 * q2:512 * q2 + 512],
                                                           ps_g6[q2], AF.Silu),
                             reads=[r_pg6[q2]], writes=[r_gs6])
                        p.op("vector",
                             lambda e, hid=hid, q2=q2: e.tensor_mul(
                                 hT_v[:, hid, 512 * q2:512 * q2 + 512],
                                 gs6[:, 512 * q2:512 * q2 + 512], ps_u6[q2]),
                             reads=[r_gs6, r_pu6[q2]], writes=[r_hT_q[hid >> 3]])
                    if hid % 8 == 7:
                        hg = hid >> 3
                        wpar = hg % 2
                        p.dma("gpsimd", "lwd", wd_v2[wpar],
                              wd_i[e_, 8 * hg:8 * hg + 8].rearrange("h p m -> p h m"),
                              writes=[r_wd2[wpar]])
                        for t8 in range(8):
                            dq_queue.append(make_unit(half, hg, t8, wpar))
                    for _ in range(2):
                        if dq_queue:
                            dq_queue.pop(0)()
        while dq_queue:
            dq_queue.pop(0)()
        y_store = p.dma("sync", "stfin", y_o, x1a.rearrange("p (t d) -> p t d", t=NT),
                        reads=[r_x1a])
        p.op("sync", lambda e: e.nop(), extra_deps=[y_store])
        p.build(es)
    return nc


def kernel_fused_maps(inputs_np):
    (x, wq, wk, wv, RT, gamma, beta, w1w, w1s, w2w, w2s, h1w, h1s, h2w, h2s,
     w3w, w3s, Wqkv, bqkv, Wproj, bproj, scale, Wg, Wu, Wd) = inputs_np
    iden = np.eye(128, dtype=np.float32)
    tri = np.triu(np.ones((128, 128), dtype=np.float32))
    com = dict(
        wq_in=wq, wk_in=wk, wv_in=wv, iden_in=iden, b0_i=stage_b0(RT),
        gamma_i=gamma[None], beta_i=beta[None],
        w1t_i=_pretile(w1w.T), w2t_i=_pretile(w2w.T),
        w1s_i=_scale128(w1s), w2s_i=_scale128(w2s),
        h1t_i=_pretile(h1w.T), h2t_i=_pretile(h2w.T),
        h1s_i=_scale128(h1s), h2s_i=_scale128(h2s),
        w3t_i=np.ascontiguousarray(w3w.T), w3s_i=w3s[None],
        wqkvT_i=np.ascontiguousarray(Wqkv.T), bqkv_i=bqkv[None],
        scale_i=scale.reshape(1, 1),
        wpT_i=np.ascontiguousarray(Wproj.T), bp_i=bproj[None], tri_i=tri,
        wg_i=np.ascontiguousarray(Wg.reshape(8, D, 32, 128).transpose(0, 2, 1, 3)),
        wu_i=np.ascontiguousarray(Wu.reshape(8, D, 32, 128).transpose(0, 2, 1, 3)),
        wd_i=np.ascontiguousarray(Wd.reshape(8, 32, 128, D)))
    grT = [stage_grT(RT, pos) for pos in range(4)]
    maps = []
    for c in range(NCORE):
        b, pos = c // 4, c % 4
        mask = np.zeros((4, 1), dtype=np.float32)
        mask[:pos] = 1.0
        maps.append(dict(x_in=np.ascontiguousarray(x[b, 2048 * pos:2048 * (pos + 1)]),
                         grT_i=grT[pos], mask_i=mask, **com))
    return maps


def kernel(L0_attn_x, L0_attn_Wq, L0_attn_Wk, L0_attn_Wv, L0_attn_rel_table,
           L0_attn_gamma, L0_attn_beta, L0_ffn_w1_w, L0_ffn_w1_s, L0_ffn_w2_w,
           L0_ffn_w2_s, L0_ffn_h1_w, L0_ffn_h1_s, L0_ffn_h2_w, L0_ffn_h2_s,
           L0_ffn_w3_w, L0_ffn_w3_s, L1_attn_Wqkv, L1_attn_bqkv, L1_attn_Wproj,
           L1_attn_bproj, L1_attn_scale, L1_ffn_Wg, L1_ffn_Wu, L1_ffn_Wd):
    args = [_f32(a) for a in (
        L0_attn_x, L0_attn_Wq, L0_attn_Wk, L0_attn_Wv, L0_attn_rel_table,
        L0_attn_gamma, L0_attn_beta, L0_ffn_w1_w, L0_ffn_w1_s, L0_ffn_w2_w,
        L0_ffn_w2_s, L0_ffn_h1_w, L0_ffn_h1_s, L0_ffn_h2_w, L0_ffn_h2_s,
        L0_ffn_w3_w, L0_ffn_w3_s, L1_attn_Wqkv, L1_attn_bqkv, L1_attn_Wproj,
        L1_attn_bproj, L1_attn_scale, L1_ffn_Wg, L1_ffn_Wu, L1_ffn_Wd)]
    maps = kernel_fused_maps(args)
    nc = _get("fused", build_fused)
    res = run_bass_kernel_spmd(nc, maps, core_ids=list(range(NCORE)))
    out = np.empty((2, S, D), dtype=np.float32)
    for c in range(NCORE):
        b, pos = c // 4, c % 4
        out[b, 2048 * pos:2048 * (pos + 1)] = _untile(res.results[c]["y_o"])
    return out
